# version 01
# baseline (speedup 1.0000x reference)
import numpy as np
import ml_dtypes
from contextlib import ExitStack

import concourse.bass as bass
import concourse.mybir as mybir
from concourse.bass_utils import run_bass_kernel_spmd

F32 = mybir.dt.float32
BF16 = mybir.dt.bfloat16
I32 = mybir.dt.int32
AF = mybir.ActivationFunctionType
ALU = mybir.AluOpType
AX = mybir.AxisListType
BF = ml_dtypes.bfloat16

NCORES = 8
L = 8192
D = 2048
TPC = L // NCORES
KD = D // 128

ENGINES = ("sync", "gpsimd", "vector", "scalar", "tensor")


class P:

    def __init__(self, name):
        self.nc = bass.Bass("TRN2", target_bir_lowering=False, debug=False)
        self.st = ExitStack()
        self.ops = {e: [] for e in ENGINES}
        self.cnt = {e: 0 for e in ENGINES}
        self.sem = {e: self.st.enter_context(self.nc.semaphore(f"sp_{e}"))
                    for e in ENGINES}
        self.name = name
        self.chs = set()
        self._cbank = self.st.enter_context(
            self.nc.sbuf_tensor("constbank", [128, 16], F32))
        self._cslots = {}
        self.const_barrier = None

    def constf(self, value):
        value = float(value)
        if value in self._cslots:
            return
        k = len(self._cslots)
        assert k < 16
        self._cslots[value] = k
        self.nc.const_aps.register(F32, value, self._cbank[:, k:k + 1])

    def cbank_host(self):
        a = np.zeros((128, 16), np.float32)
        for v, k in self._cslots.items():
            a[:, k] = v
        return a

    def din(self, name, shape, dt):
        return self.nc.dram_tensor(name, list(shape), dt, kind="ExternalInput").ap()

    def dout(self, name, shape, dt):
        return self.nc.dram_tensor(name, list(shape), dt, kind="ExternalOutput").ap()

    def dtmp(self, name, shape, dt):
        return self.nc.dram_tensor(name, list(shape), dt, kind="Internal").ap()

    def sb(self, name, shape, dt, scope=None):
        return (scope or self.st).enter_context(
            self.nc.sbuf_tensor(name, list(shape), dt))

    def ps(self, name, shape, dt=F32, scope=None):
        return (scope or self.st).enter_context(
            self.nc.psum_tensor(name, list(shape), dt))

    def scope(self):
        return self.st.enter_context(ExitStack())

    def _sem(self, name):
        if name not in self.sem:
            self.sem[name] = self.st.enter_context(self.nc.semaphore(f"ch_{name}"))
            self.cnt[name] = 0
            self.chs.add(name)
        return self.sem[name]

    def _emit(self, eng, k, deps, f, ksem=None):
        for d in deps:
            if d is not None:
                self._sem(d[0])
        deps2 = [(self.sem[d[0]], d[1]) for d in deps
                 if d is not None and d[1] > 0]
        if eng in ("vector", "scalar") and k > 0 and ksem is None \
                and self.cnt[eng] > 0:
            deps2.append((self.sem[eng], self.cnt[eng]))
        tgt = ksem if ksem is not None else eng
        self.cnt[tgt] = self.cnt.get(tgt, 0) + k
        semh = self.sem[tgt] if k else None

        def run(E, deps2=deps2, f=f, semh=semh, k=k):
            for h, v in deps2:
                E.wait_ge(h, v)
            ins = f(E)
            if semh is not None:
                ins.then_inc(semh, k)
        self.ops[eng].append(run)

    def mark(self, eng):
        return (eng, self.cnt.get(eng, 0))

    def allmarks(self):
        return [self.mark(e) for e in ENGINES] + [self.mark(c) for c in self.chs]

    def dma(self, eng, out, in_, ch, deps=()):
        self._sem(ch)
        self._emit(eng, 16, deps, lambda E: E.dma_start(out=out, in_=in_), ksem=ch)

    def op(self, eng, f, deps=(), inc=1):
        self._emit(eng, inc, deps, f)

    def wait(self, eng, deps):
        self._emit(eng, 0, deps, lambda E: None)

    def build(self):
        nc = self.nc
        blk = self.st.enter_context(nc.Block())
        cb = self.const_barrier
        for eng in ENGINES:
            ops = self.ops[eng]
            dec = getattr(blk, eng)

            def section(E, ops=ops, eng=eng, cb=cb):
                if cb is not None and eng != getattr(self, "cb_eng", "sync"):
                    E.wait_ge(self.sem[cb[0]], cb[1])
                for f in ops:
                    f(E)
            dec(section)
        self.st.close()
        return nc


NEWTON = True


def _newton_rsqrt(p, r_ap, v_ap, t_ap, deps):
    if not NEWTON:
        return
    p.op("vector", lambda E: E.tensor_tensor(t_ap, r_ap, r_ap, op=ALU.mult), deps=deps)
    p.op("vector", lambda E: E.tensor_tensor(t_ap, t_ap, v_ap, op=ALU.mult))
    p.op("vector", lambda E: E.tensor_scalar(t_ap, t_ap, -0.5, 1.5,
                                             op0=ALU.mult, op1=ALU.add))
    p.op("vector", lambda E: E.tensor_tensor(r_ap, r_ap, t_ap, op=ALU.mult))


def _mk_ident(p, tile):
    p.op("gpsimd", lambda E: E.memset(tile[:, :], 0.0))
    p.op("gpsimd", lambda E: E.affine_select(
        out=tile[:, :], in_=tile[:, :], compare_op=ALU.not_equal,
        fill=1.0, base=0, pattern=[[-1, 128]], channel_multiplier=1))


L1_CONSTS = (1e-5, 1e-6, 1.0)


def _const_inputs(consts):
    cb = np.zeros((128, 16), np.float32)
    for k, v in enumerate(consts):
        cb[:, k] = v
    colsel = np.zeros((128, 1024), np.float32)
    for c in range(32):
        colsel[:, c * 32 + c] = 1.0
    return {
        "identb": np.eye(128, dtype=BF),
        "identf32": np.eye(128, dtype=np.float32),
        "colsel_i": colsel.astype(BF),
        "onesb": np.ones((128, 1), BF),
        "onesf": np.ones((1, 128), np.float32),
        "cbank": cb,
    }


def build_L1():
    p = P("L1")
    TH = TPC + 128
    NT = TH // 128

    x_in = p.din("x9", [TH, D], F32)
    wmz = p.din("wmz", [D, 12288], F32)
    wba = p.din("wba", [D, 64], F32)
    cwr = p.din("cwr", [128, 256], F32)
    ln1c = p.din("ln1c", [128, KD], F32)
    alog = p.din("alog", [1, 32], F32)
    dtb = p.din("dtb", [1, 32], F32)
    wgT = p.din("wgT", [D, 5632], I32)
    wuT = p.din("wuT", [D, 5632], I32)
    wdT = p.din("wdT", [5632, D], I32)
    sgc = p.din("sgc", [128, 44], F32)
    suc = p.din("suc", [128, 44], F32)
    sdc = p.din("sdc", [128, KD], F32)
    identb_i = p.din("identb", [128, 128], BF16)
    identf_i = p.din("identf32", [128, 128], F32)
    colsel_i = p.din("colsel_i", [128, 1024], BF16)
    onesb_i = p.din("onesb", [128, 1], BF16)
    onesf_i = p.din("onesf", [1, 128], F32)
    cbank_i = p.din("cbank", [128, 16], F32)

    qkv_o = p.dout("qkv_fm", [8192, TPC], BF16)
    zs_o = p.dout("zs_fm", [4096, TPC], BF16)
    gbl_o = p.dout("gbl", [TPC, 96], F32)
    xnf_o = p.dout("xnf", [D, TPC], BF16)
    f0_o = p.dout("f0f", [D, TPC], F32)
    rows_d = p.dtmp("rows_d", [32, TPC], F32)

    ident = p.sb("ident", [128, 128], BF16)
    identf = p.sb("identf", [128, 128], F32)
    ones_b = p.sb("ones_b", [128, 1], BF16)
    ones_f1 = p.sb("ones_f1", [1, 128], F32)
    cw = p.sb("cw", [128, 256], F32)
    ln1 = p.sb("ln1", [128, KD], F32)
    small = p.sb("small", [1, 64], F32)
    sgs = p.sb("sgs", [128, 44], F32)
    sus = p.sb("sus", [128, 44], F32)
    sds = p.sb("sds", [128, KD], F32)
    colsel = p.sb("colsel", [128, 1024], BF16)

    for v in L1_CONSTS:
        p.constf(v)
    p.dma("sync", ident[:, :], identb_i[:, :], ch="cst")
    p.dma("sync", identf[:, :], identf_i[:, :], ch="cst")
    p.dma("sync", colsel[:, :], colsel_i[:, :], ch="cst")
    p.dma("sync", ones_b[:, :], onesb_i[:, :], ch="cst")
    p.dma("sync", ones_f1[:, :], onesf_i[:, :], ch="cst")
    p.dma("sync", p._cbank[:, :], cbank_i[:, :], ch="cst")

    p.dma("sync", cw[:, :], cwr[:, :], ch="cst")
    p.dma("sync", ln1[:, :], ln1c[:, :], ch="cst")
    p.dma("sync", small[0:1, 0:32], alog[:, :], ch="cst")
    p.dma("sync", small[0:1, 32:64], dtb[:, :], ch="cst")
    p.dma("sync", sgs[:, :], sgc[:, :], ch="cst")
    p.dma("sync", sus[:, :], suc[:, :], ch="cst")
    p.dma("sync", sds[:, :], sdc[:, :], ch="cst")
    m_cst = p.mark("sync")

    sco_c = p.scope()
    hfm = p.sb("hfm", [128, KD * TH], BF16, sco_c)

    sco_b = p.scope()
    xb = [p.sb(f"xb{i}", [128, D], F32, sco_b) for i in range(3)]
    xn_t = [p.sb(f"xn{i}", [128, D], BF16, sco_b) for i in range(NT)]
    h_t = [p.sb(f"h{i}", [128, D], BF16, sco_b) for i in range(NT)]
    st8 = [p.sb(f"st{i}", [128, 8], F32, sco_b) for i in range(3)]
    sq_s = [p.sb(f"sq{i}", [128, D], BF16, sco_b) for i in range(2)]

    m_xn_done = [None] * NT
    m_h_done = [None] * NT
    sq_free = [p.mark("vector")] * 2
    xb_free = [p.mark("vector")] * 3
    st_free = [p.mark("scalar")] * 3
    for i in range(NT):
        b = xb[i % 3]
        p.dma("sync", b[:, :], x_in[i * 128:(i + 1) * 128, :], ch=f"xb{i % 3}",
              deps=[xb_free[i % 3]])
        m_x = p.mark(f"xb{i % 3}")
        s = st8[i % 3]
        p.op("vector", lambda E, b=b, s=s: E.reduce_sum(s[:, 0:1], b[:, :], axis=AX.X),
             deps=[m_x, st_free[i % 3]])
        sq = sq_s[i % 2]
        p.op("scalar", lambda E, b=b, sq=sq: E.activation(
            sq[:, :], b[:, :], AF.Square),
            deps=[m_x, sq_free[i % 2], st_free[i % 3]])
        m_sq = p.mark("scalar")
        p.op("vector", lambda E, s=s, sq=sq: E.reduce_sum(s[:, 1:2], sq[:, :], axis=AX.X),
             deps=[m_sq])
        p.op("vector", lambda E, s=s: E.tensor_scalar(s[:, 2:3], s[:, 0:1], 1.0 / D, None, op0=ALU.mult))
        p.op("vector", lambda E, s=s: E.tensor_scalar(s[:, 3:4], s[:, 1:2], 1.0 / D, None, op0=ALU.mult))
        p.op("vector", lambda E, s=s: E.tensor_tensor(s[:, 4:5], s[:, 2:3], s[:, 2:3], op=ALU.mult))
        p.op("vector", lambda E, s=s: E.tensor_tensor(s[:, 5:6], s[:, 3:4], s[:, 4:5], op=ALU.subtract))
        p.op("vector", lambda E, s=s: E.tensor_scalar(s[:, 4:5], s[:, 5:6], 1e-5, None,
                                                      op0=ALU.add))
        m_v = p.mark("vector")
        p.op("scalar", lambda E, s=s: E.activation(s[:, 6:7], s[:, 5:6], AF.Ln, bias=1e-5),
             deps=[m_v])
        p.op("scalar", lambda E, s=s: E.activation(s[:, 7:8], s[:, 6:7], AF.Exp, scale=-0.5))
        m_r = p.mark("scalar")
        _newton_rsqrt(p, s[:, 7:8], s[:, 4:5], s[:, 6:7], [m_r])
        m_r = p.mark("vector")
        xnt = xn_t[i]
        p.op("vector", lambda E, b=b, s=s, xnt=xnt: E.tensor_scalar(
            xnt[:, :], b[:, :], s[:, 2:3], s[:, 7:8], op0=ALU.subtract, op1=ALU.mult),
            deps=[m_r])
        m_xn = p.mark("vector")
        xb_free[i % 3] = m_xn
        m_xn_done[i] = m_xn
        p.op("scalar", lambda E, xnt=xnt, sq=sq: E.activation(
            sq[:, :], xnt[:, :], AF.Square), deps=[m_xn])
        m_sq2 = p.mark("scalar")
        p.op("vector", lambda E, s=s, sq=sq: E.reduce_sum(s[:, 1:2], sq[:, :], axis=AX.X),
             deps=[m_sq2])
        sq_free[i % 2] = p.mark("vector")
        p.op("vector", lambda E, s=s: E.tensor_scalar(s[:, 3:4], s[:, 1:2], 1.0 / D, None, op0=ALU.mult))
        p.op("vector", lambda E, s=s: E.tensor_scalar(s[:, 4:5], s[:, 3:4], 1e-6, None,
                                                      op0=ALU.add))
        m_v2 = p.mark("vector")
        p.op("scalar", lambda E, s=s: E.activation(s[:, 6:7], s[:, 3:4], AF.Ln, bias=1e-6),
             deps=[m_v2])
        p.op("scalar", lambda E, s=s: E.activation(s[:, 7:8], s[:, 6:7], AF.Exp, scale=-0.5))
        m_r2 = p.mark("scalar")
        _newton_rsqrt(p, s[:, 7:8], s[:, 4:5], s[:, 6:7], [m_r2])
        m_r2 = p.mark("vector")
        ht = h_t[i]
        p.op("vector", lambda E, xnt=xnt, s=s, ht=ht: E.tensor_scalar(
            ht[:, :], xnt[:, :], s[:, 7:8], None, op0=ALU.mult), deps=[m_r2])
        m_h_done[i] = p.mark("vector")
        st_free[i % 3] = p.mark("scalar")

    sco_tr = p.scope()
    xnf = p.sb("xnf_s", [128, KD * TPC], BF16, sco_tr)
    ps_tr = p.ps("pstr", [128, TH], BF16, sco_tr)

    prev_evac = p.mark("vector")
    for kc in range(KD):
        for it in range(NT):
            dd = [m_h_done[it]]
            if it == 0:
                dd += [m_cst, prev_evac]
            p.op("tensor", lambda E, kc=kc, it=it: E.transpose(
                ps_tr[:, it * 128:(it + 1) * 128],
                h_t[it][:, kc * 128:(kc + 1) * 128], ident[:, :]),
                deps=dd, inc=(1 if it == NT - 1 else 0))
        m_t = p.mark("tensor")
        p.op("vector", lambda E, kc=kc: E.tensor_scalar(
            hfm[:, kc * TH:(kc + 1) * TH], ps_tr[:, :], ln1[:, kc:kc + 1], None,
            op0=ALU.mult), deps=[m_t, m_cst])
        m_ev = p.mark("vector")
        for it in range(1, NT):
            dd = [m_xn_done[it]]
            if it == 1:
                dd.append(m_ev)
            p.op("tensor", lambda E, kc=kc, it=it: E.transpose(
                ps_tr[:, (it - 1) * 128:it * 128],
                xn_t[it][:, kc * 128:(kc + 1) * 128], ident[:, :]),
                deps=dd, inc=(1 if it == NT - 1 else 0))
        m_t2 = p.mark("tensor")
        p.op("vector", lambda E, kc=kc: E.tensor_copy(
            xnf[:, kc * TPC:(kc + 1) * TPC], ps_tr[:, 0:TPC]), deps=[m_t2])
        prev_evac = p.mark("vector")
        p.dma("sync", xnf_o[kc * 128:(kc + 1) * 128, :],
              xnf[:, kc * TPC:(kc + 1) * TPC], ch="xnfo", deps=[prev_evac])
    m_hfm = p.mark("vector")
    m_xnf_out = p.mark("xnfo")
    sco_tr.close()
    sco_b.close()

    sco_d = p.scope()
    wpan = [p.sb(f"wpan{i}", [128, 16 * 512], BF16, sco_d) for i in range(2)]
    mixed = p.sb("mixed", [128, TH], BF16, sco_d)
    conv2 = [p.sb(f"conv{i}", [128, TPC], BF16, sco_d) for i in range(2)]
    qkstore = p.sb("qkstore", [128, 32 * TPC], BF16, sco_d)
    sqb = [p.sb(f"sqb{i}", [128, TPC], BF16, sco_d) for i in range(2)]
    diag4 = [p.sb(f"diag4{i}", [128, 512], BF16, sco_d) for i in range(2)]
    zsb = [p.sb(f"zsb{i}", [128, TPC], BF16, sco_d) for i in range(2)]
    rows32 = p.sb("rows32", [32, TPC], F32, sco_d)
    ps_mm = [p.ps(f"psmm{i}", [128, 512], scope=sco_d) for i in range(2)]
    ps_cv = p.ps("pscv", [128, TPC], scope=sco_d)
    ps_rows = p.ps("psrows", [32, TPC], scope=sco_d)

    NCG = 24
    wpan_free = [p.mark("tensor")] * 2
    m_wpan = {}

    def _fetch_panel(cg):
        p.dma("gpsimd", wpan[cg % 2][:, :].rearrange("p (kc c) -> p kc c", kc=KD),
              wmz[:, cg * 512:(cg + 1) * 512].rearrange("(kc p) c -> p kc c", p=128),
              ch=f"wp{cg % 2}", deps=[wpan_free[cg % 2]])
        m_wpan[cg] = p.mark(f"wp{cg % 2}")

    _fetch_panel(0)
    _fetch_panel(1)

    ps_free = [p.mark("vector"), p.mark("vector")]
    conv_free = [p.mark("sync")] * 2
    cv_ps_free = p.mark("scalar")
    sqb_free = [p.mark("tensor")] * 2
    zsb_free = [p.mark("sync")] * 2
    diag_free = [p.mark("tensor")] * 2
    mixed_free = p.mark("tensor")
    nqk = 0
    for cg in range(NCG):
        is_mixed = cg < 16
        for j in range(4):
            cb = cg * 4 + j
            spans = [(0, 512), (512, 512), (1024, 128)] if is_mixed \
                else [(128, 512), (640, 512)]
            ev_marks = []
            for si, (t0, tsz) in enumerate(spans):
                pm = ps_mm[si % 2]
                for kc in range(KD):
                    dd = []
                    if kc == 0:
                        dd = [m_wpan[cg], m_hfm, ps_free[si % 2]]
                    p.op("tensor", lambda E, kc=kc, cg=cg, j=j, t0=t0, tsz=tsz, pm=pm: E.matmul(
                        pm[:, 0:tsz],
                        wpan[cg % 2][:, kc * 512 + j * 128: kc * 512 + (j + 1) * 128],
                        hfm[:, kc * TH + t0: kc * TH + t0 + tsz],
                        start=(kc == 0), stop=(kc == KD - 1)),
                        deps=dd, inc=(1 if kc == KD - 1 else 0))
                m_mm = p.mark("tensor")
                if is_mixed:
                    dd = [m_mm]
                    if si == 0:
                        dd.append(mixed_free)
                    p.op("vector", lambda E, t0=t0, tsz=tsz, pm=pm: E.tensor_copy(
                        mixed[:, t0:t0 + tsz], pm[:, 0:tsz]), deps=dd)
                    ps_free[si % 2] = p.mark("vector")
                else:
                    zt = zsb[cb % 2]
                    dd = [m_mm]
                    if si == 0:
                        dd.append(zsb_free[cb % 2])
                    p.op("scalar", lambda E, t0=t0, tsz=tsz, pm=pm, zt=zt: E.activation(
                        zt[:, t0 - 128:t0 - 128 + tsz], pm[:, 0:tsz], AF.Silu), deps=dd)
                    ps_free[si % 2] = p.mark("scalar")
            if not is_mixed:
                m_z = p.mark("scalar")
                zrow = cb - 64
                p.dma("sync", zs_o[zrow * 128:(zrow + 1) * 128, :], zsb[cb % 2][:, :],
                      ch=f"zs{cb % 2}", deps=[m_z])
                zsb_free[cb % 2] = p.mark(f"zs{cb % 2}")
                continue
            m_mixed = p.mark("vector")
            cwv = cw[:, cb * 4:(cb + 1) * 4].unsqueeze(2).broadcast_to([128, 4, 128])
            identv = identf[:, :].unsqueeze(1).broadcast_to([128, 4, 128])
            dg = diag4[cb % 2]
            p.op("vector", lambda E, dg=dg, identv=identv, cwv=cwv: E.tensor_tensor(
                dg[:, :].rearrange("p (a b) -> p a b", a=4), identv, cwv, op=ALU.mult),
                deps=[m_cst, diag_free[cb % 2]])
            m_dg = p.mark("vector")
            cvt = conv2[cb % 2]
            for t2 in range(2):
                for i4 in range(4):
                    dd = [m_dg, m_mixed, cv_ps_free] if (t2 == 0 and i4 == 0) else []
                    p.op("tensor", lambda E, t2=t2, i4=i4, dg=dg: E.matmul(
                        ps_cv[:, t2 * 512:(t2 + 1) * 512], dg[:, i4 * 128:(i4 + 1) * 128],
                        mixed[:, 125 + t2 * 512 + i4: 125 + t2 * 512 + i4 + 512],
                        start=(i4 == 0), stop=(i4 == 3)),
                        deps=dd, inc=(1 if (t2 == 1 and i4 == 3) else 0))
            m_cv = p.mark("tensor")
            mixed_free = m_cv
            diag_free[cb % 2] = m_cv
            p.op("scalar", lambda E, cvt=cvt: E.activation(cvt[:, :], ps_cv[:, :], AF.Silu),
                 deps=[m_cv, conv_free[cb % 2]])
            m_cs = p.mark("scalar")
            cv_ps_free = m_cs
            if cb >= 32:
                p.dma("sync", qkv_o[cb * 128:(cb + 1) * 128, :], cvt[:, :],
                      ch=f"cv{cb % 2}", deps=[m_cs])
                conv_free[cb % 2] = p.mark(f"cv{cb % 2}")
            else:
                p.op("vector", lambda E, cb=cb, cvt=cvt: E.tensor_copy(
                    qkstore[:, cb * TPC:(cb + 1) * TPC], cvt[:, :]), deps=[m_cs])
                conv_free[cb % 2] = p.mark("vector")
                sq2 = sqb[cb % 2]
                p.op("scalar", lambda E, cvt=cvt, sq2=sq2: E.activation(
                    sq2[:, :], cvt[:, :], AF.Square), deps=[sqb_free[cb % 2]])
                m_sq3 = p.mark("scalar")
                for half in range(2):
                    p.op("tensor", lambda E, cb=cb, half=half, sq2=sq2: E.matmul(
                        ps_rows[:, half * 512:(half + 1) * 512],
                        colsel[:, cb * 32:(cb + 1) * 32],
                        sq2[:, half * 512:(half + 1) * 512],
                        start=(cb == 0), stop=(cb == 31)),
                        deps=([m_sq3, m_cst] if half == 0 else []),
                        inc=(1 if half == 1 else 0))
                sqb_free[cb % 2] = p.mark("tensor")
                nqk += 1
        wpan_free[cg % 2] = p.mark("tensor")
        if cg + 2 < NCG:
            _fetch_panel(cg + 2)

    m_rowsmm = p.mark("tensor")
    rows32v = p.sb("rows32v", [32, TPC], F32, sco_d)
    rows32t = p.sb("rows32t", [32, TPC], F32, sco_d)
    p.op("vector", lambda E: E.tensor_scalar(rows32v[:, :], ps_rows[:, :], 1e-6, None,
                                             op0=ALU.add), deps=[m_rowsmm])
    p.op("vector", lambda E: E.tensor_copy(rows32[:, :], ps_rows[:, :]))
    m_rows = p.mark("vector")
    p.op("scalar", lambda E: E.activation(rows32[:, :], rows32[:, :], AF.Ln, bias=1e-6),
         deps=[m_rows])
    p.op("scalar", lambda E: E.activation(rows32[:, :], rows32[:, :], AF.Exp, scale=-0.5))
    m_rstd = p.mark("scalar")
    _newton_rsqrt(p, rows32[:, :], rows32v[:, :], rows32t[:, :], [m_rstd])
    m_rstd = p.mark("vector")
    p.op("vector", lambda E: E.tensor_scalar(rows32[0:16, :], rows32[0:16, :],
                                             float(128.0 ** -0.5), None, op0=ALU.mult),
         deps=[m_rstd])
    m_rstd2 = p.mark("vector")
    p.dma("sync", rows_d[:, :], rows32[:, :], ch="rw", deps=[m_rstd2])
    m_rows_out = p.mark("rw")

    sco_ap = p.scope()
    flatr = p.sb("flatr", [1, 2 * TPC], F32, sco_ap)
    ps_bc = p.ps("psbc", [128, TPC], scope=sco_ap)
    bc_free = p.mark("vector")
    fl_free = p.mark("tensor")
    for grp in range(16):
        p.dma("sync", flatr[:, :],
              rows_d[grp * 2:(grp + 1) * 2, :].rearrange("a b -> (a b)").unsqueeze(0),
              ch="fl", deps=[m_rows_out, fl_free])
        m_fl = p.mark("fl")
        for jj in range(2):
            cb = grp * 2 + jj
            for half in range(2):
                dd = [m_fl, bc_free] if half == 0 else []
                p.op("tensor", lambda E, cb=cb, jj=jj, half=half: E.matmul(
                    ps_bc[:, half * 512:(half + 1) * 512], ones_f1[:, :],
                    flatr[0:1, jj * TPC + half * 512: jj * TPC + (half + 1) * 512],
                    start=True, stop=True),
                    deps=dd, inc=(1 if half == 1 else 0))
            m_bc = p.mark("tensor")
            if jj == 1:
                fl_free = m_bc
            cvt2 = conv2[cb % 2]
            p.op("vector", lambda E, cb=cb, cvt2=cvt2: E.tensor_tensor(
                cvt2[:, :], ps_bc[:, :], qkstore[:, cb * TPC:(cb + 1) * TPC],
                op=ALU.mult), deps=[m_bc, conv_free[cb % 2]])
            m_q = p.mark("vector")
            bc_free = m_q
            p.dma("sync", qkv_o[cb * 128:(cb + 1) * 128, :], cvt2[:, :],
                  ch=f"cv{cb % 2}", deps=[m_q])
            conv_free[cb % 2] = p.mark(f"cv{cb % 2}")

    sco_ba = p.scope()
    wbas = p.sb("wbas", [128, 16 * 64], BF16, sco_ba)
    dtb_bc = p.sb("dtb_bc", [128, 32], F32, sco_ba)
    nea_bc = p.sb("nea_bc", [128, 32], F32, sco_ba)
    ba_s = p.sb("ba_s", [128, 64], F32, sco_ba)
    gbl_s = [p.sb(f"gbl{i}", [128, 96], F32, sco_ba) for i in range(2)]
    scr32 = p.sb("scr32", [128, 32], F32, sco_ba)
    scr32b = p.sb("scr32b", [128, 32], F32, sco_ba)

    p.dma("gpsimd", wbas[:, :].rearrange("p (kc c) -> p kc c", kc=KD),
          wba[:, :].rearrange("(kc p) c -> p kc c", p=128), ch="wba")
    m_wba = p.mark("wba")
    p.op("scalar", lambda E: E.activation(small[0:1, 0:32], small[0:1, 0:32], AF.Exp),
         deps=[m_cst])
    m_sc1 = p.mark("scalar")
    p.op("vector", lambda E: E.tensor_scalar(small[0:1, 0:32], small[0:1, 0:32], -1.0,
                                             None, op0=ALU.mult), deps=[m_sc1])
    m_sm = p.mark("vector")
    p.op("tensor", lambda E: E.matmul(ps_bc[:, 0:32], ones_f1[:, :], small[0:1, 0:32],
                                      start=True, stop=True),
         deps=[m_sm, bc_free])
    p.op("tensor", lambda E: E.matmul(ps_bc[:, 32:64], ones_f1[:, :], small[0:1, 32:64],
                                      start=True, stop=True))
    m_bc2 = p.mark("tensor")
    p.op("vector", lambda E: E.tensor_copy(nea_bc[:, :], ps_bc[:, 0:32]), deps=[m_bc2])
    p.op("vector", lambda E: E.tensor_copy(dtb_bc[:, :], ps_bc[:, 32:64]))
    m_bcs = p.mark("vector")

    m_sca_d = p.mark("scalar")
    gbl_free = [p.mark("sync")] * 2
    ba_ps_free = [m_sca_d, p.mark("vector")]
    ba_war = p.mark("scalar")
    for tb in range(8):
        pm = ps_mm[0]
        for kc in range(KD):
            dd = ([m_wba, m_hfm, m_bcs] + ba_ps_free) if kc == 0 else []
            p.op("tensor", lambda E, tb=tb, kc=kc, pm=pm: E.matmul(
                pm[:, 0:64],
                hfm[:, kc * TH + 128 + tb * 128: kc * TH + 256 + tb * 128],
                wbas[:, kc * 64:(kc + 1) * 64],
                start=(kc == 0), stop=(kc == KD - 1)),
                deps=dd, inc=(1 if kc == KD - 1 else 0))
        m_ba = p.mark("tensor")
        p.op("vector", lambda E, pm=pm: E.tensor_copy(ba_s[:, :], pm[:, 0:64]),
             deps=[m_ba, ba_war])
        m_bas = p.mark("vector")
        ba_ps_free = [m_bas]
        gb = gbl_s[tb % 2]
        p.op("vector", lambda E: E.tensor_tensor(scr32[:, :], ba_s[:, 32:64],
                                                 dtb_bc[:, :], op=ALU.add), deps=[m_bas])
        m1 = p.mark("vector")
        p.op("scalar", lambda E: E.activation(scr32b[:, :], scr32[:, :], AF.Exp,
                                              scale=-1.0), deps=[m1])
        p.op("scalar", lambda E: E.activation(scr32b[:, :], scr32b[:, :], AF.Ln, bias=1.0))
        m2 = p.mark("scalar")
        p.op("vector", lambda E: E.tensor_tensor(scr32[:, :], scr32[:, :], scr32b[:, :],
                                                 op=ALU.add), deps=[m2])
        p.op("vector", lambda E, gb=gb: E.tensor_tensor(
            gb[:, 0:32], scr32[:, :], nea_bc[:, :], op=ALU.mult),
            deps=[gbl_free[tb % 2]])
        m3 = p.mark("vector")
        p.op("scalar", lambda E: E.activation(scr32b[:, :], ba_s[:, 0:32], AF.Exp,
                                              scale=-1.0), deps=[m3])
        p.op("scalar", lambda E: E.activation(scr32b[:, :], scr32b[:, :], AF.Ln, bias=1.0))
        m4 = p.mark("scalar")
        p.op("vector", lambda E, gb=gb: E.tensor_scalar(
            gb[:, 64:96], scr32b[:, :], -1.0, None, op0=ALU.mult), deps=[m4])
        m5 = p.mark("vector")
        p.op("scalar", lambda E, gb=gb: E.activation(gb[:, 32:64], gb[:, 64:96], AF.Exp),
             deps=[m5])
        m6 = p.mark("scalar")
        ba_war = m6
        p.dma("sync", gbl_o[tb * 128:(tb + 1) * 128, :], gb[:, :],
              ch=f"gbl{tb % 2}", deps=[m6])
        gbl_free[tb % 2] = p.mark(f"gbl{tb % 2}")

    sco_ba.close()
    sco_ap.close()
    sco_d.close()
    sco_c.close()

    barrier = p.allmarks()
    sco_e = p.scope()
    msb = p.sb("msb", [128, 44 * TPC], BF16, sco_e)
    ps_g = [p.ps(f"psg{i}", [128, 512], scope=sco_e) for i in range(2)]
    sco_g = p.scope()
    xnf2 = p.sb("xnf2_s", [128, KD * TPC], BF16, sco_g)
    wq2 = [p.sb(f"wq2{i}", [128, 16 * 512], BF16, sco_g) for i in range(2)]
    gu = [p.sb(f"gu{i}", [128, TPC], BF16, sco_g) for i in range(2)]

    p.dma("sync", xnf2[:, :].rearrange("p (kc t) -> p kc t", kc=KD),
          xnf_o[:, :].rearrange("(kc p) t -> p kc t", p=128),
          ch="xnf2", deps=barrier + [m_xnf_out])
    m_xnf2 = p.mark("xnf2")

    wq2_free = [barrier[4], barrier[4]]
    gu_free = [p.mark("vector")] * 2
    psg_free = [p.mark("scalar")] * 2
    msb_marks = []
    for hg in range(11):
        bg, bu = wq2[0], wq2[1]
        p.dma("gpsimd", bg[:, :].rearrange("p (kc h) -> p kc h", kc=KD),
              wgT[:, hg * 512:(hg + 1) * 512].rearrange("(kc p) h -> p kc h", p=128),
              ch="wg", deps=[wq2_free[0]] + (barrier if hg == 0 else []))
        mg = p.mark("wg")
        p.dma("gpsimd", bu[:, :].rearrange("p (kc h) -> p kc h", kc=KD),
              wuT[:, hg * 512:(hg + 1) * 512].rearrange("(kc p) h -> p kc h", p=128),
              ch="wu", deps=[wq2_free[1]])
        mu = p.mark("wu")
        p.op("vector", lambda E, bg=bg: E.tensor_scalar(bg[:, :], bg[:, :], 128.0, None,
                                                        op0=ALU.subtract), deps=[mg])
        m_g128 = p.mark("vector")
        p.op("vector", lambda E, bu=bu: E.tensor_scalar(bu[:, :], bu[:, :], 128.0, None,
                                                        op0=ALU.subtract), deps=[mu])
        m_u128 = p.mark("vector")
        for j in range(4):
            hb = hg * 4 + j
            for which in range(2):
                wsrc = bg if which == 0 else bu
                mdep = m_g128 if which == 0 else m_u128
                pm = ps_g[which]
                gut = gu[which]
                for t2 in range(2):
                    for kc in range(KD):
                        dd = [mdep, m_xnf2, psg_free[which]] if kc == 0 else []
                        p.op("tensor", lambda E, kc=kc, j=j, t2=t2, wsrc=wsrc, pm=pm: E.matmul(
                            pm[:, 0:512],
                            wsrc[:, kc * 512 + j * 128: kc * 512 + (j + 1) * 128],
                            xnf2[:, kc * TPC + t2 * 512: kc * TPC + (t2 + 1) * 512],
                            start=(kc == 0), stop=(kc == KD - 1)),
                            deps=dd, inc=(1 if kc == KD - 1 else 0))
                    m_pm = p.mark("tensor")
                    if which == 0:
                        p.op("scalar", lambda E, hb=hb, t2=t2, pm=pm, gut=gut: E.activation(
                            gut[:, t2 * 512:(t2 + 1) * 512], pm[:, 0:512], AF.Silu,
                            scale=sgs[:, hb:hb + 1]),
                            deps=[m_pm] + ([gu_free[0]] if t2 == 0 else []))
                    else:
                        p.op("scalar", lambda E, hb=hb, t2=t2, pm=pm, gut=gut: E.activation(
                            gut[:, t2 * 512:(t2 + 1) * 512], pm[:, 0:512], AF.Copy,
                            scale=sus[:, hb:hb + 1]),
                            deps=[m_pm] + ([gu_free[1]] if t2 == 0 else []))
                    psg_free[which] = p.mark("scalar")
            m_gu = p.mark("scalar")
            p.op("vector", lambda E, hb=hb: E.tensor_tensor(
                msb[:, hb * TPC:(hb + 1) * TPC], gu[0][:, :], gu[1][:, :], op=ALU.mult),
                deps=[m_gu])
            gu_free[0] = gu_free[1] = p.mark("vector")
            msb_marks.append(p.mark("vector"))
        wq2_free = [p.mark("tensor"), p.mark("tensor")]
    m_msb = msb_marks[-1]
    sco_g.close()

    sco_dn = p.scope()
    wdp = p.sb("wdp", [128, 44 * 512], BF16, sco_dn)
    f0s = [p.sb(f"f0s{i}", [128, TPC], F32, sco_dn) for i in range(2)]
    f0_free = [p.mark("sync")] * 2
    wdp_free = p.mark("tensor")
    psd_free = [p.mark("vector")] * 2
    for og in range(4):
        p.dma("gpsimd", wdp[:, :].rearrange("p (kc o) -> p kc o", kc=44),
              wdT[:, og * 512:(og + 1) * 512].rearrange("(kc p) o -> p kc o", p=128),
              ch="wd", deps=[wdp_free] + (barrier if og == 0 else []))
        m_wd = p.mark("wd")
        p.op("vector", lambda E: E.tensor_scalar(wdp[:, :], wdp[:, :], 128.0, None,
                                                 op0=ALU.subtract), deps=[m_wd])
        m_wd2 = p.mark("vector")
        for j in range(4):
            ob = og * 4 + j
            pm = ps_g[j % 2]
            f0t = f0s[ob % 2]
            for t2 in range(2):
                for kc in range(44):
                    dd = [m_wd2, m_msb, psd_free[j % 2]] if kc == 0 else []
                    p.op("tensor", lambda E, kc=kc, j=j, t2=t2, pm=pm: E.matmul(
                        pm[:, 0:512],
                        wdp[:, kc * 512 + j * 128: kc * 512 + (j + 1) * 128],
                        msb[:, kc * TPC + t2 * 512: kc * TPC + (t2 + 1) * 512],
                        start=(kc == 0), stop=(kc == 43)),
                        deps=dd, inc=(1 if kc == 43 else 0))
                m_dn = p.mark("tensor")
                p.op("vector", lambda E, ob=ob, t2=t2, pm=pm, f0t=f0t: E.tensor_scalar(
                    f0t[:, t2 * 512:(t2 + 1) * 512], pm[:, 0:512], sds[:, ob:ob + 1],
                    None, op0=ALU.mult),
                    deps=[m_dn] + ([f0_free[ob % 2]] if t2 == 0 else []))
                psd_free[j % 2] = p.mark("vector")
            m_f0 = p.mark("vector")
            p.dma("sync", f0_o[ob * 128:(ob + 1) * 128, :], f0s[ob % 2][:, :],
                  ch=f"f0{ob % 2}", deps=[m_f0])
            f0_free[ob % 2] = p.mark(f"f0{ob % 2}")
        wdp_free = p.mark("tensor")

    sco_dn.close()
    sco_e.close()
    p.wait("sync", p.allmarks())
    return p.build()




L2_NCH = 64
L2_STAGE = 9
L2_INVL = 6
L2_BARRIER = False
L2_SKIPEV = False


def build_L2():
    p = P("L2")
    NCH = L2_NCH
    NG = max(1, NCH // 4)
    qf = p.din("qf", [256, L], BF16)
    kf = p.din("kf", [256, L], BF16)
    vf = p.din("vf", [512, L], BF16)
    gbp = p.din("gbp", [128, 64 * 12], F32)
    identb_i = p.din("identb", [128, 128], BF16)
    identf_i = p.din("identf32", [128, 128], F32)
    onesf_i = p.din("onesf", [1, 128], F32)
    onest_i = p.din("onest", [128, 128], F32)
    mTs_i = p.din("mTs", [128, 128], F32)
    mTi_i = p.din("mTi", [128, 128], F32)
    mLs_i = p.din("mLs", [128, 128], F32)
    cbank_i = p.din("cbank", [128, 16], F32)
    cno_o = p.dout("cno", [512, L], BF16)
    rows_d = p.dtmp("rows_d2", [2, 16, 128], F32)

    for v in L1_CONSTS:
        p.constf(v)

    ident = p.sb("ident", [128, 128], BF16)
    identf = p.sb("identf", [128, 128], F32)
    onesf = p.sb("onesf_s", [1, 128], F32)
    onest = p.sb("onest_s", [128, 128], F32)
    mTs = p.sb("mTs_s", [128, 128], F32)
    mTi = p.sb("mTi_s", [128, 128], F32)
    mLs = p.sb("mLs_s", [128, 128], F32)
    gb = p.sb("gb_s", [128, 64 * 12], F32)
    n_sb = p.sb("n_sb", [128, 512], F32)
    n_bf = p.sb("n_bf", [128, 512], BF16)
    cno_sb = p.sb("cno_sb", [128, 4 * L], BF16)
    qf4 = [p.sb(f"qf4{i}", [128, 1024], BF16) for i in range(2)]
    kf4 = [p.sb(f"kf4{i}", [128, 1024], BF16) for i in range(2)]
    vf4 = [p.sb(f"vf4{i}", [128, 2048], BF16) for i in range(2)]
    rowb = p.sb("rowb", [48, 128], F32)
    lnbr = p.sb("lnbr", [16, 128], F32)
    brow = p.sb("brow", [16, 128], F32)
    bpl = p.sb("bpl", [16, 128], F32)
    bf_f = p.sb("bf_f", [1, 2048], F32)
    bplf = p.sb("bplf", [1, 2048], F32)
    ebf = p.sb("ebf", [1, 2048], F32)
    gcol = p.sb("gcol", [128, 64], F32)
    ebendb = p.sb("ebendb", [128, 16], F32)
    ebmb = p.sb("ebmb", [128, 16], F32)
    kkn = p.sb("kkn", [128, 256], BF16)
    qks = p.sb("qks", [128, 256], BF16)
    kvtm = p.sb("kvtm", [128, 768], BF16)
    dscr = p.sb("dscr", [128, 512], F32)
    expb = p.sb("expb", [128, 512], BF16)
    TAp = [p.sb(f"TA{i}", [128, 512], BF16) for i in range(2)]
    TBp = [p.sb(f"TB{i}", [128, 512], BF16) for i in range(2)]
    Aqs = p.sb("Aqs", [128, 512], BF16)
    Ysb = p.sb("Ysb", [128, 512], BF16)
    Tt = p.sb("Tt", [128, 512], BF16)
    khat = p.sb("khat", [128, 512], BF16)
    qhat = p.sb("qhat", [128, 512], BF16)
    tmpf = p.sb("tmpf", [128, 512], F32)
    x1s = p.sb("x1s", [128, 512], BF16)
    usb = p.sb("usb", [128, 512], BF16)
    sqo = p.sb("sqo", [128, 512], BF16)
    rms4 = p.sb("rms4", [128, 8], F32)
    kbar = p.sb("kbar", [128, 512], BF16)
    nsc = p.sb("nsc", [128, 512], F32)
    corn = p.sb("corn", [128, 512], BF16)
    psA = p.ps("psA", [128, 512])
    psT = p.ps("psT", [128, 768], BF16)
    psB = p.ps("psB", [128, 512])
    psQ1 = p.ps("psQ1", [128, 512])
    psQ2 = p.ps("psQ2", [128, 512])
    psY = p.ps("psY", [128, 512])
    psS = p.ps("psS", [128, 128])

    p.dma("sync", ident[:, :], identb_i[:, :], ch="cst")
    p.dma("sync", identf[:, :], identf_i[:, :], ch="cst")
    p.dma("sync", onesf[:, :], onesf_i[:, :], ch="cst")
    p.dma("sync", onest[:, :], onest_i[:, :], ch="cst")
    p.dma("sync", mTs[:, :], mTs_i[:, :], ch="cst")
    p.dma("sync", mTi[:, :], mTi_i[:, :], ch="cst")
    p.dma("sync", mLs[:, :], mLs_i[:, :], ch="cst")
    p.dma("sync", gb[:, :], gbp[:, :], ch="cst")
    p.dma("sync", p._cbank[:, :], cbank_i[:, :], ch="cst")
    m_cst = p.mark("cst")
    p.const_barrier = m_cst
    p.op("vector", lambda E: E.memset(n_sb[:, :], 0.0))
    p.op("vector", lambda E: E.memset(n_bf[:, :], 0.0))

    def rep4(ap):
        return ap.unsqueeze(2).broadcast_to([128, 4, 128])

    def out3(ap):
        return ap.rearrange("p (a b) -> p a b", a=4)

    def bcast_mask(m):
        return m[:, :].unsqueeze(1).broadcast_to([128, 4, 128])

    def khrep(ap256):
        return ap256.rearrange("p (j d) -> p j d", j=2).unsqueeze(2) \
                    .broadcast_to([128, 2, 2, 128])

    def kchunk_view(buf, cc):
        return buf[:, :].rearrange("p (j q) -> p j q", j=2)[:, :, cc * 128:(cc + 1) * 128] \
                        .unsqueeze(2).broadcast_to([128, 2, 2, 128])

    grp_dma = {}
    buf_free = [[p.mark("tensor")], [p.mark("tensor")]]

    def stage1(g):
        bsel = g % 2
        dd = list(buf_free[bsel])
        for j in range(2):
            p.dma("sync", qf4[bsel][:, j * 512:(j + 1) * 512],
                  qf[j * 128:(j + 1) * 128, g * 512:(g + 1) * 512],
                  ch=f"grp{bsel}", deps=dd)
            p.dma("sync", kf4[bsel][:, j * 512:(j + 1) * 512],
                  kf[j * 128:(j + 1) * 128, g * 512:(g + 1) * 512],
                  ch=f"grp{bsel}", deps=dd)
        for h in range(4):
            p.dma("sync", vf4[bsel][:, h * 512:(h + 1) * 512],
                  vf[h * 128:(h + 1) * 128, g * 512:(g + 1) * 512],
                  ch=f"grp{bsel}", deps=dd)
        grp_dma[g] = p.mark(f"grp{bsel}")

    m_g1 = {}

    def stage1b(g, war):
        dd = [m_cst] + war
        p.op("tensor", lambda E, g=g: E.transpose(
            psS[0:48, 0:128], gb[:, g * 48:(g + 1) * 48], identf[:, :]), deps=dd)
        m_t = p.mark("tensor")
        p.op("vector", lambda E: E.tensor_copy(rowb[:, :], psS[0:48, 0:128]),
             deps=[m_t] + war)
        p.op("vector", lambda E: E.tensor_copy(lnbr[:, :], psS[32:48, 0:128]),
             deps=[m_t])
        p.op("vector", lambda E: E.tensor_tensor_scan(
            brow[:, :], onest[0:16, :], rowb[0:16, :], 0.0, op0=ALU.mult, op1=ALU.add))
        p.op("vector", lambda E: E.tensor_tensor(bpl[:, :], brow[:, :], lnbr[:, :],
                                                 op=ALU.add))
        m_rows = p.mark("vector")
        p.dma("sync", rows_d[0, :, :], brow[:, :], ch="rw", deps=[m_rows] + war)
        p.dma("sync", rows_d[1, :, :], bpl[:, :], ch="rw", deps=[m_rows])
        m_w = p.mark("rw")
        p.dma("sync", bf_f[:, :], rows_d[0:1, :, :].rearrange("a b c -> a (b c)"),
              ch="fl", deps=[m_w] + war)
        p.dma("sync", bplf[:, :], rows_d[1:2, :, :].rearrange("a b c -> a (b c)"),
              ch="fl", deps=[m_w])
        m_fl = p.mark("fl")
        p.op("scalar", lambda E: E.activation(ebf[:, :], bf_f[:, :], AF.Exp),
             deps=[m_fl] + war)
        m_ebf = p.mark("scalar")
        p.op("tensor", lambda E: E.transpose(psS[:, 0:16], brow[:, :], identf[0:16, 0:16]),
             deps=[m_rows])
        m_t2 = p.mark("tensor")
        p.op("vector", lambda E: E.tensor_copy(gcol[:, 0:16], psS[:, 0:16]),
             deps=[m_t2])
        m_v1 = p.mark("vector")
        p.op("tensor", lambda E: E.transpose(psS[:, 16:32], bpl[:, :], identf[0:16, 0:16]),
             deps=[m_v1])
        m_t3 = p.mark("tensor")
        p.op("vector", lambda E: E.tensor_copy(gcol[:, 16:32], psS[:, 16:32]),
             deps=[m_t3])
        m_v2 = p.mark("vector")
        p.op("tensor", lambda E: E.matmul(psS[:, 32:48], onesf[:, :],
                                          bf_f[0:1, 127::128], start=True, stop=True),
             deps=[m_fl, m_v2])
        m_t4 = p.mark("tensor")
        p.op("vector", lambda E: E.tensor_copy(gcol[:, 32:48], psS[:, 32:48]),
             deps=[m_t4])
        m_v3 = p.mark("vector")
        p.op("scalar", lambda E: E.activation(ebendb[:, :], gcol[:, 32:48], AF.Exp),
             deps=[m_v3] + war)
        p.op("vector", lambda E: E.tensor_tensor(gcol[:, 48:64], gcol[:, 32:48],
                                                 gcol[:, 0:16], op=ALU.subtract))
        m_v4 = p.mark("vector")
        p.op("scalar", lambda E: E.activation(ebmb[:, :], gcol[:, 48:64], AF.Exp),
             deps=[m_v4])
        m_g1[g] = (p.mark("vector"), p.mark("scalar"), p.mark("tensor"))

    stage1(0)
    if NG > 1:
        stage1(1)
    stage1b(0, [])

    war_psA = p.mark("scalar")
    war_psT = p.mark("scalar")
    war_psB = p.mark("vector")
    war_psQa = p.mark("vector")
    war_psQb = p.mark("vector")
    war_psY = p.mark("scalar")
    war_kkn = p.mark("tensor")
    war_kvtm = p.mark("tensor")
    war_dscr = p.mark("scalar")
    war_expb = p.mark("vector")
    war_TA = p.mark("tensor")
    war_TB = p.mark("tensor")
    war_Aqs = p.mark("tensor")
    war_Y = p.mark("tensor")
    war_Tt = p.mark("tensor")
    war_khat = p.mark("tensor")
    war_usb = p.mark("tensor")
    war_sqo = p.mark("vector")
    war_corn = p.mark("tensor")
    war_n = p.mark("vector")
    war_x1 = p.mark("tensor")
    war_nsc = p.mark("tensor")
    war_kbar = p.mark("tensor")

    for c in range(NCH):
        g, cc = c // 4, c % 4
        bsel = g % 2
        mv, ms, mt = m_g1[g]
        mdma = grp_dma[g]

        def ksl(j, cc=cc, bsel=bsel):
            return kf4[bsel][:, j * 512 + cc * 128: j * 512 + cc * 128 + 128]

        def qsl(j, cc=cc, bsel=bsel):
            return qf4[bsel][:, j * 512 + cc * 128: j * 512 + cc * 128 + 128]

        def vsl(h, cc=cc, bsel=bsel):
            return vf4[bsel][:, h * 512 + cc * 128: h * 512 + cc * 128 + 128]

        bcol = gcol[:, cc * 4:cc * 4 + 4]
        bplcol = gcol[:, 16 + cc * 4: 16 + cc * 4 + 4]
        betacol = gb[:, g * 48 + 16 + cc * 4: g * 48 + 16 + cc * 4 + 4]
        ebendc = ebendb[:, cc * 4:cc * 4 + 4]
        ebmbc = ebmb[:, cc * 4:cc * 4 + 4]

        for j in range(2):
            p.op("tensor", lambda E, j=j, ksl=ksl: E.matmul(
                psA[:, j * 128:(j + 1) * 128], ksl(j), ksl(j), start=True, stop=True),
                deps=[mdma, war_psA] if j == 0 else [], inc=0)
            p.op("tensor", lambda E, j=j, ksl=ksl, qsl=qsl: E.matmul(
                psA[:, 256 + j * 128: 256 + (j + 1) * 128], ksl(j), qsl(j),
                start=True, stop=True), inc=(1 if j == 1 else 0))
        m_kq = p.mark("tensor")
        p.op("scalar", lambda E: E.activation(kkn[:, :], psA[:, 0:256], AF.Copy,
                                              scale=-1.0), deps=[m_kq, war_kkn])
        p.op("scalar", lambda E: E.activation(qks[:, :], psA[:, 256:512], AF.Copy))
        m_kkn = p.mark("scalar")
        war_psA = m_kkn
        war_kkn = None

        for j in range(2):
            p.op("tensor", lambda E, j=j, ksl=ksl: E.transpose(
                psT[:, j * 128:(j + 1) * 128], ksl(j), ident[:, :]),
                deps=[war_psT, mdma] if j == 0 else [], inc=0)
        for h in range(4):
            p.op("tensor", lambda E, h=h, vsl=vsl: E.transpose(
                psT[:, 256 + h * 128: 256 + (h + 1) * 128], vsl(h), ident[:, :]),
                inc=(1 if h == 3 else 0))
        m_tv = p.mark("tensor")
        p.op("scalar", lambda E: E.activation(kvtm[:, :], psT[:, :], AF.Copy),
             deps=[m_tv, war_kvtm])
        m_kv = p.mark("scalar")
        war_psT = m_kv
        war_kvtm = None
        ktm_view = khrep(kvtm[:, 0:256])
        vtm = kvtm[:, 256:768]
        if L2_STAGE <= 1:
            continue

        p.op("tensor", lambda E, cc=cc: E.matmul(
            psB[:, :], onesf[:, :], bplf[0:1, cc * 512:(cc + 1) * 512],
            start=True, stop=True), deps=[war_psB, mv, ms], inc=1)
        m_b1 = p.mark("tensor")
        for h in range(4):
            p.op("vector", lambda E, h=h, bcol=bcol: E.tensor_scalar(
                dscr[:, h * 128:(h + 1) * 128], psB[:, h * 128:(h + 1) * 128],
                bcol[:, h:h + 1], -1.0, op0=ALU.subtract, op1=ALU.mult),
                deps=[m_b1, war_dscr] if h == 0 else [])
        p.op("vector", lambda E: E.tensor_tensor(out3(dscr), out3(dscr), bcast_mask(mTs),
                                                 op=ALU.add))
        m_d1 = p.mark("vector")
        war_psB = m_d1
        p.op("scalar", lambda E: E.activation(expb[:, :], dscr[:, :], AF.Exp,
                                              scale=-1.0), deps=[m_d1, war_expb])
        m_e1 = p.mark("scalar")
        for h in range(4):
            p.op("vector", lambda E, h=h: E.tensor_tensor(
                TAp[0][:, h * 128:(h + 1) * 128], expb[:, h * 128:(h + 1) * 128],
                kkn[:, (h // 2) * 128:(h // 2) * 128 + 128], op=ALU.mult),
                deps=[m_e1, m_kkn, war_TA] if h == 0 else [])
        m_TAb = p.mark("vector")
        p.op("tensor", lambda E, cc=cc: E.matmul(
            psB[:, :], onesf[:, :], bf_f[0:1, cc * 512:(cc + 1) * 512],
            start=True, stop=True), deps=[war_psB], inc=1)
        m_b2 = p.mark("tensor")
        for h in range(4):
            p.op("vector", lambda E, h=h, bplcol=bplcol: E.tensor_scalar(
                dscr[:, h * 128:(h + 1) * 128], psB[:, h * 128:(h + 1) * 128],
                bplcol[:, h:h + 1], -1.0, op0=ALU.subtract, op1=ALU.mult),
                deps=[m_b2, m_e1] if h == 0 else [])
        p.op("vector", lambda E: E.tensor_tensor(out3(dscr), out3(dscr), bcast_mask(mLs),
                                                 op=ALU.subtract))
        m_d2 = p.mark("vector")
        war_psB = m_d2
        p.op("scalar", lambda E: E.activation(expb[:, :], dscr[:, :], AF.Exp),
             deps=[m_d2, m_TAb])
        m_e2 = p.mark("scalar")
        for h in range(4):
            p.op("vector", lambda E, h=h: E.tensor_tensor(
                TBp[0][:, h * 128:(h + 1) * 128], expb[:, h * 128:(h + 1) * 128],
                kkn[:, (h // 2) * 128:(h // 2) * 128 + 128], op=ALU.mult),
                deps=[m_e2, war_TB] if h == 0 else [])
        m_TA0 = p.mark("vector")
        p.op("tensor", lambda E, cc=cc: E.matmul(
            psB[:, :], onesf[:, :], bf_f[0:1, cc * 512:(cc + 1) * 512],
            start=True, stop=True), deps=[war_psB], inc=1)
        m_b3 = p.mark("tensor")
        for h in range(4):
            p.op("vector", lambda E, h=h, bcol=bcol: E.tensor_scalar(
                dscr[:, h * 128:(h + 1) * 128], psB[:, h * 128:(h + 1) * 128],
                bcol[:, h:h + 1], None, op0=ALU.subtract),
                deps=[m_b3, m_e2] if h == 0 else [])
        p.op("vector", lambda E: E.tensor_tensor(out3(dscr), out3(dscr), bcast_mask(mTi),
                                                 op=ALU.subtract))
        m_d3 = p.mark("vector")
        war_psB = m_d3
        p.op("scalar", lambda E: E.activation(expb[:, :], dscr[:, :], AF.Exp),
             deps=[m_d3, m_TA0])
        m_e3 = p.mark("scalar")
        war_dscr = m_e3
        for h in range(4):
            p.op("vector", lambda E, h=h: E.tensor_tensor(
                Aqs[:, h * 128:(h + 1) * 128], expb[:, h * 128:(h + 1) * 128],
                qks[:, (h // 2) * 128:(h // 2) * 128 + 128], op=ALU.mult),
                deps=[m_e3, war_Aqs] if h == 0 else [])
        m_Aq = p.mark("vector")
        war_expb = m_Aq
        if L2_STAGE <= 2:
            continue

        p.op("vector", lambda E: E.tensor_tensor(
            out3(Ysb[:, :]), out3(TBp[0][:, :]), bcast_mask(ident), op=ALU.add),
            deps=[m_TA0, war_Y])
        m_y = p.mark("vector")
        if L2_STAGE == 25:
            continue
        cur = 0
        for lvl in range(1, 1 + L2_INVL):
            nxt = 1 - cur
            dd = [m_TAb, m_TA0, war_psQa, war_psQb] if lvl == 1 else [m_cp]
            if lvl < L2_INVL and L2_STAGE not in (31, 33):
                for h in range(4):
                    sl = slice(h * 128, (h + 1) * 128)
                    p.op("tensor", lambda E, sl=sl, cur=cur: E.matmul(
                        psQ1[:, sl], TAp[cur][:, sl], TBp[cur][:, sl],
                        start=True, stop=True), deps=dd, inc=0)
                    dd = []
            for h in range(4):
                sl = slice(h * 128, (h + 1) * 128)
                p.op("tensor", lambda E, sl=sl, cur=cur: E.matmul(
                    psQ2[:, sl], TBp[cur][:, sl], TAp[cur][:, sl],
                    start=True, stop=True), deps=dd, inc=(1 if h == 3 else 0))
                dd = []
            m_sq = p.mark("tensor")
            for h in range(4):
                sl = slice(h * 128, (h + 1) * 128)
                p.op("tensor", lambda E, sl=sl, cur=cur: E.matmul(
                    psY[:, sl], TAp[cur][:, sl], Ysb[:, sl], start=True, stop=True),
                    deps=[m_y, war_psY] if h == 0 else [], inc=(1 if h == 3 else 0))
            m_ap = p.mark("tensor")
            if lvl < L2_INVL and L2_STAGE not in (31, 32):
                p.op("vector", lambda E, nxt=nxt: E.tensor_copy(
                    TBp[nxt][:, :], psQ1[:, :] if L2_STAGE != 33 else psQ2[:, :]),
                     deps=[m_sq])
            p.op("vector", lambda E, nxt=nxt: E.tensor_copy(TAp[nxt][:, :],
                                                            psQ2[:, :]),
                 deps=[m_sq])
            m_cp = p.mark("vector")
            war_psQa = m_cp
            war_psQb = m_cp
            p.op("vector", lambda E: E.tensor_tensor(Ysb[:, :], psY[:, :], Ysb[:, :],
                                                     op=ALU.add), deps=[m_ap])
            m_y = p.mark("vector")
            war_psY = m_y
            cur = nxt
        war_TA = m_y
        war_TB = m_y
        for h in range(4):
            p.op("tensor", lambda E, h=h: E.transpose(
                psT[:, h * 128:(h + 1) * 128], Ysb[:, h * 128:(h + 1) * 128],
                ident[:, :]), deps=[m_y, m_kv] if h == 0 else [],
                inc=(1 if h == 3 else 0))
        m_tt = p.mark("tensor")
        war_Y = m_tt
        p.op("scalar", lambda E: E.activation(Tt[:, :], psT[:, 0:512], AF.Copy),
             deps=[m_tt, war_Tt])
        m_Tt = p.mark("scalar")
        war_psT = m_Tt
        if L2_STAGE <= 3 or L2_STAGE in (31, 32, 33):
            continue

        p.op("tensor", lambda E, cc=cc: E.matmul(
            psQ1[:, :], onesf[:, :], ebf[0:1, cc * 512:(cc + 1) * 512],
            start=True, stop=True), deps=[war_psQa, ms], inc=1)
        m_eb = p.mark("tensor")
        for h in range(4):
            p.op("vector", lambda E, h=h, ksl=ksl: E.tensor_tensor(
                khat[:, h * 128:(h + 1) * 128], psQ1[:, h * 128:(h + 1) * 128],
                ksl(h // 2), op=ALU.mult),
                deps=[m_eb, war_khat] if h == 0 else [])
        for h in range(4):
            p.op("vector", lambda E, h=h, qsl=qsl: E.tensor_tensor(
                qhat[:, h * 128:(h + 1) * 128], psQ1[:, h * 128:(h + 1) * 128],
                qsl(h // 2), op=ALU.mult))
        m_kq2 = p.mark("vector")
        war_psQa = m_kq2

        for h in range(4):
            sl = slice(h * 128, (h + 1) * 128)
            p.op("tensor", lambda E, sl=sl: E.matmul(
                psQ2[:, sl], khat[:, sl], n_bf[:, sl],
                start=True, stop=True), deps=[m_kq2, war_n, war_psQb] if h == 0 else [],
                inc=(1 if h == 3 else 0))
        m_kn = p.mark("tensor")
        war_khat = m_kn
        p.op("vector", lambda E: E.tensor_tensor(tmpf[:, :], vtm, psQ2[:, :],
                                                 op=ALU.subtract), deps=[m_kn, m_kv])
        for h in range(4):
            p.op("vector", lambda E, h=h, betacol=betacol: E.tensor_scalar(
                x1s[:, h * 128:(h + 1) * 128], tmpf[:, h * 128:(h + 1) * 128],
                betacol[:, h:h + 1], None, op0=ALU.mult),
                deps=[war_x1] if h == 0 else [])
        m_x1 = p.mark("vector")
        war_psQb = m_x1

        for h in range(4):
            sl = slice(h * 128, (h + 1) * 128)
            p.op("tensor", lambda E, sl=sl: E.matmul(
                psY[:, sl], Tt[:, sl], x1s[:, sl], start=True, stop=True),
                deps=[m_x1, m_Tt, war_psY] if h == 0 else [], inc=(1 if h == 3 else 0))
        m_u = p.mark("tensor")
        war_x1 = m_u
        war_Tt = m_u
        p.op("scalar", lambda E: E.activation(usb[:, :], psY[:, 0:512], AF.Copy),
             deps=[m_u, war_usb])
        m_us = p.mark("scalar")
        war_psY = m_us
        if L2_STAGE <= 4:
            continue

        for h in range(4):
            sl = slice(h * 128, (h + 1) * 128)
            p.op("tensor", lambda E, sl=sl: E.matmul(
                psB[:, sl], qhat[:, sl], n_bf[:, sl], start=True, stop=False),
                deps=[m_kq2, m_Aq, war_psB, war_n] if h == 0 else [], inc=0)
            p.op("tensor", lambda E, sl=sl: E.matmul(
                psB[:, sl], Aqs[:, sl], usb[:, sl], start=False, stop=True),
                deps=[m_us] if h == 0 else [], inc=(1 if h == 3 else 0))
        m_out = p.mark("tensor")
        war_Aqs = m_out
        p.op("scalar", lambda E: E.activation(sqo[:, :], psB[:, :], AF.Square),
             deps=[m_out, war_sqo])
        m_sq2 = p.mark("scalar")
        p.op("vector", lambda E: E.reduce_sum(rms4[:, 0:4], out3(sqo), axis=AX.X),
             deps=[m_sq2])
        p.op("vector", lambda E: E.tensor_scalar(rms4[:, 4:8], rms4[:, 0:4],
                                                 1.0 / 128.0, None, op0=ALU.mult))
        m_r4 = p.mark("vector")
        p.op("scalar", lambda E: E.activation(rms4[:, 0:4], rms4[:, 4:8], AF.Ln,
                                              bias=1e-6), deps=[m_r4])
        p.op("scalar", lambda E: E.activation(rms4[:, 4:8], rms4[:, 0:4], AF.Exp,
                                              scale=-0.5))
        m_rst = p.mark("scalar")
        for h in range(4):
            p.op("vector", lambda E, h=h: E.tensor_scalar(
                corn[:, h * 128:(h + 1) * 128], psB[:, h * 128:(h + 1) * 128],
                rms4[:, 4 + h:5 + h], None, op0=ALU.mult),
                deps=[m_rst, war_corn] if h == 0 else [])
        m_corn = p.mark("vector")
        war_psB = m_corn
        war_sqo = m_corn
        for h in range(4):
            p.op("tensor", lambda E, h=h: E.transpose(
                psT[:, h * 128:(h + 1) * 128], corn[:, h * 128:(h + 1) * 128],
                ident[:, :]), deps=[m_corn, war_psT] if h == 0 else [],
                inc=(1 if h == 3 else 0))
        m_ct = p.mark("tensor")
        war_corn = m_ct
        if c == 0 or not L2_SKIPEV:
            p.op("vector", lambda E, c=c: E.tensor_copy(
                cno_sb[:, :].rearrange("p (h t) -> p h t", h=4)[:, :, c * 128:(c + 1) * 128],
                psT[:, 0:512].rearrange("p (h t) -> p h t", h=4)), deps=[m_ct])
        m_cno = p.mark("vector")
        war_psT = m_cno
        if L2_STAGE <= 5:
            continue

        for h in range(4):
            p.op("vector", lambda E, h=h, ebendc=ebendc: E.tensor_scalar(
                nsc[:, h * 128:(h + 1) * 128], n_sb[:, h * 128:(h + 1) * 128],
                ebendc[:, h:h + 1], None, op0=ALU.mult),
                deps=[mv, ms, war_nsc] if h == 0 else [])
        m_nsc = p.mark("vector")
        for h in range(4):
            p.op("vector", lambda E, h=h, ebmbc=ebmbc: E.tensor_scalar(
                kbar[:, h * 128:(h + 1) * 128],
                kvtm[:, (h // 2) * 128:(h // 2) * 128 + 128],
                ebmbc[:, h:h + 1], None, op0=ALU.mult),
                deps=[m_kv, war_kbar] if h == 0 else [])
        m_kb = p.mark("vector")
        for h in range(4):
            sl = slice(h * 128, (h + 1) * 128)
            p.op("tensor", lambda E, sl=sl: E.matmul(
                psQ2[:, sl], kbar[:, sl], usb[:, sl],
                start=True, stop=True), deps=[m_kb, m_us, war_psQb] if h == 0 else [],
                inc=(1 if h == 3 else 0))
        m_n2 = p.mark("tensor")
        war_nsc = m_n2
        war_kbar = m_n2
        war_usb = m_n2
        p.op("vector", lambda E: E.tensor_tensor(n_sb[:, :], psQ2[:, :],
                                                 nsc[:, :], op=ALU.add),
             deps=[m_n2, m_nsc])
        p.op("vector", lambda E: E.tensor_copy(n_bf[:, :], n_sb[:, :]))
        war_n = p.mark("vector")
        war_psQb = war_n

        if L2_BARRIER:
            bm = p.allmarks()
            for e in ENGINES:
                p.wait(e, bm)
        if cc == 3:
            buf_free[bsel] = [p.mark("tensor"), p.mark("vector")]
            if g + 2 < NG:
                stage1(g + 2)
            if g + 1 < NG:
                stage1b(g + 1, p.allmarks())

    m_fin = p.mark("vector")
    for h in range(4):
        p.dma("sync", cno_o[h * 128:(h + 1) * 128, :],
              cno_sb[:, h * L:(h + 1) * L], ch="cno", deps=[m_fin])
    p.wait("sync", p.allmarks())
    return p.build()




def build_L3():
    p = P("L3")
    IH = 40
    x_in = p.din("xt", [TPC, D], F32)
    xnf_i = p.din("xnf_i", [D, TPC], BF16)
    zsf_i = p.din("zsf_i", [4096, TPC], BF16)
    cnf_i = p.din("cnf_i", [4096, TPC], BF16)
    f0_i = p.din("f0_i", [D, TPC], F32)
    wout = p.din("wout", [4096, D], F32)
    gnw_i = p.din("gnw", [128, 1], F32)
    ln2c = p.din("ln2c", [128, KD], F32)
    wg5 = p.din("wg5", [D, 5120], F32)
    wu5 = p.din("wu5", [D, 5120], F32)
    wd5 = p.din("wd5", [5120, D], F32)
    wqT_i = p.din("wqT", [D, D], F32)
    wkT_i = p.din("wkT", [D, D], F32)
    wvT_i = p.din("wvT", [D, D], F32)
    identb_i = p.din("identb", [128, 128], BF16)
    identf_i = p.din("identf32", [128, 128], F32)
    onesb_i = p.din("onesb", [128, 1], BF16)
    onesf_i = p.din("onesf", [1, 128], F32)
    cbank_i = p.din("cbank", [128, 16], F32)

    x1_o = p.dout("x1_o", [TPC, D], F32)
    xn1_o = p.dout("xn1f", [D, TPC], BF16)
    q_o = p.dout("qf_o", [D, TPC], BF16)
    k_o = p.dout("kf_o", [D, TPC], BF16)
    v_o = p.dout("vf_o", [D, TPC], BF16)
    m5_d = p.dtmp("m5_d", [5120, TPC], BF16)
    x1a_d = p.dtmp("x1a_d", [D, TPC], F32)

    for v in L1_CONSTS:
        p.constf(v)

    ident = p.sb("ident_s", [128, 128], BF16)
    identf = p.sb("identf_s", [128, 128], F32)
    ones_b = p.sb("ones_b", [128, 1], BF16)
    ones_f1 = p.sb("ones_f1", [1, 128], F32)
    gnw = p.sb("gnw_s", [128, 1], F32)
    ln2 = p.sb("ln2_s", [128, KD], F32)

    p.dma("sync", ident[:, :], identb_i[:, :], ch="cst")
    p.dma("sync", identf[:, :], identf_i[:, :], ch="cst")
    p.dma("sync", ones_b[:, :], onesb_i[:, :], ch="cst")
    p.dma("sync", ones_f1[:, :], onesf_i[:, :], ch="cst")
    p.dma("sync", gnw[:, :], gnw_i[:, :], ch="cst")
    p.dma("sync", ln2[:, :], ln2c[:, :], ch="cst")
    p.dma("sync", p._cbank[:, :], cbank_i[:, :], ch="cst")
    m_cst = p.mark("cst")
    p.const_barrier = m_cst

    h2f = p.sb("h2f", [128, KD * TPC], BF16)
    sco_h = p.scope()
    hfm = p.sb("hfm_s", [128, KD * TPC], BF16, sco_h)
    sco1 = p.scope()
    xnf = p.sb("xnf_s", [128, KD * TPC], BF16, sco1)
    zsf = p.sb("zsf_s", [128, 32 * TPC], BF16, sco1)
    cn2 = [p.sb(f"cn2{i}", [128, TPC], BF16, sco1) for i in range(2)]
    wpan = [p.sb(f"wop{i}", [128, 32 * 256], BF16, sco1) for i in range(2)]
    x1b = [p.sb(f"x1b{i}", [128, TPC], F32, sco1) for i in range(2)]
    psA = p.ps("psA", [128, TPC], scope=sco1)
    psB = p.ps("psB", [128, TPC], scope=sco1)

    p.dma("sync", xnf[:, :].rearrange("p (kc t) -> p kc t", kc=KD),
          xnf_i[:, :].rearrange("(kc p) t -> p kc t", p=128), ch="xnf")
    m_xnf = p.mark("xnf")
    p.dma("sync", zsf[:, :].rearrange("p (kc t) -> p kc t", kc=32),
          zsf_i[:, :].rearrange("(kc p) t -> p kc t", p=128), ch="zsf")
    m_zsf = p.mark("zsf")
    m_gated = [None] * 32
    cn_free = [(p.mark("vector"))] * 2
    for kc in range(32):
        p.dma("sync", cn2[kc % 2][:, :], cnf_i[kc * 128:(kc + 1) * 128, :],
              ch=f"cn{kc % 2}", deps=[cn_free[kc % 2]])
        m_cn = p.mark(f"cn{kc % 2}")
        p.op("vector", lambda E, kc=kc: E.scalar_tensor_tensor(
            zsf[:, kc * TPC:(kc + 1) * TPC], cn2[kc % 2][:, :], gnw[:, 0:1],
            zsf[:, kc * TPC:(kc + 1) * TPC], op0=ALU.mult, op1=ALU.mult),
            deps=[m_cn, m_zsf, m_cst])
        m_gated[kc] = p.mark("vector")
        cn_free[kc % 2] = m_gated[kc]
    m_gate_all = p.mark("vector")

    wo_free = [[p.mark("tensor")]] * 2
    m_wo = {}

    def fetch_wout(og):
        p.dma("gpsimd", wpan[og % 2][:, :].rearrange("p (kc c) -> p kc c", kc=32),
              wout[:, og * 256:(og + 1) * 256].rearrange("(kc p) c -> p kc c", p=128),
              ch=f"wo{og % 2}", deps=wo_free[og % 2])
        m_wo[og] = p.mark(f"wo{og % 2}")

    fetch_wout(0)
    fetch_wout(1)
    ps_free = [p.mark("vector")] * 2
    x1b_free = [p.mark("sync")] * 2
    for og in range(8):
        for j in range(2):
            ob = og * 2 + j
            pm = psA if ob % 2 == 0 else psB
            for t2 in range(2):
                for kc in range(32):
                    dd = []
                    if kc == 0 and t2 == 0:
                        dd = [m_wo[og], m_gate_all, ps_free[ob % 2]]
                    p.op("tensor", lambda E, kc=kc, og=og, j=j, t2=t2, pm=pm: E.matmul(
                        pm[:, t2 * 512:(t2 + 1) * 512],
                        wpan[og % 2][:, kc * 256 + j * 128: kc * 256 + (j + 1) * 128],
                        zsf[:, kc * TPC + t2 * 512: kc * TPC + (t2 + 1) * 512],
                        start=(kc == 0), stop=(kc == 31)),
                        deps=dd, inc=(1 if kc == 31 else 0))
            m_mm = p.mark("tensor")
            p.op("vector", lambda E, ob=ob, pm=pm: E.tensor_tensor(
                hfm[:, ob * TPC:(ob + 1) * TPC], pm[:, :],
                xnf[:, ob * TPC:(ob + 1) * TPC], op=ALU.add),
                deps=[m_mm, m_xnf])
            p.op("vector", lambda E, ob=ob, pm=pm: E.tensor_tensor(
                x1b[ob % 2][:, :], pm[:, :], xnf[:, ob * TPC:(ob + 1) * TPC],
                op=ALU.add), deps=[x1b_free[ob % 2]])
            m_ev = p.mark("vector")
            ps_free[ob % 2] = m_ev
            p.dma("sync", x1a_d[ob * 128:(ob + 1) * 128, :], x1b[ob % 2][:, :],
                  ch=f"x1b{ob % 2}", deps=[m_ev])
            x1b_free[ob % 2] = p.mark(f"x1b{ob % 2}")
        wo_free[og % 2] = [p.mark("tensor")]
        if og + 2 < 8:
            fetch_wout(og + 2)
    m_hfm = p.mark("vector")
    sco1.close()

    barrier2 = p.allmarks()
    sco2 = p.scope()
    sqs = [p.sb(f"sqs{i}", [128, TPC], BF16, sco2) for i in range(2)]
    rowt = p.sb("rowt", [1, 3 * TPC], F32, sco2)
    ps_row = p.ps("psrow", [1, TPC], scope=sco2)
    ps_bc = p.ps("psbc", [128, TPC], scope=sco2)
    sq_free = [p.mark("tensor")] * 2
    for kc in range(KD):
        p.op("scalar", lambda E, kc=kc: E.activation(
            sqs[kc % 2][:, :], hfm[:, kc * TPC:(kc + 1) * TPC], AF.Square),
            deps=[m_hfm, sq_free[kc % 2]] + (barrier2 if kc == 0 else []))
        m_sq = p.mark("scalar")
        for half in range(2):
            p.op("tensor", lambda E, kc=kc, half=half: E.matmul(
                ps_row[:, half * 512:(half + 1) * 512], ones_b[:, :],
                sqs[kc % 2][:, half * 512:(half + 1) * 512],
                start=(kc == 0), stop=(kc == KD - 1)),
                deps=[m_sq] if half == 0 else [], inc=(1 if half == 1 else 0))
        sq_free[kc % 2] = p.mark("tensor")
    m_ssq = p.mark("tensor")
    p.op("vector", lambda E: E.tensor_scalar(rowt[:, 0:TPC], ps_row[:, :],
                                             1.0 / D, 1e-6, op0=ALU.mult, op1=ALU.add),
         deps=[m_ssq])
    m_v = p.mark("vector")
    p.op("scalar", lambda E: E.activation(rowt[:, TPC:2 * TPC], rowt[:, 0:TPC],
                                          AF.Ln, bias=1e-6), deps=[m_v])
    p.op("scalar", lambda E: E.activation(rowt[:, TPC:2 * TPC], rowt[:, TPC:2 * TPC],
                                          AF.Exp, scale=-0.5))
    m_s = p.mark("scalar")
    _newton_rsqrt(p, rowt[:, TPC:2 * TPC], rowt[:, 0:TPC], rowt[:, 2 * TPC:], [m_s])
    m_rst = p.mark("vector")
    for half in range(2):
        p.op("tensor", lambda E, half=half: E.matmul(
            ps_bc[:, half * 512:(half + 1) * 512], ones_f1[:, :],
            rowt[0:1, TPC + half * 512: TPC + (half + 1) * 512],
            start=True, stop=True), deps=[m_rst, m_cst] if half == 0 else [],
            inc=(1 if half == 1 else 0))
    m_bc = p.mark("tensor")
    for kc in range(KD):
        p.op("vector", lambda E, kc=kc: E.scalar_tensor_tensor(
            h2f[:, kc * TPC:(kc + 1) * TPC], hfm[:, kc * TPC:(kc + 1) * TPC],
            ln2[:, kc:kc + 1], ps_bc[:, :], op0=ALU.mult, op1=ALU.mult),
            deps=[m_bc] if kc == 0 else [])
    m_h2 = p.mark("vector")
    sco2.close()
    sco_h.close()


    sco3 = p.scope()
    wq2 = [p.sb(f"wi{i}", [128, KD * 512], BF16, sco3) for i in range(2)]
    gu = [p.sb(f"gu{i}", [128, TPC], BF16, sco3) for i in range(2)]
    m5b = [p.sb(f"m5b{i}", [128, TPC], BF16, sco3) for i in range(2)]
    psg = [p.ps(f"psg{i}", [128, 512], scope=sco3) for i in range(2)]
    m5b_free = [p.mark("sync")] * 2
    wi_free = [[p.mark("tensor")]] * 2
    gu_free = [p.mark("vector")] * 2
    psg_free = [p.mark("scalar")] * 2
    m_wi = {}

    def fetch_wi(ig, which):
        w = wg5 if which == 0 else wu5
        p.dma("gpsimd", wq2[which][:, :].rearrange("p (kc c) -> p kc c", kc=KD),
              w[:, ig * 512:(ig + 1) * 512].rearrange("(kc p) c -> p kc c", p=128),
              ch=f"wi{which}", deps=wi_free[which])
        m_wi[(ig, which)] = p.mark(f"wi{which}")

    wi_free = [[p.mark("tensor")] + p.allmarks()] * 2
    fetch_wi(0, 0)
    fetch_wi(0, 1)
    for ig in range(10):
        for j in range(4):
            ib = ig * 4 + j
            for which in range(2):
                pm = psg[which]
                for t2 in range(2):
                    for kc in range(KD):
                        dd = []
                        if kc == 0:
                            dd = [m_wi[(ig, which)], psg_free[which]]
                            if t2 == 0 and j == 0 and which == 0 and ig == 0:
                                dd.append(m_h2)
                        p.op("tensor", lambda E, kc=kc, j=j, t2=t2, pm=pm, which=which: E.matmul(
                            pm[:, 0:512],
                            wq2[which][:, kc * 512 + j * 128: kc * 512 + (j + 1) * 128],
                            h2f[:, kc * TPC + t2 * 512: kc * TPC + (t2 + 1) * 512],
                            start=(kc == 0), stop=(kc == KD - 1)),
                            deps=dd, inc=(1 if kc == KD - 1 else 0))
                    m_pm = p.mark("tensor")
                    if which == 0:
                        p.op("scalar", lambda E, t2=t2, pm=pm: E.activation(
                            gu[0][:, t2 * 512:(t2 + 1) * 512], pm[:, 0:512], AF.Silu),
                            deps=[m_pm] + ([gu_free[0]] if t2 == 0 else []))
                    else:
                        p.op("scalar", lambda E, t2=t2, pm=pm: E.activation(
                            gu[1][:, t2 * 512:(t2 + 1) * 512], pm[:, 0:512], AF.Copy),
                            deps=[m_pm] + ([gu_free[1]] if t2 == 0 else []))
                    psg_free[which] = p.mark("scalar")
            m_gu = p.mark("scalar")
            p.op("vector", lambda E, ib=ib: E.tensor_tensor(
                m5b[ib % 2][:, :], gu[0][:, :], gu[1][:, :],
                op=ALU.mult), deps=[m_gu, m5b_free[ib % 2]])
            gu_free[0] = gu_free[1] = p.mark("vector")
            p.dma("sync", m5_d[ib * 128:(ib + 1) * 128, :], m5b[ib % 2][:, :],
                  ch=f"m5{ib % 2}", deps=[p.mark("vector")])
            m5b_free[ib % 2] = p.mark(f"m5{ib % 2}")
        wi_free = [[p.mark("tensor")], [p.mark("tensor")]]
        if ig + 1 < 10:
            fetch_wi(ig + 1, 0)
            fetch_wi(ig + 1, 1)
    m_m5 = p.mark("vector")
    sco3.close()

    barrier4 = p.allmarks()
    for _e in ENGINES:
        p.wait(_e, barrier4)
    sco4 = p.scope()
    wdp = p.sb("wdp", [128, IH * 512], BF16, sco4)
    m5h = p.sb("m5h", [128, IH * 512], BF16, sco4)
    f0b = [p.sb(f"f0b{i}", [128, 512], F32, sco4) for i in range(2)]
    x1ab = [p.sb(f"x1ab{i}", [128, 512], F32, sco4) for i in range(2)]
    x1s = [p.sb(f"x1s{i}", [128, 512], F32, sco4) for i in range(2)]
    psd = [p.ps(f"psd{i}", [128, 512], scope=sco4) for i in range(2)]
    x1f_d = p.dtmp("x1f_d", [D, TPC], F32)
    wd_free = [p.mark("tensor")]
    m5h_free = [p.mark("tensor")]
    psd_free = [p.mark("vector")] * 2
    fb_free = [p.mark("vector")] * 2
    x1s_free = [p.mark("sync")] * 2
    for t2 in range(2):
        p.dma("sync", m5h[:, :].rearrange("p (kc c) -> p kc c", kc=IH),
              m5_d[:, t2 * 512:(t2 + 1) * 512].rearrange("(kc p) c -> p kc c", p=128),
              ch="m5h", deps=m5h_free + [p.mark("m50"), p.mark("m51")])
        m_m5h = p.mark("m5h")
        for og in range(4):
            p.dma("gpsimd", wdp[:, :].rearrange("p (kc c) -> p kc c", kc=IH),
                  wd5[:, og * 512:(og + 1) * 512].rearrange("(kc p) c -> p kc c", p=128),
                  ch="wd", deps=wd_free)
            m_wd = p.mark("wd")
            for j in range(4):
                ob = og * 4 + j
                pm = psd[j % 2]
                p.dma("sync", f0b[ob % 2][:, :],
                      f0_i[ob * 128:(ob + 1) * 128, t2 * 512:(t2 + 1) * 512],
                      ch=f"f0{ob % 2}", deps=[fb_free[ob % 2]])
                p.dma("sync", x1ab[ob % 2][:, :],
                      x1a_d[ob * 128:(ob + 1) * 128, t2 * 512:(t2 + 1) * 512],
                      ch=f"f0{ob % 2}")
                m_fb = p.mark(f"f0{ob % 2}")
                for kc in range(IH):
                    dd = [m_wd, m_m5h, psd_free[j % 2]] if kc == 0 else []
                    p.op("tensor", lambda E, kc=kc, j=j, pm=pm: E.matmul(
                        pm[:, 0:512],
                        wdp[:, kc * 512 + j * 128: kc * 512 + (j + 1) * 128],
                        m5h[:, kc * 512:(kc + 1) * 512],
                        start=(kc == 0), stop=(kc == IH - 1)),
                        deps=dd, inc=(1 if kc == IH - 1 else 0))
                m_dn = p.mark("tensor")
                p.op("vector", lambda E, ob=ob, pm=pm: E.tensor_tensor(
                    x1s[ob % 2][:, :], pm[:, 0:512], x1ab[ob % 2][:, :], op=ALU.add),
                    deps=[m_dn, m_fb, x1s_free[ob % 2]])
                p.op("vector", lambda E, ob=ob: E.tensor_tensor(
                    x1s[ob % 2][:, :], x1s[ob % 2][:, :], f0b[ob % 2][:, :],
                    op=ALU.add))
                m_x1 = p.mark("vector")
                psd_free[j % 2] = m_x1
                fb_free[ob % 2] = m_x1
                p.dma("sync", x1f_d[ob * 128:(ob + 1) * 128, t2 * 512:(t2 + 1) * 512],
                      x1s[ob % 2][:, :], ch=f"x1f{ob % 2}", deps=[m_x1])
                x1s_free[ob % 2] = p.mark(f"x1f{ob % 2}")
            wd_free = [p.mark("tensor")]
        m5h_free = [p.mark("tensor")]
    m_x1f = [p.mark("x1f0"), p.mark("x1f1")]
    sco4.close()

    barrier5 = p.allmarks()
    xn1f = p.sb("xn1f_s", [128, KD * TPC], BF16)
    sco5 = p.scope()
    x1r = [p.sb(f"x1r{i}", [128, KD * 128], F32, sco5) for i in range(2)]
    xb = [p.sb(f"xb{i}", [128, D], F32, sco5) for i in range(2)]
    x1t = [p.sb(f"x1t{i}", [128, D], F32, sco5) for i in range(2)]
    xn1t = p.sb("xn1t", [128, D], BF16, sco5)
    sq5 = p.sb("sq5", [128, D], BF16, sco5)
    st5 = p.sb("st5", [128, 8], F32, sco5)
    ps5 = p.ps("ps5", [128, D // 2], scope=sco5)
    ps5b = p.ps("ps5b", [128, TPC], BF16, scope=sco5)
    x1r_free = [p.mark("tensor")] * 2
    xb_free = [p.mark("vector")] * 2
    x1t_free = [p.mark("sync")] * 2
    m_xn1_done = [None] * 8
    for it in range(8):
        p.dma("sync", x1r[it % 2][:, :].rearrange("p (kc t) -> p kc t", kc=KD),
              x1f_d[:, it * 128:(it + 1) * 128].rearrange("(kc p) t -> p kc t", p=128),
              ch=f"x1r{it % 2}", deps=[x1r_free[it % 2]] + (barrier5 + m_x1f if it < 2 else []))
        m_rl = p.mark(f"x1r{it % 2}")
        p.dma("sync", xb[it % 2][:, :], x_in[it * 128:(it + 1) * 128, :],
              ch=f"xb{it % 2}", deps=[xb_free[it % 2]])
        m_xb = p.mark(f"xb{it % 2}")
        for half in range(2):
            for kc in range(8):
                kk = half * 8 + kc
                dd = []
                if half == 0 and kc == 0:
                    dd = [m_rl, x1r_free[it % 2]]
                if half == 1 and kc == 0:
                    dd = [p.mark("vector")]
                p.op("tensor", lambda E, kk=kk, it=it, half=half, kc=kc: E.transpose(
                    ps5[:, kc * 128:(kc + 1) * 128], x1r[it % 2][:, kk * 128:(kk + 1) * 128],
                    identf[:, :]),
                    deps=dd, inc=(1 if kc == 7 else 0))
            m_tr = p.mark("tensor")
            p.op("vector", lambda E, it=it, half=half: E.tensor_tensor(
                x1t[it % 2][:, half * 1024:(half + 1) * 1024], ps5[:, :],
                xb[it % 2][:, half * 1024:(half + 1) * 1024], op=ALU.add),
                deps=[m_tr, m_xb] + ([x1t_free[it % 2]] if half == 0 else []))
        m_x1t = p.mark("vector")
        x1r_free[it % 2] = p.mark("tensor")
        p.dma("sync", x1_o[it * 128:(it + 1) * 128, :], x1t[it % 2][:, :],
              ch=f"x1o{it % 2}", deps=[m_x1t])
        x1t_free[it % 2] = p.mark(f"x1o{it % 2}")
        s = st5
        p.op("vector", lambda E, it=it, s=s: E.reduce_sum(
            s[:, 0:1], x1t[it % 2][:, :], axis=AX.X), deps=[m_x1t])
        p.op("scalar", lambda E, it=it: E.activation(sq5[:, :], x1t[it % 2][:, :],
                                                     AF.Square), deps=[m_x1t])
        m_sq = p.mark("scalar")
        p.op("vector", lambda E, s=s: E.reduce_sum(s[:, 1:2], sq5[:, :], axis=AX.X),
             deps=[m_sq])
        p.op("vector", lambda E, s=s: E.tensor_scalar(s[:, 2:3], s[:, 0:1], 1.0 / D, None, op0=ALU.mult))
        p.op("vector", lambda E, s=s: E.tensor_scalar(s[:, 3:4], s[:, 1:2], 1.0 / D, None, op0=ALU.mult))
        p.op("vector", lambda E, s=s: E.tensor_tensor(s[:, 4:5], s[:, 2:3], s[:, 2:3], op=ALU.mult))
        p.op("vector", lambda E, s=s: E.tensor_tensor(s[:, 5:6], s[:, 3:4], s[:, 4:5], op=ALU.subtract))
        p.op("vector", lambda E, s=s: E.tensor_scalar(s[:, 4:5], s[:, 5:6], 1e-5, None, op0=ALU.add))
        m_v5 = p.mark("vector")
        p.op("scalar", lambda E, s=s: E.activation(s[:, 6:7], s[:, 5:6], AF.Ln, bias=1e-5),
             deps=[m_v5])
        p.op("scalar", lambda E, s=s: E.activation(s[:, 7:8], s[:, 6:7], AF.Exp, scale=-0.5))
        m_r5 = p.mark("scalar")
        _newton_rsqrt(p, s[:, 7:8], s[:, 4:5], s[:, 6:7], [m_r5])
        m_r5 = p.mark("vector")
        p.op("vector", lambda E, it=it, s=s: E.tensor_scalar(
            xn1t[:, :], x1t[it % 2][:, :], s[:, 2:3], s[:, 7:8],
            op0=ALU.subtract, op1=ALU.mult), deps=[m_r5])
        m_xn1 = p.mark("vector")
        ev5 = p.mark("vector")
        for half in range(2):
            for kc8 in range(8):
                kc = half * 8 + kc8
                dd = [m_xn1] if (half == 0 and kc8 == 0) else []
                if kc8 == 0 and half == 1:
                    dd = [ev5]
                p.op("tensor", lambda E, kc=kc, kc8=kc8, it=it: E.transpose(
                    ps5b[:, kc8 * 128:(kc8 + 1) * 128],
                    xn1t[:, kc * 128:(kc + 1) * 128], ident[:, :]),
                    deps=dd, inc=(1 if kc8 == 7 else 0))
            m_t5 = p.mark("tensor")
            p.op("vector", lambda E, it=it, half=half: E.tensor_copy(
                xn1f[:, :].rearrange("p (kc t) -> p kc t", kc=KD)
                [:, half * 8:(half + 1) * 8, it * 128:(it + 1) * 128],
                ps5b[:, 0:TPC].rearrange("p (a t) -> p a t", a=8)), deps=[m_t5])
            ev5 = p.mark("vector")
        m_xn1_done[it] = p.mark("vector")
        xb_free[it % 2] = m_xn1_done[it]
    m_xn1f = p.mark("vector")
    for kc in range(KD):
        p.dma("sync", xn1_o[kc * 128:(kc + 1) * 128, :],
              xn1f[:, kc * TPC:(kc + 1) * TPC], ch="xn1o", deps=[m_xn1f])

    barrier6 = p.allmarks()
    for _e in ENGINES:
        p.wait(_e, barrier6)
    sco6 = p.scope()
    wq6 = [p.sb(f"wq6{i}", [128, KD * 512], BF16, sco6) for i in range(2)]
    qb = [p.sb(f"qb{i}", [128, TPC], BF16, sco6) for i in range(2)]
    ps6 = [p.ps(f"ps6{i}", [128, 512], scope=sco6) for i in range(2)]
    outs = [q_o, k_o, v_o]
    wsrcs = [wqT_i, wkT_i, wvT_i]
    wq6_free = [[p.mark("tensor")]] * 2
    qb_free = [p.mark("sync")] * 2
    ps6_free = [p.mark("vector")] * 2
    m_w6 = {}

    def fetch_w6(idx):
        which, ng = idx // 4, idx % 4
        p.dma("gpsimd", wq6[idx % 2][:, :].rearrange("p (kc c) -> p kc c", kc=KD),
              wsrcs[which][:, ng * 512:(ng + 1) * 512].rearrange("(kc p) c -> p kc c", p=128),
              ch=f"w6{idx % 2}", deps=wq6_free[idx % 2] + (barrier6 if idx < 2 else []))
        m_w6[idx] = p.mark(f"w6{idx % 2}")

    fetch_w6(0)
    fetch_w6(1)
    for idx in range(12):
        which, ng = idx // 4, idx % 4
        for j in range(4):
            nb = ng * 4 + j
            pm = ps6[j % 2]
            for t2 in range(2):
                for kc in range(KD):
                    dd = [m_w6[idx], m_xn1f, ps6_free[j % 2]] if kc == 0 else []
                    p.op("tensor", lambda E, kc=kc, j=j, t2=t2, pm=pm, idx=idx: E.matmul(
                        pm[:, 0:512],
                        wq6[idx % 2][:, kc * 512 + j * 128: kc * 512 + (j + 1) * 128],
                        xn1f[:, kc * TPC + t2 * 512: kc * TPC + (t2 + 1) * 512],
                        start=(kc == 0), stop=(kc == KD - 1)),
                        deps=dd, inc=(1 if kc == KD - 1 else 0))
                m_pm = p.mark("tensor")
                p.op("vector", lambda E, t2=t2, pm=pm, j=j: E.tensor_copy(
                    qb[j % 2][:, t2 * 512:(t2 + 1) * 512], pm[:, 0:512]),
                    deps=[m_pm] + ([qb_free[j % 2]] if t2 == 0 else []))
                ps6_free[j % 2] = p.mark("vector")
            m_qb = p.mark("vector")
            p.dma("sync", outs[which][nb * 128:(nb + 1) * 128, :], qb[j % 2][:, :],
                  ch=f"qo{j % 2}", deps=[m_qb])
            qb_free[j % 2] = p.mark(f"qo{j % 2}")
        wq6_free[idx % 2] = [p.mark("tensor")]
        if idx + 2 < 12:
            fetch_w6(idx + 2)
    sco6.close()
    p.wait("sync", p.allmarks())
    return p.build()




def build_L4():
    p = P("L4")
    NT16 = 16
    qf = p.din("qfh", [256, L], BF16)
    kf = p.din("kfh", [256, L], BF16)
    vf = p.din("vfh", [256, L], BF16)
    identb_i = p.din("identb", [128, 128], BF16)
    onesb_i = p.din("onesb", [128, 1], BF16)
    onesf_i = p.din("onesf", [1, 128], F32)
    mdiag_i = p.din("mdiag", [128, 4 * 512], F32)
    cbank_i = p.din("cbank", [128, 16], F32)
    z_o = p.dout("z_fm", [256, L], BF16)

    for v in L1_CONSTS:
        p.constf(v)

    ident = p.sb("ident_s", [128, 128], BF16)
    ones_b = p.sb("ones_b", [128, 1], BF16)
    ones_f1 = p.sb("ones_f1", [1, 128], F32)
    mdiag = p.sb("mdiag_s", [128, 4 * 512], F32)
    kb_s = p.sb("kb_s", [128, L], BF16)
    qb_s = p.sb("qb_s", [128, L], BF16)
    vb_s = p.sb("vb_s", [128, L], BF16)
    vtm = p.sb("vtm", [128, L], BF16)
    zb_s = p.sb("zb_s", [128, L], BF16)
    ptile = [p.sb(f"pt{i}", [128, 512], BF16) for i in range(2)]
    dscr = p.sb("dscr4", [128, 512], F32)
    rowf = p.sb("rowf", [1, 1024], F32)
    rbc = p.sb("rbc", [128, 512], F32)
    psS = [p.ps(f"psS{i}", [128, 512]) for i in range(2)]
    psZ = p.ps("psZ", [128, 512])
    psR = p.ps("psR", [1, 512])
    psB2 = p.ps("psB2", [128, 512])
    psT4 = p.ps("psT4", [128, 512], BF16)

    p.dma("sync", ident[:, :], identb_i[:, :], ch="cst")
    p.dma("sync", ones_b[:, :], onesb_i[:, :], ch="cst")
    p.dma("sync", ones_f1[:, :], onesf_i[:, :], ch="cst")
    p.dma("sync", mdiag[:, :], mdiag_i[:, :], ch="cst")
    p.dma("sync", p._cbank[:, :], cbank_i[:, :], ch="cst")
    m_cst = p.mark("cst")
    p.const_barrier = m_cst

    zdma = [p.mark("z0"), p.mark("z1")]
    for hh in range(2):
        chq = f"qkv{hh}"
        p.dma("sync", kb_s[:, :], kf[hh * 128:(hh + 1) * 128, :], ch=chq,
              deps=([p.mark("tensor")] if hh else []))
        p.dma("sync", qb_s[:, :], qf[hh * 128:(hh + 1) * 128, :], ch=chq)
        p.dma("sync", vb_s[:, :], vf[hh * 128:(hh + 1) * 128, :], ch=chq)
        m_ld = p.mark(chq)
        vt_free = p.mark("vector")
        for grp in range(16):
            for i4 in range(4):
                kb = grp * 4 + i4
                p.op("tensor", lambda E, kb=kb, i4=i4: E.transpose(
                    psT4[:, i4 * 128:(i4 + 1) * 128],
                    vb_s[:, kb * 128:(kb + 1) * 128], ident[:, :]),
                    deps=([m_ld, m_cst, vt_free] if i4 == 0 else []),
                    inc=(1 if i4 == 3 else 0))
            m_t = p.mark("tensor")
            p.op("vector", lambda E, grp=grp: E.tensor_copy(
                vtm[:, grp * 512:(grp + 1) * 512], psT4[:, :]), deps=[m_t])
            vt_free = p.mark("vector")
        m_vtm = p.mark("vector")

        pt_free = [p.mark("tensor")] * 2
        dscr_free = p.mark("scalar")
        psS_free = [p.mark("scalar")] * 2
        psZR_free = p.mark("vector")
        for tq in range(NT16):
            nkb = 4 * tq + 4
            for kb in range(nkb):
                pm = psS[kb % 2]
                dd = [m_ld, psS_free[kb % 2]]
                if kb == 0:
                    dd.append(psZR_free)
                p.op("tensor", lambda E, kb=kb, tq=tq, pm=pm: E.matmul(
                    pm[:, :], kb_s[:, kb * 128:(kb + 1) * 128],
                    qb_s[:, tq * 512:(tq + 1) * 512], start=True, stop=True),
                    deps=dd, inc=1)
                m_s = p.mark("tensor")
                pt = ptile[kb % 2]
                if kb >= 4 * tq:
                    m = kb - 4 * tq
                    p.op("vector", lambda E, m=m, pm=pm: E.tensor_tensor(
                        dscr[:, :], pm[:, :], mdiag[:, m * 512:(m + 1) * 512],
                        op=ALU.add), deps=[m_s, dscr_free])
                    m_d = p.mark("vector")
                    p.op("scalar", lambda E, pt=pt: E.activation(
                        pt[:, :], dscr[:, :], AF.Exp, scale=0.08838834764831845),
                        deps=[m_d, pt_free[kb % 2]])
                else:
                    p.op("scalar", lambda E, pt=pt, pm=pm: E.activation(
                        pt[:, :], pm[:, :], AF.Exp, scale=0.08838834764831845),
                        deps=[m_s, pt_free[kb % 2]])
                m_p = p.mark("scalar")
                psS_free[kb % 2] = m_p
                dscr_free = m_p
                p.op("tensor", lambda E, kb=kb, pt=pt, nkb=nkb: E.matmul(
                    psZ[:, :], vtm[:, kb * 128:(kb + 1) * 128], pt[:, :],
                    start=(kb == 0), stop=(kb == nkb - 1)),
                    deps=[m_p, m_vtm] if kb == 0 else [m_p], inc=0)
                p.op("tensor", lambda E, kb=kb, pt=pt, nkb=nkb: E.matmul(
                    psR[:, :], ones_b[:, :], pt[:, :],
                    start=(kb == 0), stop=(kb == nkb - 1)), inc=1)
                pt_free[kb % 2] = p.mark("tensor")
            m_pv = p.mark("tensor")
            p.op("vector", lambda E: E.tensor_copy(rowf[0:1, 0:512], psR[:, :]),
                 deps=[m_pv])
            p.op("vector", lambda E: E.reciprocal(rowf[0:1, 512:1024], rowf[0:1, 0:512]))
            m_r = p.mark("vector")
            p.op("tensor", lambda E: E.matmul(psB2[:, :], ones_f1[:, :],
                                              rowf[0:1, 512:1024], start=True,
                                              stop=True), deps=[m_r], inc=1)
            m_bc = p.mark("tensor")
            p.op("vector", lambda E: E.tensor_copy(rbc[:, :], psB2[:, :]),
                 deps=[m_bc])
            m_rbc = p.mark("vector")
            p.op("vector", lambda E, tq=tq: E.tensor_tensor(
                zb_s[:, tq * 512:(tq + 1) * 512], psZ[:, :], rbc[:, :],
                op=ALU.mult),
                deps=[m_rbc] + ([zdma[0]] if (hh == 1 and tq == 0) else []))
            psZR_free = p.mark("vector")
        m_z = p.mark("vector")
        p.dma("sync", z_o[hh * 128:(hh + 1) * 128, :], zb_s[:, :], ch=f"z{hh}",
              deps=[m_z])
        zdma[hh] = p.mark(f"z{hh}")
    p.wait("sync", p.allmarks())
    return p.build()




def build_L5():
    p = P("L5")
    xn1f_i = p.din("xn1f_i", [D, TPC], BF16)
    zf_i = p.din("zf_i", [D, TPC], BF16)
    x1_i = p.din("x1_i", [TPC, D], F32)
    woT_d = p.din("woT", [D, D], F32)
    WT_d = p.din("WT", [D, D], F32)
    acat_d = p.din("acatT", [D, 256], F32)
    bcat_d = p.din("bcatT", [256, D], F32)
    biasr_d = p.din("biasr", [1, D], F32)
    rw1T_d = p.din("rw1T", [D, 64], F32)
    rw2T_d = p.din("rw2T", [64, 32], F32)
    rb1c_d = p.din("rb1c", [64, 1], F32)
    rb2c_d = p.din("rb2c", [32, 1], F32)
    gatesc_d = p.din("gatesc", [32, 1], F32)
    identb_i = p.din("identb", [128, 128], BF16)
    identf_i = p.din("identf32", [128, 128], F32)
    onesf_i = p.din("onesf", [1, 128], F32)
    cbank_i = p.din("cbank", [128, 16], F32)
    x2_o = p.dout("x2_o", [TPC, D], F32)
    for v in L1_CONSTS:
        p.constf(v)

    identf = p.sb("identf", [128, 128], F32)
    identsb = p.sb("identsb", [128, 128], BF16)
    onesf_s = p.sb("onesf_s", [1, 128], F32)
    xn1f = p.sb("xn1f_s", [128, KD * TPC], BF16)
    zfs = p.sb("zfs", [128, KD * TPC], BF16)
    acat = p.sb("acat_s", [128, KD * 256], BF16)
    bcat = p.sb("bcat_s", [128, 2 * D], BF16)
    biasr = p.sb("biasr_s", [1, D], F32)
    rw1s = p.sb("rw1s", [128, KD * 64], BF16)
    rw2s = p.sb("rw2s", [64, 32], BF16)
    rb1s = p.sb("rb1s", [64, 1], F32)
    rb2s = p.sb("rb2s", [32, 1], F32)
    gts = p.sb("gts", [32, 1], F32)
    h_fm = p.sb("h_fm", [64, TPC], BF16)
    lg_fm = p.sb("lg_fm", [32, TPC], F32)
    coefs = p.sb("coefs", [128, 8 * 16], F32)
    rt = p.sb("rt", [128, 5 * 16], F32)
    rcol = p.sb("rcol", [128, 8], F32)
    wxaT = p.sb("wxaT", [128, 8 * 256], BF16)
    wxa = p.sb("wxa", [128, 256], BF16)
    wob = [p.sb(f"wob{i}", [128, KD * 512], BF16) for i in range(2)]
    wwb = [p.sb(f"wwb{i}", [128, KD * 512], BF16) for i in range(2)]
    x1b = [p.sb(f"x1b{i}", [128, 8 * 512], F32) for i in range(2)]
    x2sb = [p.sb(f"x2sb{i}", [128, 512], F32) for i in range(2)]
    psM = [p.ps(f"psM{i}", [128, 512]) for i in range(2)]
    psR = p.ps("psR5", [64, 512])
    psL = p.ps("psL5", [32, 512])
    psT5 = p.ps("psT5", [128, 16])
    psxa = p.ps("psxa", [128, 256])
    psw = p.ps("psw", [128, 256], BF16)

    p.dma("sync", identf[:, :], identf_i[:, :], ch="cst")
    p.dma("sync", identsb[:, :], identb_i[:, :], ch="cst")
    p.dma("sync", onesf_s[:, :], onesf_i[:, :], ch="cst")
    p.dma("sync", p._cbank[:, :], cbank_i[:, :], ch="cst")
    p.dma("sync", biasr[0:1, :], biasr_d[0:1, :], ch="cst")
    p.dma("sync", rb1s[:, :], rb1c_d[:, :], ch="cst")
    p.dma("sync", rb2s[:, :], rb2c_d[:, :], ch="cst")
    p.dma("sync", gts[:, :], gatesc_d[:, :], ch="cst")
    m_cst = p.mark("cst")
    p.const_barrier = m_cst
    p.dma("sync", xn1f[:, :].rearrange("p (kc t) -> p kc t", kc=KD),
          xn1f_i[:, :].rearrange("(kc p) t -> p kc t", p=128), ch="act")
    p.dma("sync", zfs[:, :].rearrange("p (kc t) -> p kc t", kc=KD),
          zf_i[:, :].rearrange("(kc p) t -> p kc t", p=128), ch="act")
    m_act = p.mark("act")
    p.dma("gpsimd", acat[:, :].rearrange("p (kc c) -> p kc c", kc=KD),
          acat_d[:, :].rearrange("(kc p) c -> p kc c", p=128), ch="wsm")
    p.dma("gpsimd", bcat[:, :].rearrange("p (kc c) -> p kc c", kc=2),
          bcat_d[:, :].rearrange("(kc p) c -> p kc c", p=128), ch="wsm")
    p.dma("gpsimd", rw1s[:, :].rearrange("p (kc c) -> p kc c", kc=KD),
          rw1T_d[:, :].rearrange("(kc p) c -> p kc c", p=128), ch="wsm")
    p.dma("gpsimd", rw2s[:, :], rw2T_d[:, :], ch="wsm")
    m_wsm = p.mark("wsm")
    m_wgt = [None, None]
    m_x1 = [None, None]
    for ob in range(2):
        ch = f"wg{ob}"
        p.dma("gpsimd", wob[ob][:, :].rearrange("p (kc c) -> p kc c", kc=KD),
              woT_d[:, ob * 512:(ob + 1) * 512]
              .rearrange("(kc p) c -> p kc c", p=128), ch=ch)
        p.dma("gpsimd", wwb[ob][:, :].rearrange("p (kc c) -> p kc c", kc=KD),
              WT_d[:, ob * 512:(ob + 1) * 512]
              .rearrange("(kc p) c -> p kc c", p=128), ch=ch)
        m_wgt[ob] = p.mark(ch)
        p.dma("sync", x1b[ob][:, :].rearrange("p (tc c) -> p tc c", tc=8),
              x1_i[:, ob * 512:(ob + 1) * 512]
              .rearrange("(tc p) c -> p tc c", p=128), ch=f"x1{ob}")
        m_x1[ob] = p.mark(f"x1{ob}")

    for hf in range(2):
        for k in range(KD):
            p.op("tensor", lambda E, hf=hf, k=k: E.matmul(
                psR[:, :], rw1s[:, k * 64:(k + 1) * 64],
                xn1f[:, k * TPC + hf * 512: k * TPC + hf * 512 + 512],
                start=(k == 0), stop=(k == KD - 1)),
                deps=([m_act, m_wsm, m_cst] if (hf == 0 and k == 0)
                      else ([p.mark("scalar")] if k == 0 else [])),
                inc=(1 if k == KD - 1 else 0))
        m_mm = p.mark("tensor")
        p.op("scalar", lambda E, hf=hf: E.activation(
            h_fm[:, hf * 512:(hf + 1) * 512], psR[:, :], AF.Silu,
            bias=rb1s[:, 0:1]), deps=[m_mm])
    m_h = p.mark("scalar")
    for hf in range(2):
        p.op("tensor", lambda E, hf=hf: E.matmul(
            psL[:, :], rw2s[:, :], h_fm[:, hf * 512:(hf + 1) * 512],
            start=True, stop=True),
            deps=([m_h] if hf == 0 else [p.mark("vector")]), inc=1)
        m_mm = p.mark("tensor")
        p.op("vector", lambda E, hf=hf: E.tensor_scalar(
            lg_fm[:, hf * 512:(hf + 1) * 512], psL[:, :], rb2s[:, 0:1],
            gts[:, 0:1], op0=ALU.add, op1=ALU.add), deps=[m_mm])
    m_lg = p.mark("vector")
    for tq in range(8):
        p.op("tensor", lambda E, tq=tq: E.transpose(
            psT5[:, :], lg_fm[0:16, tq * 128:(tq + 1) * 128],
            identf[0:16, 0:16]),
            deps=[m_lg] if tq == 0 else [p.mark("vector")], inc=1)
        m_T = p.mark("tensor")
        lg = rt[:, 0:16]
        iseq = rt[:, 16:32]
        l2 = rt[:, 32:48]
        iseq2 = rt[:, 48:64]
        tmp = rt[:, 64:80]
        p.op("vector", lambda E, lg=lg: E.tensor_copy(lg, psT5[:, :]),
             deps=[m_T])
        p.op("vector", lambda E, lg=lg: E.reduce_max(rcol[:, 0:1], lg, axis=AX.X))
        p.op("vector", lambda E, lg=lg, iseq=iseq: E.tensor_scalar(
            iseq, lg, rcol[:, 0:1], None, op0=ALU.is_equal))
        p.op("vector", lambda E, iseq=iseq, tmp=tmp: E.tensor_scalar(
            tmp, iseq, -1e30, None, op0=ALU.mult))
        p.op("vector", lambda E, lg=lg, tmp=tmp, l2=l2: E.tensor_tensor(
            l2, lg, tmp, op=ALU.add))
        p.op("vector", lambda E, l2=l2: E.reduce_max(rcol[:, 1:2], l2, axis=AX.X))
        p.op("vector", lambda E, l2=l2, iseq2=iseq2: E.tensor_scalar(
            iseq2, l2, rcol[:, 1:2], None, op0=ALU.is_equal))
        p.op("vector", lambda E: E.tensor_tensor(
            rcol[:, 2:3], rcol[:, 1:2], rcol[:, 0:1], op=ALU.subtract))
        m_v = p.mark("vector")
        p.op("scalar", lambda E: E.activation(rcol[:, 3:4], rcol[:, 2:3], AF.Exp),
             deps=[m_v])
        m_e = p.mark("scalar")
        p.op("vector", lambda E: E.tensor_scalar(
            rcol[:, 4:5], rcol[:, 3:4], 1.0, None, op0=ALU.add), deps=[m_e])
        p.op("vector", lambda E: E.reciprocal(rcol[:, 5:6], rcol[:, 4:5]))
        p.op("vector", lambda E: E.tensor_tensor(
            rcol[:, 6:7], rcol[:, 3:4], rcol[:, 5:6], op=ALU.mult))
        p.op("vector", lambda E, iseq=iseq: E.tensor_scalar(
            iseq, iseq, rcol[:, 5:6], None, op0=ALU.mult))
        p.op("vector", lambda E, iseq2=iseq2, tmp=tmp: E.tensor_scalar(
            tmp, iseq2, rcol[:, 6:7], None, op0=ALU.mult))
        p.op("vector", lambda E, tq=tq, iseq=iseq, tmp=tmp: E.tensor_tensor(
            coefs[:, tq * 16:(tq + 1) * 16], iseq, tmp, op=ALU.add))
    m_coef = p.mark("vector")

    psxa_free = m_coef
    psw_free = m_coef
    for tq in range(8):
        for k in range(KD):
            p.op("tensor", lambda E, tq=tq, k=k: E.matmul(
                psxa[:, :], xn1f[:, k * TPC + tq * 128: k * TPC + tq * 128 + 128],
                acat[:, k * 256:(k + 1) * 256],
                start=(k == 0), stop=(k == KD - 1)),
                deps=([psxa_free] if k == 0 else []),
                inc=(1 if k == KD - 1 else 0))
        m_mm = p.mark("tensor")
        p.op("vector", lambda E, tq=tq: E.tensor_tensor(
            wxa[:, :].rearrange("p (r e) -> p r e", r=16),
            psxa[:, :].rearrange("p (r e) -> p r e", r=16),
            coefs[:, tq * 16:(tq + 1) * 16].unsqueeze(1)
            .broadcast_to([128, 16, 16]), op=ALU.mult), deps=[m_mm, m_coef])
        m_wxa = p.mark("vector")
        psxa_free = m_wxa
        for pp in range(2):
            p.op("tensor", lambda E, pp=pp: E.transpose(
                psw[:, pp * 128:(pp + 1) * 128],
                wxa[:, pp * 128:(pp + 1) * 128], identsb[:, :]),
                deps=[m_wxa, psw_free] if pp == 0 else [], inc=(pp == 1))
        m_tr = p.mark("tensor")
        p.op("vector", lambda E, tq=tq: E.tensor_copy(
            wxaT[:, tq * 256:(tq + 1) * 256], psw[:, :]), deps=[m_tr])
        psw_free = p.mark("vector")
    m_wxaT = p.mark("vector")

    psM_free = [m_wxaT, m_wxaT]
    out_free = [None, None]
    ev_done = {}
    for ob in range(4):
        b = ob % 2
        for tq in range(8):
            dd0 = [m_wgt[b], psM_free[tq % 2]]
            if ob == 0 and tq == 0:
                dd0 += [m_act, m_wxaT, m_wsm]
            for k in range(KD):
                p.op("tensor", lambda E, tq=tq, k=k, b=b: E.matmul(
                    psM[tq % 2][:, :],
                    zfs[:, k * TPC + tq * 128: k * TPC + tq * 128 + 128],
                    wob[b][:, k * 512:(k + 1) * 512],
                    start=(k == 0), stop=False),
                    deps=(dd0 if k == 0 else []), inc=0)
            for k in range(KD):
                p.op("tensor", lambda E, tq=tq, k=k, b=b: E.matmul(
                    psM[tq % 2][:, :],
                    xn1f[:, k * TPC + tq * 128: k * TPC + tq * 128 + 128],
                    wwb[b][:, k * 512:(k + 1) * 512],
                    start=False, stop=False), inc=0)
            p.op("tensor", lambda E, tq=tq, ob=ob: E.matmul(
                psM[tq % 2][:, :], onesf_s[0:1, 0:128],
                biasr[0:1, ob * 512:(ob + 1) * 512],
                start=False, stop=False), inc=0)
            for pp in range(2):
                p.op("tensor", lambda E, tq=tq, pp=pp, ob=ob: E.matmul(
                    psM[tq % 2][:, :],
                    wxaT[:, tq * 256 + pp * 128: tq * 256 + pp * 128 + 128],
                    bcat[:, pp * D + ob * 512: pp * D + ob * 512 + 512],
                    start=False, stop=(pp == 1)), inc=(1 if pp == 1 else 0))
            m_mm = p.mark("tensor")
            dd = [m_mm, m_x1[b]]
            if out_free[tq % 2] is not None:
                dd.append(out_free[tq % 2])
            p.op("vector", lambda E, tq=tq, b=b: E.tensor_tensor(
                x2sb[tq % 2][:, :], psM[tq % 2][:, :],
                x1b[b][:, tq * 512:(tq + 1) * 512], op=ALU.add), deps=dd)
            psM_free[tq % 2] = p.mark("vector")
            p.dma("sync", x2_o[tq * 128:(tq + 1) * 128, ob * 512:(ob + 1) * 512],
                  x2sb[tq % 2][:, :], ch=f"out{tq % 2}", deps=[psM_free[tq % 2]])
            out_free[tq % 2] = p.mark(f"out{tq % 2}")
        if ob + 2 < 4:
            m_done = p.mark("tensor")
            nb = (ob + 2) % 2
            ch = f"wg{nb}"
            p.dma("gpsimd", wob[nb][:, :].rearrange("p (kc c) -> p kc c", kc=KD),
                  woT_d[:, (ob + 2) * 512:(ob + 3) * 512]
                  .rearrange("(kc p) c -> p kc c", p=128), ch=ch,
                  deps=[m_done])
            p.dma("gpsimd", wwb[nb][:, :].rearrange("p (kc c) -> p kc c", kc=KD),
                  WT_d[:, (ob + 2) * 512:(ob + 3) * 512]
                  .rearrange("(kc p) c -> p kc c", p=128), ch=ch)
            m_wgt[nb] = p.mark(ch)
            p.dma("sync", x1b[nb][:, :].rearrange("p (tc c) -> p tc c", tc=8),
                  x1_i[:, (ob + 2) * 512:(ob + 3) * 512]
                  .rearrange("(tc p) c -> p tc c", p=128), ch=f"x1{nb}",
                  deps=[p.mark("vector")])
            m_x1[nb] = p.mark(f"x1{nb}")
    p.wait("sync", p.allmarks())
    return p.build()


def _prep_L1_shared(ins):
    Wq = np.asarray(ins["L0_attn_W_qkvz"]).reshape(D, 16, 768)
    wmz = np.concatenate([
        Wq[:, :, 0:128].reshape(D, -1), Wq[:, :, 128:256].reshape(D, -1),
        Wq[:, :, 256:512].reshape(D, -1), Wq[:, :, 512:768].reshape(D, -1)], axis=1)
    wmz = np.ascontiguousarray(wmz, np.float32)
    wb = np.asarray(ins["L0_attn_W_ba"]).reshape(D, 16, 4)
    wba = np.ascontiguousarray(np.concatenate(
        [wb[:, :, 0:2].reshape(D, -1), wb[:, :, 2:4].reshape(D, -1)], axis=1), np.float32)
    cwr = np.ascontiguousarray(
        np.asarray(ins["L0_attn_conv_w"]).reshape(64, 128, 4).transpose(1, 0, 2)
        .reshape(128, 256), np.float32)
    ln1c = np.ascontiguousarray(np.asarray(ins["L0_attn_ln1_w"]).reshape(KD, 128).T,
                                np.float32)
    alog = np.asarray(ins["L0_attn_A_log"]).reshape(1, 32).astype(np.float32)
    dtb = np.asarray(ins["L0_attn_dt_bias"]).reshape(1, 32).astype(np.float32)
    wgT = np.ascontiguousarray(np.asarray(ins["L0_ffn_wq_gate"]).T)
    wuT = np.ascontiguousarray(np.asarray(ins["L0_ffn_wq_up"]).T)
    wdT = np.ascontiguousarray(np.asarray(ins["L0_ffn_wq_down"]).T)
    sgc = np.ascontiguousarray(np.asarray(ins["L0_ffn_s_gate"]).reshape(44, 128).T,
                               np.float32)
    suc = np.ascontiguousarray(np.asarray(ins["L0_ffn_s_up"]).reshape(44, 128).T,
                               np.float32)
    sdc = np.ascontiguousarray(np.asarray(ins["L0_ffn_s_down"]).reshape(KD, 128).T,
                               np.float32)
    d = dict(wmz=wmz, wba=wba, cwr=cwr, ln1c=ln1c, alog=alog, dtb=dtb,
             wgT=wgT, wuT=wuT, wdT=wdT, sgc=sgc, suc=suc, sdc=sdc)
    d.update(_const_inputs(L1_CONSTS))
    return d


def _l2_const_inputs():
    r = np.arange(128)
    mTs = np.where(r[:, None] < r[None, :], 0.0, 1e30).astype(np.float32)
    mTi = np.where(r[:, None] <= r[None, :], 0.0, 1e30).astype(np.float32)
    mLs = np.where(r[None, :] < r[:, None], 0.0, 1e30).astype(np.float32)
    d = _const_inputs(L1_CONSTS)
    return {
        "identb": d["identb"], "identf32": d["identf32"], "cbank": d["cbank"],
        "onesf": d["onesf"], "onest": np.ones((128, 128), np.float32),
        "mTs": mTs, "mTi": mTi, "mLs": mLs,
    }


def _l2_maps(qkv_fm, gbl):
    consts = _l2_const_inputs()
    maps = []
    for c in range(NCORES):
        qf = np.ascontiguousarray(qkv_fm[256 * c:256 * (c + 1)])
        kf = np.ascontiguousarray(qkv_fm[2048 + 256 * c:2048 + 256 * (c + 1)])
        vf = np.ascontiguousarray(qkv_fm[4096 + 512 * c:4096 + 512 * (c + 1)])
        cols = []
        for blk in range(3):
            sl = gbl[:, 32 * blk + 4 * c: 32 * blk + 4 * c + 4]
            cols.append(sl.reshape(64, 128, 4))
        gbp = np.zeros((128, 64 * 12), np.float32)
        for grp in range(16):
            for bi in range(3):
                blkv = cols[bi][grp * 4:(grp + 1) * 4]
                gbp[:, grp * 48 + 16 * bi: grp * 48 + 16 * (bi + 1)] = \
                    blkv.transpose(1, 0, 2).reshape(128, 16)
        m = dict(consts)
        m.update(qf=qf, kf=kf, vf=vf, gbp=gbp)
        maps.append(m)
    return maps


def run_L2(qkv_fm, gbl):
    nc = build_L2()
    res = run_bass_kernel_spmd(nc, _l2_maps(qkv_fm, gbl), core_ids=list(range(NCORES)))
    return res.results


def _l1_maps(ins):
    shared = _prep_L1_shared(ins)
    x = np.asarray(ins["L0_attn_hidden_states"]).reshape(L, D).astype(np.float32)
    maps = []
    for c in range(NCORES):
        t0 = c * TPC
        x9 = np.zeros((TPC + 128, D), np.float32)
        if c > 0:
            x9[0:128] = x[t0 - 128:t0]
        x9[128:] = x[t0:t0 + TPC]
        m = dict(shared)
        m["x9"] = x9
        maps.append(m)
    return maps


def run_L1(ins):
    nc = build_L1()
    res = run_bass_kernel_spmd(nc, _l1_maps(ins), core_ids=list(range(NCORES)))
    return res.results


def _prep_L5_shared(ins):
    woT = np.ascontiguousarray(np.asarray(ins["L1_attn_Wo"], np.float32).T)
    WT = np.ascontiguousarray(np.asarray(ins["L1_ffn_W"], np.float32).T)
    A = np.asarray(ins["L1_ffn_A"], np.float32)
    Bm = np.asarray(ins["L1_ffn_B"], np.float32)
    acatT = np.ascontiguousarray(A.transpose(2, 1, 0).reshape(D, 256))
    bcatT = np.ascontiguousarray(Bm.transpose(2, 0, 1).reshape(256, D))
    biasr = np.ascontiguousarray(
        np.asarray(ins["L1_ffn_bias"], np.float32).reshape(1, D))
    rw1T = np.ascontiguousarray(np.asarray(ins["L1_ffn_rw1"], np.float32).T)
    rw2T = np.ascontiguousarray(np.asarray(ins["L1_ffn_rw2"], np.float32).T)
    rb1c = np.ascontiguousarray(
        np.asarray(ins["L1_ffn_rb1"], np.float32).reshape(64, 1))
    rb2c = np.ascontiguousarray(
        np.asarray(ins["L1_ffn_rb2"], np.float32).reshape(32, 1))
    gatesc = np.zeros((32, 1), np.float32)
    gatesc[0:16, 0] = np.asarray(ins["L1_ffn_gates"], np.float32)
    d = dict(woT=woT, WT=WT, acatT=acatT, bcatT=bcatT, biasr=biasr,
             rw1T=rw1T, rw2T=rw2T, rb1c=rb1c, rb2c=rb2c, gatesc=gatesc)
    c = _const_inputs(L1_CONSTS)
    d.update(identb=c["identb"], identf32=c["identf32"], onesf=c["onesf"],
             cbank=c["cbank"])
    return d


def _l5_maps(shared, xn1_fm, z_fm, x1_tm):
    maps = []
    for c in range(NCORES):
        t0 = c * TPC
        m = dict(shared)
        m.update(xn1f_i=np.ascontiguousarray(xn1_fm[:, t0:t0 + TPC]),
                 zf_i=np.ascontiguousarray(z_fm[:, t0:t0 + TPC]),
                 x1_i=np.ascontiguousarray(x1_tm[t0:t0 + TPC]))
        maps.append(m)
    return maps


def _l4_const_inputs():
    d = _const_inputs(L1_CONSTS)
    r128 = np.arange(128)
    r512 = np.arange(512)
    mdiag = np.zeros((128, 4 * 512), np.float32)
    for m in range(4):
        mdiag[:, m * 512:(m + 1) * 512] = np.where(
            (128 * m + r128)[:, None] <= r512[None, :], 0.0, -1e30)
    return {"identb": d["identb"], "onesb": d["onesb"], "onesf": d["onesf"],
            "cbank": d["cbank"], "mdiag": mdiag}


def _l4_maps(q_fm, k_fm, v_fm):
    consts = _l4_const_inputs()
    maps = []
    for c in range(NCORES):
        m = dict(consts)
        m.update(qfh=np.ascontiguousarray(q_fm[256 * c:256 * (c + 1)]),
                 kfh=np.ascontiguousarray(k_fm[256 * c:256 * (c + 1)]),
                 vfh=np.ascontiguousarray(v_fm[256 * c:256 * (c + 1)]))
        maps.append(m)
    return maps


def kernel(**inputs):
    ins = {k: np.asarray(v) for k, v in inputs.items()}
    consts = _const_inputs(L1_CONSTS)

    r1 = run_L1(ins)
    qkv_fm = np.concatenate([np.asarray(r["qkv_fm"]) for r in r1], axis=1)
    zs_fm = np.concatenate([np.asarray(r["zs_fm"]) for r in r1], axis=1)
    xn_fm = np.concatenate([np.asarray(r["xnf"]) for r in r1], axis=1)
    f0_fm = np.concatenate([np.asarray(r["f0f"]) for r in r1], axis=1)
    gbl = np.concatenate([np.asarray(r["gbl"]) for r in r1], axis=0)

    r2 = run_L2(qkv_fm, gbl)
    cn_fm = np.concatenate([np.asarray(r["cno"]) for r in r2], axis=0)

    x = np.asarray(ins["L0_attn_hidden_states"], np.float32).reshape(L, D)
    shared3 = dict(
        wout=np.ascontiguousarray(ins["L0_attn_W_out"], np.float32),
        gnw=np.ascontiguousarray(
            np.asarray(ins["L0_attn_gated_norm_w"]).reshape(128, 1), np.float32),
        ln2c=np.ascontiguousarray(
            np.asarray(ins["L0_attn_ln2_w"]).reshape(KD, 128).T, np.float32),
        wg5=np.ascontiguousarray(ins["L0_attn_W_gate"], np.float32),
        wu5=np.ascontiguousarray(ins["L0_attn_W_up"], np.float32),
        wd5=np.ascontiguousarray(ins["L0_attn_W_down"], np.float32),
        wqT=np.ascontiguousarray(np.asarray(ins["L1_attn_Wq"]).T, np.float32),
        wkT=np.ascontiguousarray(np.asarray(ins["L1_attn_Wk"]).T, np.float32),
        wvT=np.ascontiguousarray(np.asarray(ins["L1_attn_Wv"]).T, np.float32),
        identb=consts["identb"], identf32=consts["identf32"],
        onesb=consts["onesb"], onesf=consts["onesf"], cbank=consts["cbank"],
    )
    maps3 = []
    for c in range(NCORES):
        s = slice(c * TPC, (c + 1) * TPC)
        m = dict(shared3)
        m["xt"] = np.ascontiguousarray(x[s])
        m["xnf_i"] = np.ascontiguousarray(xn_fm[:, s])
        m["zsf_i"] = np.ascontiguousarray(zs_fm[:, s])
        m["cnf_i"] = np.ascontiguousarray(cn_fm[:, s])
        m["f0_i"] = np.ascontiguousarray(f0_fm[:, s])
        maps3.append(m)
    r3 = run_bass_kernel_spmd(build_L3(), maps3,
                              core_ids=list(range(NCORES))).results
    x1_tm = np.concatenate([np.asarray(r["x1_o"]) for r in r3], axis=0)
    xn1_fm = np.concatenate([np.asarray(r["xn1f"]) for r in r3], axis=1)
    q_fm = np.concatenate([np.asarray(r["qf_o"]) for r in r3], axis=1)
    k_fm = np.concatenate([np.asarray(r["kf_o"]) for r in r3], axis=1)
    v_fm = np.concatenate([np.asarray(r["vf_o"]) for r in r3], axis=1)

    r4 = run_bass_kernel_spmd(build_L4(), _l4_maps(q_fm, k_fm, v_fm),
                              core_ids=list(range(NCORES))).results
    z_fm = np.concatenate([np.asarray(r["z_fm"]) for r in r4], axis=0)

    maps5 = _l5_maps(_prep_L5_shared(ins), xn1_fm, z_fm, x1_tm)
    r5 = run_bass_kernel_spmd(build_L5(), maps5,
                              core_ids=list(range(NCORES))).results
    x2 = np.concatenate([np.asarray(r["x2_o"]) for r in r5], axis=0)
    return np.ascontiguousarray(x2.reshape(1, L, D), np.float32)


# version 02
# speedup vs baseline: 1.2023x; 1.2023x over previous
import numpy as np
import ml_dtypes
from contextlib import ExitStack

import concourse.bass as bass
import concourse.mybir as mybir
from concourse.bass_utils import run_bass_kernel_spmd

F32 = mybir.dt.float32
BF16 = mybir.dt.bfloat16
I32 = mybir.dt.int32
AF = mybir.ActivationFunctionType
ALU = mybir.AluOpType
AX = mybir.AxisListType
BF = ml_dtypes.bfloat16

NCORES = 8
L = 8192
D = 2048
TPC = L // NCORES
KD = D // 128

ENGINES = ("sync", "gpsimd", "vector", "scalar", "tensor")


class P:

    def __init__(self, name):
        self.nc = bass.Bass("TRN2", target_bir_lowering=False, debug=False)
        self.st = ExitStack()
        self.ops = {e: [] for e in ENGINES}
        self.cnt = {e: 0 for e in ENGINES}
        self.sem = {e: self.st.enter_context(self.nc.semaphore(f"sp_{e}"))
                    for e in ENGINES}
        self.name = name
        self.chs = set()
        self._cbank = self.st.enter_context(
            self.nc.sbuf_tensor("constbank", [128, 16], F32))
        self._cslots = {}
        self.const_barrier = None

    def constf(self, value):
        value = float(value)
        if value in self._cslots:
            return
        k = len(self._cslots)
        assert k < 16
        self._cslots[value] = k
        self.nc.const_aps.register(F32, value, self._cbank[:, k:k + 1])

    def cbank_host(self):
        a = np.zeros((128, 16), np.float32)
        for v, k in self._cslots.items():
            a[:, k] = v
        return a

    def din(self, name, shape, dt):
        return self.nc.dram_tensor(name, list(shape), dt, kind="ExternalInput").ap()

    def dout(self, name, shape, dt):
        return self.nc.dram_tensor(name, list(shape), dt, kind="ExternalOutput").ap()

    def dtmp(self, name, shape, dt):
        return self.nc.dram_tensor(name, list(shape), dt, kind="Internal").ap()

    def sb(self, name, shape, dt, scope=None):
        return (scope or self.st).enter_context(
            self.nc.sbuf_tensor(name, list(shape), dt))

    def ps(self, name, shape, dt=F32, scope=None):
        return (scope or self.st).enter_context(
            self.nc.psum_tensor(name, list(shape), dt))

    def scope(self):
        return self.st.enter_context(ExitStack())

    def _sem(self, name):
        if name not in self.sem:
            self.sem[name] = self.st.enter_context(self.nc.semaphore(f"ch_{name}"))
            self.cnt[name] = 0
            self.chs.add(name)
        return self.sem[name]

    def _emit(self, eng, k, deps, f, ksem=None):
        for d in deps:
            if d is not None:
                self._sem(d[0])
        deps2 = [(self.sem[d[0]], d[1]) for d in deps
                 if d is not None and d[1] > 0]
        if eng in ("vector", "scalar") and k > 0 and ksem is None \
                and self.cnt[eng] > 0:
            deps2.append((self.sem[eng], self.cnt[eng]))
        tgt = ksem if ksem is not None else eng
        self.cnt[tgt] = self.cnt.get(tgt, 0) + k
        semh = self.sem[tgt] if k else None

        def run(E, deps2=deps2, f=f, semh=semh, k=k):
            for h, v in deps2:
                E.wait_ge(h, v)
            ins = f(E)
            if semh is not None:
                ins.then_inc(semh, k)
        self.ops[eng].append(run)

    def mark(self, eng):
        return (eng, self.cnt.get(eng, 0))

    def allmarks(self):
        return [self.mark(e) for e in ENGINES] + [self.mark(c) for c in self.chs]

    def dma(self, eng, out, in_, ch, deps=()):
        self._sem(ch)
        self._emit(eng, 16, deps, lambda E: E.dma_start(out=out, in_=in_), ksem=ch)

    def op(self, eng, f, deps=(), inc=1):
        self._emit(eng, inc, deps, f)

    def wait(self, eng, deps):
        self._emit(eng, 0, deps, lambda E: None)

    def build(self):
        nc = self.nc
        blk = self.st.enter_context(nc.Block())
        cb = self.const_barrier
        for eng in ENGINES:
            ops = self.ops[eng]
            dec = getattr(blk, eng)

            def section(E, ops=ops, eng=eng, cb=cb):
                if cb is not None and eng != getattr(self, "cb_eng", "sync"):
                    E.wait_ge(self.sem[cb[0]], cb[1])
                for f in ops:
                    f(E)
            dec(section)
        self.st.close()
        return nc


NEWTON = True


def _newton_rsqrt(p, r_ap, v_ap, t_ap, deps):
    if not NEWTON:
        return
    p.op("vector", lambda E: E.tensor_tensor(t_ap, r_ap, r_ap, op=ALU.mult), deps=deps)
    p.op("vector", lambda E: E.tensor_tensor(t_ap, t_ap, v_ap, op=ALU.mult))
    p.op("vector", lambda E: E.tensor_scalar(t_ap, t_ap, -0.5, 1.5,
                                             op0=ALU.mult, op1=ALU.add))
    p.op("vector", lambda E: E.tensor_tensor(r_ap, r_ap, t_ap, op=ALU.mult))


def _mk_ident(p, tile):
    p.op("gpsimd", lambda E: E.memset(tile[:, :], 0.0))
    p.op("gpsimd", lambda E: E.affine_select(
        out=tile[:, :], in_=tile[:, :], compare_op=ALU.not_equal,
        fill=1.0, base=0, pattern=[[-1, 128]], channel_multiplier=1))


L1_CONSTS = (1e-5, 1e-6, 1.0)


def _const_inputs(consts):
    cb = np.zeros((128, 16), np.float32)
    for k, v in enumerate(consts):
        cb[:, k] = v
    colsel = np.zeros((128, 1024), np.float32)
    for c in range(32):
        colsel[:, c * 32 + c] = 1.0
    return {
        "identb": np.eye(128, dtype=BF),
        "identf32": np.eye(128, dtype=np.float32),
        "colsel_i": colsel.astype(BF),
        "onesb": np.ones((128, 1), BF),
        "onesf": np.ones((1, 128), np.float32),
        "cbank": cb,
    }


def build_L1():
    p = P("L1")
    TH = TPC + 128
    NT = TH // 128

    x_in = p.din("x9", [TH, D], F32)
    wmz = p.din("wmz", [D, 12288], F32)
    wba = p.din("wba", [D, 64], F32)
    cwr = p.din("cwr", [128, 256], F32)
    ln1c = p.din("ln1c", [128, KD], F32)
    alog = p.din("alog", [1, 32], F32)
    dtb = p.din("dtb", [1, 32], F32)
    wgT = p.din("wgT", [D, 5632], I32)
    wuT = p.din("wuT", [D, 5632], I32)
    wdT = p.din("wdT", [5632, D], I32)
    sgc = p.din("sgc", [128, 44], F32)
    suc = p.din("suc", [128, 44], F32)
    sdc = p.din("sdc", [128, KD], F32)
    identb_i = p.din("identb", [128, 128], BF16)
    identf_i = p.din("identf32", [128, 128], F32)
    colsel_i = p.din("colsel_i", [128, 1024], BF16)
    onesb_i = p.din("onesb", [128, 1], BF16)
    onesf_i = p.din("onesf", [1, 128], F32)
    cbank_i = p.din("cbank", [128, 16], F32)

    qkv_o = p.dout("qkv_fm", [8192, TPC], BF16)
    zs_o = p.dout("zs_fm", [4096, TPC], BF16)
    gbl_o = p.dout("gbl", [TPC, 96], F32)
    xnf_o = p.dout("xnf", [D, TPC], BF16)
    f0_o = p.dout("f0f", [D, TPC], F32)
    rows_d = p.dtmp("rows_d", [32, TPC], F32)

    ident = p.sb("ident", [128, 128], BF16)
    identf = p.sb("identf", [128, 128], F32)
    ones_b = p.sb("ones_b", [128, 1], BF16)
    ones_f1 = p.sb("ones_f1", [1, 128], F32)
    cw = p.sb("cw", [128, 256], F32)
    ln1 = p.sb("ln1", [128, KD], F32)
    small = p.sb("small", [1, 64], F32)
    sgs = p.sb("sgs", [128, 44], F32)
    sus = p.sb("sus", [128, 44], F32)
    sds = p.sb("sds", [128, KD], F32)
    colsel = p.sb("colsel", [128, 1024], BF16)

    for v in L1_CONSTS:
        p.constf(v)
    p.dma("sync", ident[:, :], identb_i[:, :], ch="cst")
    p.dma("sync", identf[:, :], identf_i[:, :], ch="cst")
    p.dma("sync", colsel[:, :], colsel_i[:, :], ch="cst")
    p.dma("sync", ones_b[:, :], onesb_i[:, :], ch="cst")
    p.dma("sync", ones_f1[:, :], onesf_i[:, :], ch="cst")
    p.dma("sync", p._cbank[:, :], cbank_i[:, :], ch="cst")

    p.dma("sync", cw[:, :], cwr[:, :], ch="cst")
    p.dma("sync", ln1[:, :], ln1c[:, :], ch="cst")
    p.dma("sync", small[0:1, 0:32], alog[:, :], ch="cst")
    p.dma("sync", small[0:1, 32:64], dtb[:, :], ch="cst")
    p.dma("sync", sgs[:, :], sgc[:, :], ch="cst")
    p.dma("sync", sus[:, :], suc[:, :], ch="cst")
    p.dma("sync", sds[:, :], sdc[:, :], ch="cst")
    m_cst = p.mark("sync")

    sco_c = p.scope()
    hfm = p.sb("hfm", [128, KD * TH], BF16, sco_c)

    sco_b = p.scope()
    xb = [p.sb(f"xb{i}", [128, D], F32, sco_b) for i in range(3)]
    xn_t = [p.sb(f"xn{i}", [128, D], BF16, sco_b) for i in range(NT)]
    h_t = [p.sb(f"h{i}", [128, D], BF16, sco_b) for i in range(NT)]
    st8 = [p.sb(f"st{i}", [128, 8], F32, sco_b) for i in range(3)]
    sq_s = [p.sb(f"sq{i}", [128, D], BF16, sco_b) for i in range(2)]

    m_xn_done = [None] * NT
    m_h_done = [None] * NT
    sq_free = [p.mark("vector")] * 2
    xb_free = [p.mark("vector")] * 3
    st_free = [p.mark("scalar")] * 3
    for i in range(NT):
        b = xb[i % 3]
        p.dma("sync", b[:, :], x_in[i * 128:(i + 1) * 128, :], ch=f"xb{i % 3}",
              deps=[xb_free[i % 3]])
        m_x = p.mark(f"xb{i % 3}")
        s = st8[i % 3]
        p.op("vector", lambda E, b=b, s=s: E.reduce_sum(s[:, 0:1], b[:, :], axis=AX.X),
             deps=[m_x, st_free[i % 3]])
        sq = sq_s[i % 2]
        p.op("scalar", lambda E, b=b, sq=sq: E.activation(
            sq[:, :], b[:, :], AF.Square),
            deps=[m_x, sq_free[i % 2], st_free[i % 3]])
        m_sq = p.mark("scalar")
        p.op("vector", lambda E, s=s, sq=sq: E.reduce_sum(s[:, 1:2], sq[:, :], axis=AX.X),
             deps=[m_sq])
        p.op("vector", lambda E, s=s: E.tensor_scalar(s[:, 2:3], s[:, 0:1], 1.0 / D, None, op0=ALU.mult))
        p.op("vector", lambda E, s=s: E.tensor_scalar(s[:, 3:4], s[:, 1:2], 1.0 / D, None, op0=ALU.mult))
        p.op("vector", lambda E, s=s: E.tensor_tensor(s[:, 4:5], s[:, 2:3], s[:, 2:3], op=ALU.mult))
        p.op("vector", lambda E, s=s: E.tensor_tensor(s[:, 5:6], s[:, 3:4], s[:, 4:5], op=ALU.subtract))
        p.op("vector", lambda E, s=s: E.tensor_scalar(s[:, 4:5], s[:, 5:6], 1e-5, None,
                                                      op0=ALU.add))
        m_v = p.mark("vector")
        p.op("scalar", lambda E, s=s: E.activation(s[:, 6:7], s[:, 5:6], AF.Ln, bias=1e-5),
             deps=[m_v])
        p.op("scalar", lambda E, s=s: E.activation(s[:, 7:8], s[:, 6:7], AF.Exp, scale=-0.5))
        m_r = p.mark("scalar")
        _newton_rsqrt(p, s[:, 7:8], s[:, 4:5], s[:, 6:7], [m_r])
        m_r = p.mark("vector")
        xnt = xn_t[i]
        p.op("vector", lambda E, b=b, s=s, xnt=xnt: E.tensor_scalar(
            xnt[:, :], b[:, :], s[:, 2:3], s[:, 7:8], op0=ALU.subtract, op1=ALU.mult),
            deps=[m_r])
        m_xn = p.mark("vector")
        xb_free[i % 3] = m_xn
        m_xn_done[i] = m_xn
        p.op("scalar", lambda E, xnt=xnt, sq=sq: E.activation(
            sq[:, :], xnt[:, :], AF.Square), deps=[m_xn])
        m_sq2 = p.mark("scalar")
        p.op("vector", lambda E, s=s, sq=sq: E.reduce_sum(s[:, 1:2], sq[:, :], axis=AX.X),
             deps=[m_sq2])
        sq_free[i % 2] = p.mark("vector")
        p.op("vector", lambda E, s=s: E.tensor_scalar(s[:, 3:4], s[:, 1:2], 1.0 / D, None, op0=ALU.mult))
        p.op("vector", lambda E, s=s: E.tensor_scalar(s[:, 4:5], s[:, 3:4], 1e-6, None,
                                                      op0=ALU.add))
        m_v2 = p.mark("vector")
        p.op("scalar", lambda E, s=s: E.activation(s[:, 6:7], s[:, 3:4], AF.Ln, bias=1e-6),
             deps=[m_v2])
        p.op("scalar", lambda E, s=s: E.activation(s[:, 7:8], s[:, 6:7], AF.Exp, scale=-0.5))
        m_r2 = p.mark("scalar")
        _newton_rsqrt(p, s[:, 7:8], s[:, 4:5], s[:, 6:7], [m_r2])
        m_r2 = p.mark("vector")
        ht = h_t[i]
        p.op("vector", lambda E, xnt=xnt, s=s, ht=ht: E.tensor_scalar(
            ht[:, :], xnt[:, :], s[:, 7:8], None, op0=ALU.mult), deps=[m_r2])
        m_h_done[i] = p.mark("vector")
        st_free[i % 3] = p.mark("scalar")

    sco_tr = p.scope()
    xnf = p.sb("xnf_s", [128, KD * TPC], BF16, sco_tr)
    ps_tr = p.ps("pstr", [128, TH], BF16, sco_tr)

    prev_evac = p.mark("vector")
    for kc in range(KD):
        for it in range(NT):
            dd = [m_h_done[it]]
            if it == 0:
                dd += [m_cst, prev_evac]
            p.op("tensor", lambda E, kc=kc, it=it: E.transpose(
                ps_tr[:, it * 128:(it + 1) * 128],
                h_t[it][:, kc * 128:(kc + 1) * 128], ident[:, :]),
                deps=dd, inc=(1 if it == NT - 1 else 0))
        m_t = p.mark("tensor")
        p.op("vector", lambda E, kc=kc: E.tensor_scalar(
            hfm[:, kc * TH:(kc + 1) * TH], ps_tr[:, :], ln1[:, kc:kc + 1], None,
            op0=ALU.mult), deps=[m_t, m_cst])
        m_ev = p.mark("vector")
        for it in range(1, NT):
            dd = [m_xn_done[it]]
            if it == 1:
                dd.append(m_ev)
            p.op("tensor", lambda E, kc=kc, it=it: E.transpose(
                ps_tr[:, (it - 1) * 128:it * 128],
                xn_t[it][:, kc * 128:(kc + 1) * 128], ident[:, :]),
                deps=dd, inc=(1 if it == NT - 1 else 0))
        m_t2 = p.mark("tensor")
        p.op("vector", lambda E, kc=kc: E.tensor_copy(
            xnf[:, kc * TPC:(kc + 1) * TPC], ps_tr[:, 0:TPC]), deps=[m_t2])
        prev_evac = p.mark("vector")
        p.dma("sync", xnf_o[kc * 128:(kc + 1) * 128, :],
              xnf[:, kc * TPC:(kc + 1) * TPC], ch="xnfo", deps=[prev_evac])
    m_hfm = p.mark("vector")
    m_xnf_out = p.mark("xnfo")
    sco_tr.close()
    sco_b.close()

    sco_d = p.scope()
    wpan = [p.sb(f"wpan{i}", [128, 16 * 512], BF16, sco_d) for i in range(2)]
    mixed = p.sb("mixed", [128, TH], BF16, sco_d)
    conv2 = [p.sb(f"conv{i}", [128, TPC], BF16, sco_d) for i in range(2)]
    qkstore = p.sb("qkstore", [128, 32 * TPC], BF16, sco_d)
    sqb = [p.sb(f"sqb{i}", [128, TPC], BF16, sco_d) for i in range(2)]
    diag4 = [p.sb(f"diag4{i}", [128, 512], BF16, sco_d) for i in range(2)]
    zsb = [p.sb(f"zsb{i}", [128, TPC], BF16, sco_d) for i in range(2)]
    rows32 = p.sb("rows32", [32, TPC], F32, sco_d)
    ps_mm = [p.ps(f"psmm{i}", [128, 512], scope=sco_d) for i in range(2)]
    ps_cv = p.ps("pscv", [128, TPC], scope=sco_d)
    ps_rows = p.ps("psrows", [32, TPC], scope=sco_d)

    NCG = 24
    wpan_free = [p.mark("tensor")] * 2
    m_wpan = {}

    def _fetch_panel(cg):
        p.dma("gpsimd", wpan[cg % 2][:, :].rearrange("p (kc c) -> p kc c", kc=KD),
              wmz[:, cg * 512:(cg + 1) * 512].rearrange("(kc p) c -> p kc c", p=128),
              ch=f"wp{cg % 2}", deps=[wpan_free[cg % 2]])
        m_wpan[cg] = p.mark(f"wp{cg % 2}")

    _fetch_panel(0)
    _fetch_panel(1)

    ps_free = [p.mark("vector"), p.mark("vector")]
    conv_free = [p.mark("sync")] * 2
    cv_ps_free = p.mark("scalar")
    sqb_free = [p.mark("tensor")] * 2
    zsb_free = [p.mark("sync")] * 2
    diag_free = [p.mark("tensor")] * 2
    mixed_free = p.mark("tensor")
    nqk = 0
    for cg in range(NCG):
        is_mixed = cg < 16
        for j in range(4):
            cb = cg * 4 + j
            spans = [(0, 512), (512, 512), (1024, 128)] if is_mixed \
                else [(128, 512), (640, 512)]
            ev_marks = []
            for si, (t0, tsz) in enumerate(spans):
                pm = ps_mm[si % 2]
                for kc in range(KD):
                    dd = []
                    if kc == 0:
                        dd = [m_wpan[cg], m_hfm, ps_free[si % 2]]
                    p.op("tensor", lambda E, kc=kc, cg=cg, j=j, t0=t0, tsz=tsz, pm=pm: E.matmul(
                        pm[:, 0:tsz],
                        wpan[cg % 2][:, kc * 512 + j * 128: kc * 512 + (j + 1) * 128],
                        hfm[:, kc * TH + t0: kc * TH + t0 + tsz],
                        start=(kc == 0), stop=(kc == KD - 1)),
                        deps=dd, inc=(1 if kc == KD - 1 else 0))
                m_mm = p.mark("tensor")
                if is_mixed:
                    dd = [m_mm]
                    if si == 0:
                        dd.append(mixed_free)
                    p.op("vector", lambda E, t0=t0, tsz=tsz, pm=pm: E.tensor_copy(
                        mixed[:, t0:t0 + tsz], pm[:, 0:tsz]), deps=dd)
                    ps_free[si % 2] = p.mark("vector")
                else:
                    zt = zsb[cb % 2]
                    dd = [m_mm]
                    if si == 0:
                        dd.append(zsb_free[cb % 2])
                    p.op("scalar", lambda E, t0=t0, tsz=tsz, pm=pm, zt=zt: E.activation(
                        zt[:, t0 - 128:t0 - 128 + tsz], pm[:, 0:tsz], AF.Silu), deps=dd)
                    ps_free[si % 2] = p.mark("scalar")
            if not is_mixed:
                m_z = p.mark("scalar")
                zrow = cb - 64
                p.dma("sync", zs_o[zrow * 128:(zrow + 1) * 128, :], zsb[cb % 2][:, :],
                      ch=f"zs{cb % 2}", deps=[m_z])
                zsb_free[cb % 2] = p.mark(f"zs{cb % 2}")
                continue
            m_mixed = p.mark("vector")
            cwv = cw[:, cb * 4:(cb + 1) * 4].unsqueeze(2).broadcast_to([128, 4, 128])
            identv = identf[:, :].unsqueeze(1).broadcast_to([128, 4, 128])
            dg = diag4[cb % 2]
            p.op("vector", lambda E, dg=dg, identv=identv, cwv=cwv: E.tensor_tensor(
                dg[:, :].rearrange("p (a b) -> p a b", a=4), identv, cwv, op=ALU.mult),
                deps=[m_cst, diag_free[cb % 2]])
            m_dg = p.mark("vector")
            cvt = conv2[cb % 2]
            for t2 in range(2):
                for i4 in range(4):
                    dd = [m_dg, m_mixed, cv_ps_free] if (t2 == 0 and i4 == 0) else []
                    p.op("tensor", lambda E, t2=t2, i4=i4, dg=dg: E.matmul(
                        ps_cv[:, t2 * 512:(t2 + 1) * 512], dg[:, i4 * 128:(i4 + 1) * 128],
                        mixed[:, 125 + t2 * 512 + i4: 125 + t2 * 512 + i4 + 512],
                        start=(i4 == 0), stop=(i4 == 3)),
                        deps=dd, inc=(1 if (t2 == 1 and i4 == 3) else 0))
            m_cv = p.mark("tensor")
            mixed_free = m_cv
            diag_free[cb % 2] = m_cv
            p.op("scalar", lambda E, cvt=cvt: E.activation(cvt[:, :], ps_cv[:, :], AF.Silu),
                 deps=[m_cv, conv_free[cb % 2]])
            m_cs = p.mark("scalar")
            cv_ps_free = m_cs
            if cb >= 32:
                p.dma("sync", qkv_o[cb * 128:(cb + 1) * 128, :], cvt[:, :],
                      ch=f"cv{cb % 2}", deps=[m_cs])
                conv_free[cb % 2] = p.mark(f"cv{cb % 2}")
            else:
                p.op("vector", lambda E, cb=cb, cvt=cvt: E.tensor_copy(
                    qkstore[:, cb * TPC:(cb + 1) * TPC], cvt[:, :]), deps=[m_cs])
                conv_free[cb % 2] = p.mark("vector")
                sq2 = sqb[cb % 2]
                p.op("scalar", lambda E, cvt=cvt, sq2=sq2: E.activation(
                    sq2[:, :], cvt[:, :], AF.Square), deps=[sqb_free[cb % 2]])
                m_sq3 = p.mark("scalar")
                for half in range(2):
                    p.op("tensor", lambda E, cb=cb, half=half, sq2=sq2: E.matmul(
                        ps_rows[:, half * 512:(half + 1) * 512],
                        colsel[:, cb * 32:(cb + 1) * 32],
                        sq2[:, half * 512:(half + 1) * 512],
                        start=(cb == 0), stop=(cb == 31)),
                        deps=([m_sq3, m_cst] if half == 0 else []),
                        inc=(1 if half == 1 else 0))
                sqb_free[cb % 2] = p.mark("tensor")
                nqk += 1
        wpan_free[cg % 2] = p.mark("tensor")
        if cg + 2 < NCG:
            _fetch_panel(cg + 2)

    m_rowsmm = p.mark("tensor")
    rows32v = p.sb("rows32v", [32, TPC], F32, sco_d)
    rows32t = p.sb("rows32t", [32, TPC], F32, sco_d)
    p.op("vector", lambda E: E.tensor_scalar(rows32v[:, :], ps_rows[:, :], 1e-6, None,
                                             op0=ALU.add), deps=[m_rowsmm])
    p.op("vector", lambda E: E.tensor_copy(rows32[:, :], ps_rows[:, :]))
    m_rows = p.mark("vector")
    p.op("scalar", lambda E: E.activation(rows32[:, :], rows32[:, :], AF.Ln, bias=1e-6),
         deps=[m_rows])
    p.op("scalar", lambda E: E.activation(rows32[:, :], rows32[:, :], AF.Exp, scale=-0.5))
    m_rstd = p.mark("scalar")
    _newton_rsqrt(p, rows32[:, :], rows32v[:, :], rows32t[:, :], [m_rstd])
    m_rstd = p.mark("vector")
    p.op("vector", lambda E: E.tensor_scalar(rows32[0:16, :], rows32[0:16, :],
                                             float(128.0 ** -0.5), None, op0=ALU.mult),
         deps=[m_rstd])
    m_rstd2 = p.mark("vector")
    p.dma("sync", rows_d[:, :], rows32[:, :], ch="rw", deps=[m_rstd2])
    m_rows_out = p.mark("rw")

    sco_ap = p.scope()
    flatr = p.sb("flatr", [1, 2 * TPC], F32, sco_ap)
    ps_bc = p.ps("psbc", [128, TPC], scope=sco_ap)
    bc_free = p.mark("vector")
    fl_free = p.mark("tensor")
    for grp in range(16):
        p.dma("sync", flatr[:, :],
              rows_d[grp * 2:(grp + 1) * 2, :].rearrange("a b -> (a b)").unsqueeze(0),
              ch="fl", deps=[m_rows_out, fl_free])
        m_fl = p.mark("fl")
        for jj in range(2):
            cb = grp * 2 + jj
            for half in range(2):
                dd = [m_fl, bc_free] if half == 0 else []
                p.op("tensor", lambda E, cb=cb, jj=jj, half=half: E.matmul(
                    ps_bc[:, half * 512:(half + 1) * 512], ones_f1[:, :],
                    flatr[0:1, jj * TPC + half * 512: jj * TPC + (half + 1) * 512],
                    start=True, stop=True),
                    deps=dd, inc=(1 if half == 1 else 0))
            m_bc = p.mark("tensor")
            if jj == 1:
                fl_free = m_bc
            cvt2 = conv2[cb % 2]
            p.op("vector", lambda E, cb=cb, cvt2=cvt2: E.tensor_tensor(
                cvt2[:, :], ps_bc[:, :], qkstore[:, cb * TPC:(cb + 1) * TPC],
                op=ALU.mult), deps=[m_bc, conv_free[cb % 2]])
            m_q = p.mark("vector")
            bc_free = m_q
            p.dma("sync", qkv_o[cb * 128:(cb + 1) * 128, :], cvt2[:, :],
                  ch=f"cv{cb % 2}", deps=[m_q])
            conv_free[cb % 2] = p.mark(f"cv{cb % 2}")

    sco_ba = p.scope()
    wbas = p.sb("wbas", [128, 16 * 64], BF16, sco_ba)
    dtb_bc = p.sb("dtb_bc", [128, 32], F32, sco_ba)
    nea_bc = p.sb("nea_bc", [128, 32], F32, sco_ba)
    ba_s = p.sb("ba_s", [128, 64], F32, sco_ba)
    gbl_s = [p.sb(f"gbl{i}", [128, 96], F32, sco_ba) for i in range(2)]
    scr32 = p.sb("scr32", [128, 32], F32, sco_ba)
    scr32b = p.sb("scr32b", [128, 32], F32, sco_ba)

    p.dma("gpsimd", wbas[:, :].rearrange("p (kc c) -> p kc c", kc=KD),
          wba[:, :].rearrange("(kc p) c -> p kc c", p=128), ch="wba")
    m_wba = p.mark("wba")
    p.op("scalar", lambda E: E.activation(small[0:1, 0:32], small[0:1, 0:32], AF.Exp),
         deps=[m_cst])
    m_sc1 = p.mark("scalar")
    p.op("vector", lambda E: E.tensor_scalar(small[0:1, 0:32], small[0:1, 0:32], -1.0,
                                             None, op0=ALU.mult), deps=[m_sc1])
    m_sm = p.mark("vector")
    p.op("tensor", lambda E: E.matmul(ps_bc[:, 0:32], ones_f1[:, :], small[0:1, 0:32],
                                      start=True, stop=True),
         deps=[m_sm, bc_free])
    p.op("tensor", lambda E: E.matmul(ps_bc[:, 32:64], ones_f1[:, :], small[0:1, 32:64],
                                      start=True, stop=True))
    m_bc2 = p.mark("tensor")
    p.op("vector", lambda E: E.tensor_copy(nea_bc[:, :], ps_bc[:, 0:32]), deps=[m_bc2])
    p.op("vector", lambda E: E.tensor_copy(dtb_bc[:, :], ps_bc[:, 32:64]))
    m_bcs = p.mark("vector")

    m_sca_d = p.mark("scalar")
    gbl_free = [p.mark("sync")] * 2
    ba_ps_free = [m_sca_d, p.mark("vector")]
    ba_war = p.mark("scalar")
    for tb in range(8):
        pm = ps_mm[0]
        for kc in range(KD):
            dd = ([m_wba, m_hfm, m_bcs] + ba_ps_free) if kc == 0 else []
            p.op("tensor", lambda E, tb=tb, kc=kc, pm=pm: E.matmul(
                pm[:, 0:64],
                hfm[:, kc * TH + 128 + tb * 128: kc * TH + 256 + tb * 128],
                wbas[:, kc * 64:(kc + 1) * 64],
                start=(kc == 0), stop=(kc == KD - 1)),
                deps=dd, inc=(1 if kc == KD - 1 else 0))
        m_ba = p.mark("tensor")
        p.op("vector", lambda E, pm=pm: E.tensor_copy(ba_s[:, :], pm[:, 0:64]),
             deps=[m_ba, ba_war])
        m_bas = p.mark("vector")
        ba_ps_free = [m_bas]
        gb = gbl_s[tb % 2]
        p.op("vector", lambda E: E.tensor_tensor(scr32[:, :], ba_s[:, 32:64],
                                                 dtb_bc[:, :], op=ALU.add), deps=[m_bas])
        m1 = p.mark("vector")
        p.op("scalar", lambda E: E.activation(scr32b[:, :], scr32[:, :], AF.Exp,
                                              scale=-1.0), deps=[m1])
        p.op("scalar", lambda E: E.activation(scr32b[:, :], scr32b[:, :], AF.Ln, bias=1.0))
        m2 = p.mark("scalar")
        p.op("vector", lambda E: E.tensor_tensor(scr32[:, :], scr32[:, :], scr32b[:, :],
                                                 op=ALU.add), deps=[m2])
        p.op("vector", lambda E, gb=gb: E.tensor_tensor(
            gb[:, 0:32], scr32[:, :], nea_bc[:, :], op=ALU.mult),
            deps=[gbl_free[tb % 2]])
        m3 = p.mark("vector")
        p.op("scalar", lambda E: E.activation(scr32b[:, :], ba_s[:, 0:32], AF.Exp,
                                              scale=-1.0), deps=[m3])
        p.op("scalar", lambda E: E.activation(scr32b[:, :], scr32b[:, :], AF.Ln, bias=1.0))
        m4 = p.mark("scalar")
        p.op("vector", lambda E, gb=gb: E.tensor_scalar(
            gb[:, 64:96], scr32b[:, :], -1.0, None, op0=ALU.mult), deps=[m4])
        m5 = p.mark("vector")
        p.op("scalar", lambda E, gb=gb: E.activation(gb[:, 32:64], gb[:, 64:96], AF.Exp),
             deps=[m5])
        m6 = p.mark("scalar")
        ba_war = m6
        p.dma("sync", gbl_o[tb * 128:(tb + 1) * 128, :], gb[:, :],
              ch=f"gbl{tb % 2}", deps=[m6])
        gbl_free[tb % 2] = p.mark(f"gbl{tb % 2}")

    sco_ba.close()
    sco_ap.close()
    sco_d.close()
    sco_c.close()

    barrier = p.allmarks()
    sco_e = p.scope()
    msb = p.sb("msb", [128, 44 * TPC], BF16, sco_e)
    ps_g = [p.ps(f"psg{i}", [128, 512], scope=sco_e) for i in range(2)]
    sco_g = p.scope()
    xnf2 = p.sb("xnf2_s", [128, KD * TPC], BF16, sco_g)
    wq2 = [p.sb(f"wq2{i}", [128, 16 * 512], BF16, sco_g) for i in range(2)]
    gu = [p.sb(f"gu{i}", [128, TPC], BF16, sco_g) for i in range(2)]

    p.dma("sync", xnf2[:, :].rearrange("p (kc t) -> p kc t", kc=KD),
          xnf_o[:, :].rearrange("(kc p) t -> p kc t", p=128),
          ch="xnf2", deps=barrier + [m_xnf_out])
    m_xnf2 = p.mark("xnf2")

    wq2_free = [barrier[4], barrier[4]]
    gu_free = [p.mark("vector")] * 2
    psg_free = [p.mark("scalar")] * 2
    msb_marks = []
    for hg in range(11):
        bg, bu = wq2[0], wq2[1]
        p.dma("gpsimd", bg[:, :].rearrange("p (kc h) -> p kc h", kc=KD),
              wgT[:, hg * 512:(hg + 1) * 512].rearrange("(kc p) h -> p kc h", p=128),
              ch="wg", deps=[wq2_free[0]] + (barrier if hg == 0 else []))
        mg = p.mark("wg")
        p.dma("gpsimd", bu[:, :].rearrange("p (kc h) -> p kc h", kc=KD),
              wuT[:, hg * 512:(hg + 1) * 512].rearrange("(kc p) h -> p kc h", p=128),
              ch="wu", deps=[wq2_free[1]])
        mu = p.mark("wu")
        p.op("vector", lambda E, bg=bg: E.tensor_scalar(bg[:, :], bg[:, :], 128.0, None,
                                                        op0=ALU.subtract), deps=[mg])
        m_g128 = p.mark("vector")
        p.op("vector", lambda E, bu=bu: E.tensor_scalar(bu[:, :], bu[:, :], 128.0, None,
                                                        op0=ALU.subtract), deps=[mu])
        m_u128 = p.mark("vector")
        for j in range(4):
            hb = hg * 4 + j
            for which in range(2):
                wsrc = bg if which == 0 else bu
                mdep = m_g128 if which == 0 else m_u128
                pm = ps_g[which]
                gut = gu[which]
                for t2 in range(2):
                    for kc in range(KD):
                        dd = [mdep, m_xnf2, psg_free[which]] if kc == 0 else []
                        p.op("tensor", lambda E, kc=kc, j=j, t2=t2, wsrc=wsrc, pm=pm: E.matmul(
                            pm[:, 0:512],
                            wsrc[:, kc * 512 + j * 128: kc * 512 + (j + 1) * 128],
                            xnf2[:, kc * TPC + t2 * 512: kc * TPC + (t2 + 1) * 512],
                            start=(kc == 0), stop=(kc == KD - 1)),
                            deps=dd, inc=(1 if kc == KD - 1 else 0))
                    m_pm = p.mark("tensor")
                    if which == 0:
                        p.op("scalar", lambda E, hb=hb, t2=t2, pm=pm, gut=gut: E.activation(
                            gut[:, t2 * 512:(t2 + 1) * 512], pm[:, 0:512], AF.Silu,
                            scale=sgs[:, hb:hb + 1]),
                            deps=[m_pm] + ([gu_free[0]] if t2 == 0 else []))
                    else:
                        p.op("scalar", lambda E, hb=hb, t2=t2, pm=pm, gut=gut: E.activation(
                            gut[:, t2 * 512:(t2 + 1) * 512], pm[:, 0:512], AF.Copy,
                            scale=sus[:, hb:hb + 1]),
                            deps=[m_pm] + ([gu_free[1]] if t2 == 0 else []))
                    psg_free[which] = p.mark("scalar")
            m_gu = p.mark("scalar")
            p.op("vector", lambda E, hb=hb: E.tensor_tensor(
                msb[:, hb * TPC:(hb + 1) * TPC], gu[0][:, :], gu[1][:, :], op=ALU.mult),
                deps=[m_gu])
            gu_free[0] = gu_free[1] = p.mark("vector")
            msb_marks.append(p.mark("vector"))
        wq2_free = [p.mark("tensor"), p.mark("tensor")]
    m_msb = msb_marks[-1]
    sco_g.close()

    sco_dn = p.scope()
    wdp = p.sb("wdp", [128, 44 * 512], BF16, sco_dn)
    f0s = [p.sb(f"f0s{i}", [128, TPC], F32, sco_dn) for i in range(2)]
    f0_free = [p.mark("sync")] * 2
    wdp_free = p.mark("tensor")
    psd_free = [p.mark("vector")] * 2
    for og in range(4):
        p.dma("gpsimd", wdp[:, :].rearrange("p (kc o) -> p kc o", kc=44),
              wdT[:, og * 512:(og + 1) * 512].rearrange("(kc p) o -> p kc o", p=128),
              ch="wd", deps=[wdp_free] + (barrier if og == 0 else []))
        m_wd = p.mark("wd")
        p.op("vector", lambda E: E.tensor_scalar(wdp[:, :], wdp[:, :], 128.0, None,
                                                 op0=ALU.subtract), deps=[m_wd])
        m_wd2 = p.mark("vector")
        for j in range(4):
            ob = og * 4 + j
            pm = ps_g[j % 2]
            f0t = f0s[ob % 2]
            for t2 in range(2):
                for kc in range(44):
                    dd = [m_wd2, m_msb, psd_free[j % 2]] if kc == 0 else []
                    p.op("tensor", lambda E, kc=kc, j=j, t2=t2, pm=pm: E.matmul(
                        pm[:, 0:512],
                        wdp[:, kc * 512 + j * 128: kc * 512 + (j + 1) * 128],
                        msb[:, kc * TPC + t2 * 512: kc * TPC + (t2 + 1) * 512],
                        start=(kc == 0), stop=(kc == 43)),
                        deps=dd, inc=(1 if kc == 43 else 0))
                m_dn = p.mark("tensor")
                p.op("vector", lambda E, ob=ob, t2=t2, pm=pm, f0t=f0t: E.tensor_scalar(
                    f0t[:, t2 * 512:(t2 + 1) * 512], pm[:, 0:512], sds[:, ob:ob + 1],
                    None, op0=ALU.mult),
                    deps=[m_dn] + ([f0_free[ob % 2]] if t2 == 0 else []))
                psd_free[j % 2] = p.mark("vector")
            m_f0 = p.mark("vector")
            p.dma("sync", f0_o[ob * 128:(ob + 1) * 128, :], f0s[ob % 2][:, :],
                  ch=f"f0{ob % 2}", deps=[m_f0])
            f0_free[ob % 2] = p.mark(f"f0{ob % 2}")
        wdp_free = p.mark("tensor")

    sco_dn.close()
    sco_e.close()
    p.wait("sync", p.allmarks())
    return p.build()




L2_NCH = 64
L2_STAGE = 9
L2_INVL = 6
L2_BARRIER = False
L2_SKIPEV = False


def build_L2():
    p = P("L2")
    NCH = L2_NCH
    NG = max(1, NCH // 4)
    qf = p.din("qf", [256, L], BF16)
    kf = p.din("kf", [256, L], BF16)
    vf = p.din("vf", [512, L], BF16)
    gbp = p.din("gbp", [128, 64 * 12], F32)
    identb_i = p.din("identb", [128, 128], BF16)
    identf_i = p.din("identf32", [128, 128], F32)
    onesf_i = p.din("onesf", [1, 128], F32)
    onest_i = p.din("onest", [128, 128], F32)
    mTs_i = p.din("mTs", [128, 128], F32)
    mTi_i = p.din("mTi", [128, 128], F32)
    mLs_i = p.din("mLs", [128, 128], F32)
    e4n_i = p.din("e4n", [4, 512], F32)
    e4p_i = p.din("e4p", [4, 512], F32)
    onesfn_i = p.din("onesfn", [1, 128], F32)
    cbank_i = p.din("cbank", [128, 16], F32)
    cno_o = p.dout("cno", [512, L], BF16)
    rows_d = p.dtmp("rows_d2", [2, 16, 128], F32)

    for v in L1_CONSTS:
        p.constf(v)

    ident = p.sb("ident", [128, 128], BF16)
    identf = p.sb("identf", [128, 128], F32)
    onesf = p.sb("onesf_s", [1, 128], F32)
    onest = p.sb("onest_s", [128, 128], F32)
    mTs = p.sb("mTs_s", [128, 128], F32)
    mTi = p.sb("mTi_s", [128, 128], F32)
    mLs = p.sb("mLs_s", [128, 128], F32)
    gb = p.sb("gb_s", [128, 64 * 12], F32)
    n_sb = p.sb("n_sb", [128, 512], F32)
    n_bf = p.sb("n_bf", [128, 512], BF16)
    cno_sb = p.sb("cno_sb", [128, 4 * L], BF16)
    qf4 = [p.sb(f"qf4{i}", [128, 1024], BF16) for i in range(2)]
    kf4 = [p.sb(f"kf4{i}", [128, 1024], BF16) for i in range(2)]
    vf4 = [p.sb(f"vf4{i}", [128, 2048], BF16) for i in range(2)]
    rowb = p.sb("rowb", [48, 128], F32)
    lnbr = p.sb("lnbr", [16, 128], F32)
    brow = p.sb("brow", [16, 128], F32)
    bpl = p.sb("bpl", [16, 128], F32)
    bf_f = p.sb("bf_f", [1, 2048], F32)
    bplf = p.sb("bplf", [1, 2048], F32)
    bs4g = p.sb("bs4g", [4, 512], F32)
    bpl4g = p.sb("bpl4g", [4, 512], F32)
    e4n = p.sb("e4n_s", [4, 512], F32)
    e4p = p.sb("e4p_s", [4, 512], F32)
    onesfn = p.sb("onesfn_s", [1, 128], F32)
    ebf = p.sb("ebf", [1, 2048], F32)
    gcol = p.sb("gcol", [128, 64], F32)
    ebendb = p.sb("ebendb", [128, 16], F32)
    ebmb = p.sb("ebmb", [128, 16], F32)
    kkn = p.sb("kkn", [128, 256], BF16)
    qks = p.sb("qks", [128, 256], BF16)
    kvtm = p.sb("kvtm", [128, 768], BF16)
    dscr = p.sb("dscr", [128, 512], F32)
    expb = p.sb("expb", [128, 512], BF16)
    TAp = [p.sb(f"TA{i}", [128, 512], BF16) for i in range(2)]
    TBp = [p.sb(f"TB{i}", [128, 512], BF16) for i in range(2)]
    Aqs = p.sb("Aqs", [128, 512], BF16)
    Ysb = p.sb("Ysb", [128, 512], BF16)
    Tt = p.sb("Tt", [128, 512], BF16)
    khat = p.sb("khat", [128, 512], BF16)
    qhat = p.sb("qhat", [128, 512], BF16)
    tmpf = p.sb("tmpf", [128, 512], F32)
    x1s = p.sb("x1s", [128, 512], BF16)
    usb = p.sb("usb", [128, 512], BF16)
    sqo = p.sb("sqo", [128, 512], BF16)
    rms4 = p.sb("rms4", [128, 8], F32)
    kbar = p.sb("kbar", [128, 512], BF16)
    nsc = p.sb("nsc", [128, 512], F32)
    corn = p.sb("corn", [128, 512], BF16)
    psA = p.ps("psA", [128, 512])
    psT = p.ps("psT", [128, 768], BF16)
    psB = p.ps("psB", [128, 512])
    psQ1 = p.ps("psQ1", [128, 512])
    psQ2 = p.ps("psQ2", [128, 512])
    psY = p.ps("psY", [128, 512])
    psS = p.ps("psS", [128, 128])

    p.dma("sync", ident[:, :], identb_i[:, :], ch="cst")
    p.dma("sync", identf[:, :], identf_i[:, :], ch="cst")
    p.dma("sync", onesf[:, :], onesf_i[:, :], ch="cst")
    p.dma("sync", onest[:, :], onest_i[:, :], ch="cst")
    p.dma("sync", mTs[:, :], mTs_i[:, :], ch="cst")
    p.dma("sync", mTi[:, :], mTi_i[:, :], ch="cst")
    p.dma("sync", mLs[:, :], mLs_i[:, :], ch="cst")
    p.dma("sync", e4n[:, :], e4n_i[:, :], ch="cst")
    p.dma("sync", e4p[:, :], e4p_i[:, :], ch="cst")
    p.dma("sync", onesfn[:, :], onesfn_i[:, :], ch="cst")
    p.dma("sync", gb[:, :], gbp[:, :], ch="cst")
    p.dma("sync", p._cbank[:, :], cbank_i[:, :], ch="cst")
    m_cst = p.mark("cst")
    p.const_barrier = m_cst
    p.op("vector", lambda E: E.memset(n_sb[:, :], 0.0))
    p.op("vector", lambda E: E.memset(n_bf[:, :], 0.0))

    def rep4(ap):
        return ap.unsqueeze(2).broadcast_to([128, 4, 128])

    def out3(ap):
        return ap.rearrange("p (a b) -> p a b", a=4)

    def bcast_mask(m):
        return m[:, :].unsqueeze(1).broadcast_to([128, 4, 128])

    def v4(ap):
        return ap[:, :].rearrange("p (j r d) -> p j r d", j=2, r=2)

    def khrep(ap256):
        return ap256.rearrange("p (j d) -> p j d", j=2).unsqueeze(2) \
                    .broadcast_to([128, 2, 2, 128])

    def kchunk_view(buf, cc):
        return buf[:, :].rearrange("p (j q) -> p j q", j=2)[:, :, cc * 128:(cc + 1) * 128] \
                        .unsqueeze(2).broadcast_to([128, 2, 2, 128])

    grp_dma = {}
    buf_free = [[p.mark("tensor")], [p.mark("tensor")]]

    def stage1(g):
        bsel = g % 2
        dd = list(buf_free[bsel])
        for j in range(2):
            p.dma("sync", qf4[bsel][:, j * 512:(j + 1) * 512],
                  qf[j * 128:(j + 1) * 128, g * 512:(g + 1) * 512],
                  ch=f"grp{bsel}", deps=dd)
            p.dma("sync", kf4[bsel][:, j * 512:(j + 1) * 512],
                  kf[j * 128:(j + 1) * 128, g * 512:(g + 1) * 512],
                  ch=f"grp{bsel}", deps=dd)
        for h in range(4):
            p.dma("sync", vf4[bsel][:, h * 512:(h + 1) * 512],
                  vf[h * 128:(h + 1) * 128, g * 512:(g + 1) * 512],
                  ch=f"grp{bsel}", deps=dd)
        grp_dma[g] = p.mark(f"grp{bsel}")

    m_g1 = {}

    def stage1b(g, war):
        dd = [m_cst] + war
        p.op("tensor", lambda E, g=g: E.transpose(
            psS[0:48, 0:128], gb[:, g * 48:(g + 1) * 48], identf[:, :]), deps=dd)
        m_t = p.mark("tensor")
        p.op("vector", lambda E: E.tensor_copy(rowb[:, :], psS[0:48, 0:128]),
             deps=[m_t] + war)
        p.op("vector", lambda E: E.tensor_copy(lnbr[:, :], psS[32:48, 0:128]),
             deps=[m_t])
        p.op("vector", lambda E: E.tensor_tensor_scan(
            brow[:, :], onest[0:16, :], rowb[0:16, :], 0.0, op0=ALU.mult, op1=ALU.add))
        p.op("vector", lambda E: E.tensor_tensor(bpl[:, :], brow[:, :], lnbr[:, :],
                                                 op=ALU.add))
        m_rows = p.mark("vector")
        p.dma("sync", rows_d[0, :, :], brow[:, :], ch="rw", deps=[m_rows] + war)
        p.dma("sync", rows_d[1, :, :], bpl[:, :], ch="rw", deps=[m_rows])
        m_w = p.mark("rw")
        p.dma("sync", bf_f[:, :], rows_d[0:1, :, :].rearrange("a b c -> a (b c)"),
              ch="fl", deps=[m_w] + war)
        p.dma("sync", bplf[:, :], rows_d[1:2, :, :].rearrange("a b c -> a (b c)"),
              ch="fl", deps=[m_w])
        p.dma("sync", bs4g[:, :].rearrange("p (cc s) -> p cc s", cc=4),
              rows_d[0, :, :].rearrange("(cc h) s -> h cc s", h=4),
              ch="fl", deps=[m_w])
        p.dma("sync", bpl4g[:, :].rearrange("p (cc s) -> p cc s", cc=4),
              rows_d[1, :, :].rearrange("(cc h) s -> h cc s", h=4),
              ch="fl", deps=[m_w])
        m_fl = p.mark("fl")
        p.op("scalar", lambda E: E.activation(ebf[:, :], bf_f[:, :], AF.Exp),
             deps=[m_fl] + war)
        m_ebf = p.mark("scalar")
        p.op("tensor", lambda E: E.transpose(psS[:, 0:16], brow[:, :], identf[0:16, 0:16]),
             deps=[m_rows])
        m_t2 = p.mark("tensor")
        p.op("vector", lambda E: E.tensor_copy(gcol[:, 0:16], psS[:, 0:16]),
             deps=[m_t2])
        m_v1 = p.mark("vector")
        p.op("tensor", lambda E: E.matmul(psS[:, 32:48], onesf[:, :],
                                          bf_f[0:1, 127::128], start=True, stop=True),
             deps=[m_fl, m_v1])
        m_t4 = p.mark("tensor")
        p.op("vector", lambda E: E.tensor_copy(gcol[:, 32:48], psS[:, 32:48]),
             deps=[m_t4])
        m_v3 = p.mark("vector")
        p.op("scalar", lambda E: E.activation(ebendb[:, :], gcol[:, 32:48], AF.Exp),
             deps=[m_v3] + war)
        p.op("vector", lambda E: E.tensor_tensor(gcol[:, 48:64], gcol[:, 32:48],
                                                 gcol[:, 0:16], op=ALU.subtract))
        m_v4 = p.mark("vector")
        p.op("scalar", lambda E: E.activation(ebmb[:, :], gcol[:, 48:64], AF.Exp),
             deps=[m_v4])
        m_g1[g] = (p.mark("vector"), p.mark("scalar"), p.mark("tensor"))

    stage1(0)
    if NG > 1:
        stage1(1)
    stage1b(0, [])

    war_psA = p.mark("scalar")
    war_psT = p.mark("scalar")
    war_psB = p.mark("vector")
    war_psQa = p.mark("vector")
    war_psQb = p.mark("vector")
    war_psY = p.mark("scalar")
    war_kkn = p.mark("tensor")
    war_kvtm = p.mark("tensor")
    war_dscr = p.mark("scalar")
    war_expb = p.mark("vector")
    war_TA = p.mark("tensor")
    war_TB = p.mark("tensor")
    war_Aqs = p.mark("tensor")
    war_Y = p.mark("tensor")
    war_Tt = p.mark("tensor")
    war_khat = p.mark("tensor")
    war_usb = p.mark("tensor")
    war_sqo = p.mark("vector")
    war_corn = p.mark("tensor")
    war_n = p.mark("vector")
    war_x1 = p.mark("tensor")
    war_nsc = p.mark("tensor")
    war_kbar = p.mark("tensor")

    for c in range(NCH):
        g, cc = c // 4, c % 4
        bsel = g % 2
        mv, ms, mt = m_g1[g]
        mdma = grp_dma[g]

        def ksl(j, cc=cc, bsel=bsel):
            return kf4[bsel][:, j * 512 + cc * 128: j * 512 + cc * 128 + 128]

        def qsl(j, cc=cc, bsel=bsel):
            return qf4[bsel][:, j * 512 + cc * 128: j * 512 + cc * 128 + 128]

        def vsl(h, cc=cc, bsel=bsel):
            return vf4[bsel][:, h * 512 + cc * 128: h * 512 + cc * 128 + 128]

        bcol = gcol[:, cc * 4:cc * 4 + 4]
        bplcol = gcol[:, 16 + cc * 4: 16 + cc * 4 + 4]
        betacol = gb[:, g * 48 + 16 + cc * 4: g * 48 + 16 + cc * 4 + 4]
        ebendc = ebendb[:, cc * 4:cc * 4 + 4]
        ebmbc = ebmb[:, cc * 4:cc * 4 + 4]

        for j in range(2):
            p.op("tensor", lambda E, j=j, ksl=ksl: E.matmul(
                psA[:, j * 128:(j + 1) * 128], ksl(j), ksl(j), start=True, stop=True),
                deps=[mdma, war_psA] if j == 0 else [], inc=0)
            p.op("tensor", lambda E, j=j, ksl=ksl, qsl=qsl: E.matmul(
                psA[:, 256 + j * 128: 256 + (j + 1) * 128], ksl(j), qsl(j),
                start=True, stop=True), inc=(1 if j == 1 else 0))
        m_kq = p.mark("tensor")
        p.op("scalar", lambda E: E.activation(kkn[:, :], psA[:, 0:256], AF.Copy,
                                              scale=-1.0), deps=[m_kq, war_kkn])
        p.op("scalar", lambda E: E.activation(qks[:, :], psA[:, 256:512], AF.Copy))
        m_kkn = p.mark("scalar")
        war_psA = m_kkn
        war_kkn = None

        for j in range(2):
            p.op("tensor", lambda E, j=j, ksl=ksl: E.transpose(
                psT[:, j * 128:(j + 1) * 128], ksl(j), ident[:, :]),
                deps=[war_psT, mdma] if j == 0 else [], inc=0)
        for h in range(4):
            p.op("tensor", lambda E, h=h, vsl=vsl: E.transpose(
                psT[:, 256 + h * 128: 256 + (h + 1) * 128], vsl(h), ident[:, :]),
                inc=(1 if h == 3 else 0))
        m_tv = p.mark("tensor")
        p.op("scalar", lambda E: E.activation(kvtm[:, :], psT[:, :], AF.Copy),
             deps=[m_tv, war_kvtm])
        m_kv = p.mark("scalar")
        war_psT = m_kv
        war_kvtm = None
        ktm_view = khrep(kvtm[:, 0:256])
        vtm = kvtm[:, 256:768]
        if L2_STAGE <= 1:
            continue

        bs4c = bs4g[0:4, cc * 128:(cc + 1) * 128]
        bpl4c = bpl4g[0:4, cc * 128:(cc + 1) * 128]
        p.op("tensor", lambda E, cc=cc: E.matmul(
            psB[:, :], onesf[:, :], bplf[0:1, cc * 512:(cc + 1) * 512],
            start=True, stop=False), deps=[war_psB, mv, ms], inc=0)
        p.op("tensor", lambda E, bs4c=bs4c: E.matmul(
            psB[:, :], bs4c, e4n[:, :], start=False, stop=True), inc=1)
        m_b1 = p.mark("tensor")
        p.op("vector", lambda E: E.tensor_tensor(
            out3(dscr), out3(psB), bcast_mask(mTs), op=ALU.subtract),
            deps=[m_b1, war_dscr])
        m_d1 = p.mark("vector")
        war_psB = m_d1
        p.op("scalar", lambda E: E.activation(expb[:, :], dscr[:, :], AF.Exp),
             deps=[m_d1, war_expb])
        m_e1 = p.mark("scalar")
        p.op("vector", lambda E: E.tensor_tensor(
            v4(TAp[0]), v4(expb), khrep(kkn[:, :]), op=ALU.mult),
            deps=[m_e1, m_kkn, war_TA])
        m_TAb = p.mark("vector")
        p.op("tensor", lambda E, cc=cc: E.matmul(
            psB[:, :], onesfn[:, :], bf_f[0:1, cc * 512:(cc + 1) * 512],
            start=True, stop=False), deps=[war_psB], inc=0)
        p.op("tensor", lambda E, bpl4c=bpl4c: E.matmul(
            psB[:, :], bpl4c, e4p[:, :], start=False, stop=True), inc=1)
        m_b2 = p.mark("tensor")
        p.op("vector", lambda E: E.tensor_tensor(
            out3(dscr), out3(psB), bcast_mask(mLs), op=ALU.subtract),
            deps=[m_b2, m_e1])
        m_d2 = p.mark("vector")
        war_psB = m_d2
        p.op("scalar", lambda E: E.activation(expb[:, :], dscr[:, :], AF.Exp),
             deps=[m_d2, m_TAb])
        m_e2 = p.mark("scalar")
        p.op("vector", lambda E: E.tensor_tensor(
            v4(TBp[0]), v4(expb), khrep(kkn[:, :]), op=ALU.mult),
            deps=[m_e2, war_TB])
        m_TA0 = p.mark("vector")
        p.op("tensor", lambda E, cc=cc: E.matmul(
            psB[:, :], onesf[:, :], bf_f[0:1, cc * 512:(cc + 1) * 512],
            start=True, stop=False), deps=[war_psB], inc=0)
        p.op("tensor", lambda E, bs4c=bs4c: E.matmul(
            psB[:, :], bs4c, e4n[:, :], start=False, stop=True), inc=1)
        m_b3 = p.mark("tensor")
        p.op("vector", lambda E: E.tensor_tensor(
            out3(dscr), out3(psB), bcast_mask(mTi), op=ALU.subtract),
            deps=[m_b3, m_e2])
        m_d3 = p.mark("vector")
        war_psB = m_d3
        p.op("scalar", lambda E: E.activation(expb[:, :], dscr[:, :], AF.Exp),
             deps=[m_d3, m_TA0])
        m_e3 = p.mark("scalar")
        war_dscr = m_e3
        p.op("vector", lambda E: E.tensor_tensor(
            v4(Aqs), v4(expb), khrep(qks[:, :]), op=ALU.mult),
            deps=[m_e3, war_Aqs])
        m_Aq = p.mark("vector")
        war_expb = m_Aq
        if L2_STAGE <= 2:
            continue

        p.op("vector", lambda E: E.tensor_tensor(
            out3(Ysb[:, :]), out3(TBp[0][:, :]), bcast_mask(ident), op=ALU.add),
            deps=[m_TA0, war_Y])
        m_y = p.mark("vector")
        if L2_STAGE == 25:
            continue
        cur = 0
        for lvl in range(1, 1 + L2_INVL):
            nxt = 1 - cur
            dd = ([m_TAb, m_TA0, war_psQa, war_psQb] if lvl == 1
                  else [m_cp, m_cps])
            if lvl < L2_INVL and L2_STAGE not in (31, 33):
                for h in range(4):
                    sl = slice(h * 128, (h + 1) * 128)
                    p.op("tensor", lambda E, sl=sl, cur=cur: E.matmul(
                        psQ1[:, sl], TAp[cur][:, sl], TBp[cur][:, sl],
                        start=True, stop=True), deps=dd, inc=0)
                    dd = []
            for h in range(4):
                sl = slice(h * 128, (h + 1) * 128)
                p.op("tensor", lambda E, sl=sl, cur=cur: E.matmul(
                    psQ2[:, sl], TBp[cur][:, sl], TAp[cur][:, sl],
                    start=True, stop=True), deps=dd, inc=(1 if h == 3 else 0))
                dd = []
            m_sq = p.mark("tensor")
            for h in range(4):
                sl = slice(h * 128, (h + 1) * 128)
                p.op("tensor", lambda E, sl=sl, cur=cur: E.matmul(
                    psY[:, sl], TAp[cur][:, sl], Ysb[:, sl], start=True, stop=True),
                    deps=[m_y, war_psY] if h == 0 else [], inc=(1 if h == 3 else 0))
            m_ap = p.mark("tensor")
            if lvl < L2_INVL and L2_STAGE not in (31, 32):
                p.op("scalar", lambda E, nxt=nxt: E.activation(
                    TBp[nxt][:, :], psQ1[:, :] if L2_STAGE != 33 else psQ2[:, :],
                    AF.Copy), deps=[m_sq])
            m_cps = p.mark("scalar")
            p.op("vector", lambda E, nxt=nxt: E.tensor_copy(TAp[nxt][:, :],
                                                            psQ2[:, :]),
                 deps=[m_sq])
            m_cp = p.mark("vector")
            war_psQa = m_cps
            war_psQb = m_cp
            p.op("vector", lambda E: E.tensor_tensor(Ysb[:, :], psY[:, :], Ysb[:, :],
                                                     op=ALU.add), deps=[m_ap])
            m_y = p.mark("vector")
            war_psY = m_y
            cur = nxt
        war_TA = m_y
        war_TB = m_y
        for h in range(4):
            p.op("tensor", lambda E, h=h: E.transpose(
                psT[:, h * 128:(h + 1) * 128], Ysb[:, h * 128:(h + 1) * 128],
                ident[:, :]), deps=[m_y, m_kv] if h == 0 else [],
                inc=(1 if h == 3 else 0))
        m_tt = p.mark("tensor")
        war_Y = m_tt
        p.op("scalar", lambda E: E.activation(Tt[:, :], psT[:, 0:512], AF.Copy),
             deps=[m_tt, war_Tt])
        m_Tt = p.mark("scalar")
        war_psT = m_Tt
        if L2_STAGE <= 3 or L2_STAGE in (31, 32, 33):
            continue

        p.op("tensor", lambda E, cc=cc: E.matmul(
            psQ1[:, :], onesf[:, :], ebf[0:1, cc * 512:(cc + 1) * 512],
            start=True, stop=True), deps=[war_psQa, ms], inc=1)
        m_eb = p.mark("tensor")
        p.op("vector", lambda E, bsel=bsel, cc=cc: E.tensor_tensor(
            v4(khat), v4(psQ1), kchunk_view(kf4[bsel], cc), op=ALU.mult),
            deps=[m_eb, war_khat, mdma])
        p.op("vector", lambda E, bsel=bsel, cc=cc: E.tensor_tensor(
            v4(qhat), v4(psQ1), kchunk_view(qf4[bsel], cc), op=ALU.mult))
        m_kq2 = p.mark("vector")
        war_psQa = m_kq2

        for h in range(4):
            sl = slice(h * 128, (h + 1) * 128)
            p.op("tensor", lambda E, sl=sl: E.matmul(
                psQ2[:, sl], khat[:, sl], n_bf[:, sl],
                start=True, stop=True), deps=[m_kq2, war_n, war_psQb] if h == 0 else [],
                inc=(1 if h == 3 else 0))
        m_kn = p.mark("tensor")
        war_khat = m_kn
        p.op("vector", lambda E: E.tensor_tensor(tmpf[:, :], vtm, psQ2[:, :],
                                                 op=ALU.subtract), deps=[m_kn, m_kv])
        for h in range(4):
            p.op("vector", lambda E, h=h, betacol=betacol: E.tensor_scalar(
                x1s[:, h * 128:(h + 1) * 128], tmpf[:, h * 128:(h + 1) * 128],
                betacol[:, h:h + 1], None, op0=ALU.mult),
                deps=[war_x1] if h == 0 else [])
        m_x1 = p.mark("vector")
        war_psQb = m_x1

        for h in range(4):
            sl = slice(h * 128, (h + 1) * 128)
            p.op("tensor", lambda E, sl=sl: E.matmul(
                psY[:, sl], Tt[:, sl], x1s[:, sl], start=True, stop=True),
                deps=[m_x1, m_Tt, war_psY] if h == 0 else [], inc=(1 if h == 3 else 0))
        m_u = p.mark("tensor")
        war_x1 = m_u
        war_Tt = m_u
        p.op("scalar", lambda E: E.activation(usb[:, :], psY[:, 0:512], AF.Copy),
             deps=[m_u, war_usb])
        m_us = p.mark("scalar")
        war_psY = m_us
        if L2_STAGE <= 4:
            continue

        for h in range(4):
            sl = slice(h * 128, (h + 1) * 128)
            p.op("tensor", lambda E, sl=sl: E.matmul(
                psB[:, sl], qhat[:, sl], n_bf[:, sl], start=True, stop=False),
                deps=[m_kq2, m_Aq, war_psB, war_n] if h == 0 else [], inc=0)
            p.op("tensor", lambda E, sl=sl: E.matmul(
                psB[:, sl], Aqs[:, sl], usb[:, sl], start=False, stop=True),
                deps=[m_us] if h == 0 else [], inc=(1 if h == 3 else 0))
        m_out = p.mark("tensor")
        war_Aqs = m_out
        p.op("scalar", lambda E: E.activation(sqo[:, :], psB[:, :], AF.Square),
             deps=[m_out, war_sqo])
        m_sq2 = p.mark("scalar")
        p.op("vector", lambda E: E.reduce_sum(rms4[:, 0:4], out3(sqo), axis=AX.X),
             deps=[m_sq2])
        p.op("vector", lambda E: E.tensor_scalar(rms4[:, 4:8], rms4[:, 0:4],
                                                 1.0 / 128.0, None, op0=ALU.mult))
        m_r4 = p.mark("vector")
        p.op("scalar", lambda E: E.activation(rms4[:, 0:4], rms4[:, 4:8], AF.Ln,
                                              bias=1e-6), deps=[m_r4])
        p.op("scalar", lambda E: E.activation(rms4[:, 4:8], rms4[:, 0:4], AF.Exp,
                                              scale=-0.5))
        m_rst = p.mark("scalar")
        for h in range(4):
            p.op("vector", lambda E, h=h: E.tensor_scalar(
                corn[:, h * 128:(h + 1) * 128], psB[:, h * 128:(h + 1) * 128],
                rms4[:, 4 + h:5 + h], None, op0=ALU.mult),
                deps=[m_rst, war_corn] if h == 0 else [])
        m_corn = p.mark("vector")
        war_psB = m_corn
        war_sqo = m_corn
        for h in range(4):
            p.op("tensor", lambda E, h=h: E.transpose(
                psT[:, h * 128:(h + 1) * 128], corn[:, h * 128:(h + 1) * 128],
                ident[:, :]), deps=[m_corn, war_psT] if h == 0 else [],
                inc=(1 if h == 3 else 0))
        m_ct = p.mark("tensor")
        war_corn = m_ct
        if c == 0 or not L2_SKIPEV:
            p.op("vector", lambda E, c=c: E.tensor_copy(
                cno_sb[:, :].rearrange("p (h t) -> p h t", h=4)[:, :, c * 128:(c + 1) * 128],
                psT[:, 0:512].rearrange("p (h t) -> p h t", h=4)), deps=[m_ct])
        m_cno = p.mark("vector")
        war_psT = m_cno
        if L2_STAGE <= 5:
            continue

        for h in range(4):
            p.op("vector", lambda E, h=h, ebendc=ebendc: E.tensor_scalar(
                nsc[:, h * 128:(h + 1) * 128], n_sb[:, h * 128:(h + 1) * 128],
                ebendc[:, h:h + 1], None, op0=ALU.mult),
                deps=[mv, ms, war_nsc] if h == 0 else [])
        m_nsc = p.mark("vector")
        for h in range(4):
            p.op("vector", lambda E, h=h, ebmbc=ebmbc: E.tensor_scalar(
                kbar[:, h * 128:(h + 1) * 128],
                kvtm[:, (h // 2) * 128:(h // 2) * 128 + 128],
                ebmbc[:, h:h + 1], None, op0=ALU.mult),
                deps=[m_kv, war_kbar] if h == 0 else [])
        m_kb = p.mark("vector")
        for h in range(4):
            sl = slice(h * 128, (h + 1) * 128)
            p.op("tensor", lambda E, sl=sl: E.matmul(
                psQ2[:, sl], kbar[:, sl], usb[:, sl],
                start=True, stop=True), deps=[m_kb, m_us, war_psQb] if h == 0 else [],
                inc=(1 if h == 3 else 0))
        m_n2 = p.mark("tensor")
        war_nsc = m_n2
        war_kbar = m_n2
        war_usb = m_n2
        p.op("vector", lambda E: E.tensor_tensor(n_sb[:, :], psQ2[:, :],
                                                 nsc[:, :], op=ALU.add),
             deps=[m_n2, m_nsc])
        p.op("vector", lambda E: E.tensor_copy(n_bf[:, :], n_sb[:, :]))
        war_n = p.mark("vector")
        war_psQb = war_n

        if L2_BARRIER:
            bm = p.allmarks()
            for e in ENGINES:
                p.wait(e, bm)
        if cc == 3:
            buf_free[bsel] = [p.mark("tensor"), p.mark("vector")]
            if g + 2 < NG:
                stage1(g + 2)
            if g + 1 < NG:
                stage1b(g + 1, p.allmarks())

    m_fin = p.mark("vector")
    for h in range(4):
        p.dma("sync", cno_o[h * 128:(h + 1) * 128, :],
              cno_sb[:, h * L:(h + 1) * L], ch="cno", deps=[m_fin])
    p.wait("sync", p.allmarks())
    return p.build()




def build_L3():
    p = P("L3")
    IH = 40
    x_in = p.din("xt", [TPC, D], F32)
    xnf_i = p.din("xnf_i", [D, TPC], BF16)
    zsf_i = p.din("zsf_i", [4096, TPC], BF16)
    cnf_i = p.din("cnf_i", [4096, TPC], BF16)
    f0_i = p.din("f0_i", [D, TPC], F32)
    wout = p.din("wout", [4096, D], F32)
    gnw_i = p.din("gnw", [128, 1], F32)
    ln2c = p.din("ln2c", [128, KD], F32)
    wg5 = p.din("wg5", [D, 5120], F32)
    wu5 = p.din("wu5", [D, 5120], F32)
    wd5 = p.din("wd5", [5120, D], F32)
    wqT_i = p.din("wqT", [D, D], F32)
    wkT_i = p.din("wkT", [D, D], F32)
    wvT_i = p.din("wvT", [D, D], F32)
    identb_i = p.din("identb", [128, 128], BF16)
    identf_i = p.din("identf32", [128, 128], F32)
    onesb_i = p.din("onesb", [128, 1], BF16)
    onesf_i = p.din("onesf", [1, 128], F32)
    cbank_i = p.din("cbank", [128, 16], F32)

    x1_o = p.dout("x1_o", [TPC, D], F32)
    xn1_o = p.dout("xn1f", [D, TPC], BF16)
    q_o = p.dout("qf_o", [D, TPC], BF16)
    k_o = p.dout("kf_o", [D, TPC], BF16)
    v_o = p.dout("vf_o", [D, TPC], BF16)
    m5_d = p.dtmp("m5_d", [5120, TPC], BF16)
    x1a_d = p.dtmp("x1a_d", [D, TPC], F32)

    for v in L1_CONSTS:
        p.constf(v)

    ident = p.sb("ident_s", [128, 128], BF16)
    identf = p.sb("identf_s", [128, 128], F32)
    ones_b = p.sb("ones_b", [128, 1], BF16)
    ones_f1 = p.sb("ones_f1", [1, 128], F32)
    gnw = p.sb("gnw_s", [128, 1], F32)
    ln2 = p.sb("ln2_s", [128, KD], F32)

    p.dma("sync", ident[:, :], identb_i[:, :], ch="cst")
    p.dma("sync", identf[:, :], identf_i[:, :], ch="cst")
    p.dma("sync", ones_b[:, :], onesb_i[:, :], ch="cst")
    p.dma("sync", ones_f1[:, :], onesf_i[:, :], ch="cst")
    p.dma("sync", gnw[:, :], gnw_i[:, :], ch="cst")
    p.dma("sync", ln2[:, :], ln2c[:, :], ch="cst")
    p.dma("sync", p._cbank[:, :], cbank_i[:, :], ch="cst")
    m_cst = p.mark("cst")
    p.const_barrier = m_cst

    h2f = p.sb("h2f", [128, KD * TPC], BF16)
    sco_h = p.scope()
    hfm = p.sb("hfm_s", [128, KD * TPC], BF16, sco_h)
    sco1 = p.scope()
    xnf = p.sb("xnf_s", [128, KD * TPC], BF16, sco1)
    zsf = p.sb("zsf_s", [128, 32 * TPC], BF16, sco1)
    cn2 = [p.sb(f"cn2{i}", [128, TPC], BF16, sco1) for i in range(2)]
    wpan = [p.sb(f"wop{i}", [128, 32 * 256], BF16, sco1) for i in range(2)]
    x1b = [p.sb(f"x1b{i}", [128, TPC], F32, sco1) for i in range(2)]
    psA = p.ps("psA", [128, TPC], scope=sco1)
    psB = p.ps("psB", [128, TPC], scope=sco1)

    p.dma("sync", xnf[:, :].rearrange("p (kc t) -> p kc t", kc=KD),
          xnf_i[:, :].rearrange("(kc p) t -> p kc t", p=128), ch="xnf")
    m_xnf = p.mark("xnf")
    p.dma("sync", zsf[:, :].rearrange("p (kc t) -> p kc t", kc=32),
          zsf_i[:, :].rearrange("(kc p) t -> p kc t", p=128), ch="zsf")
    m_zsf = p.mark("zsf")
    m_gated = [None] * 32
    cn_free = [(p.mark("vector"))] * 2
    for kc in range(32):
        p.dma("sync", cn2[kc % 2][:, :], cnf_i[kc * 128:(kc + 1) * 128, :],
              ch=f"cn{kc % 2}", deps=[cn_free[kc % 2]])
        m_cn = p.mark(f"cn{kc % 2}")
        p.op("vector", lambda E, kc=kc: E.scalar_tensor_tensor(
            zsf[:, kc * TPC:(kc + 1) * TPC], cn2[kc % 2][:, :], gnw[:, 0:1],
            zsf[:, kc * TPC:(kc + 1) * TPC], op0=ALU.mult, op1=ALU.mult),
            deps=[m_cn, m_zsf, m_cst])
        m_gated[kc] = p.mark("vector")
        cn_free[kc % 2] = m_gated[kc]
    m_gate_all = p.mark("vector")

    wo_free = [[p.mark("tensor")]] * 2
    m_wo = {}

    def fetch_wout(og):
        p.dma("gpsimd", wpan[og % 2][:, :].rearrange("p (kc c) -> p kc c", kc=32),
              wout[:, og * 256:(og + 1) * 256].rearrange("(kc p) c -> p kc c", p=128),
              ch=f"wo{og % 2}", deps=wo_free[og % 2])
        m_wo[og] = p.mark(f"wo{og % 2}")

    fetch_wout(0)
    fetch_wout(1)
    ps_free = [p.mark("vector")] * 2
    x1b_free = [p.mark("sync")] * 2
    for og in range(8):
        for j in range(2):
            ob = og * 2 + j
            pm = psA if ob % 2 == 0 else psB
            for t2 in range(2):
                for kc in range(32):
                    dd = []
                    if kc == 0 and t2 == 0:
                        dd = [m_wo[og], m_gate_all, ps_free[ob % 2]]
                    p.op("tensor", lambda E, kc=kc, og=og, j=j, t2=t2, pm=pm: E.matmul(
                        pm[:, t2 * 512:(t2 + 1) * 512],
                        wpan[og % 2][:, kc * 256 + j * 128: kc * 256 + (j + 1) * 128],
                        zsf[:, kc * TPC + t2 * 512: kc * TPC + (t2 + 1) * 512],
                        start=(kc == 0), stop=(kc == 31)),
                        deps=dd, inc=(1 if kc == 31 else 0))
            m_mm = p.mark("tensor")
            p.op("vector", lambda E, ob=ob, pm=pm: E.tensor_tensor(
                hfm[:, ob * TPC:(ob + 1) * TPC], pm[:, :],
                xnf[:, ob * TPC:(ob + 1) * TPC], op=ALU.add),
                deps=[m_mm, m_xnf])
            p.op("vector", lambda E, ob=ob, pm=pm: E.tensor_tensor(
                x1b[ob % 2][:, :], pm[:, :], xnf[:, ob * TPC:(ob + 1) * TPC],
                op=ALU.add), deps=[x1b_free[ob % 2]])
            m_ev = p.mark("vector")
            ps_free[ob % 2] = m_ev
            p.dma("sync", x1a_d[ob * 128:(ob + 1) * 128, :], x1b[ob % 2][:, :],
                  ch=f"x1b{ob % 2}", deps=[m_ev])
            x1b_free[ob % 2] = p.mark(f"x1b{ob % 2}")
        wo_free[og % 2] = [p.mark("tensor")]
        if og + 2 < 8:
            fetch_wout(og + 2)
    m_hfm = p.mark("vector")
    sco1.close()

    barrier2 = p.allmarks()
    sco2 = p.scope()
    sqs = [p.sb(f"sqs{i}", [128, TPC], BF16, sco2) for i in range(2)]
    rowt = p.sb("rowt", [1, 3 * TPC], F32, sco2)
    ps_row = p.ps("psrow", [1, TPC], scope=sco2)
    ps_bc = p.ps("psbc", [128, TPC], scope=sco2)
    sq_free = [p.mark("tensor")] * 2
    for kc in range(KD):
        p.op("scalar", lambda E, kc=kc: E.activation(
            sqs[kc % 2][:, :], hfm[:, kc * TPC:(kc + 1) * TPC], AF.Square),
            deps=[m_hfm, sq_free[kc % 2]] + (barrier2 if kc == 0 else []))
        m_sq = p.mark("scalar")
        for half in range(2):
            p.op("tensor", lambda E, kc=kc, half=half: E.matmul(
                ps_row[:, half * 512:(half + 1) * 512], ones_b[:, :],
                sqs[kc % 2][:, half * 512:(half + 1) * 512],
                start=(kc == 0), stop=(kc == KD - 1)),
                deps=[m_sq] if half == 0 else [], inc=(1 if half == 1 else 0))
        sq_free[kc % 2] = p.mark("tensor")
    m_ssq = p.mark("tensor")
    p.op("vector", lambda E: E.tensor_scalar(rowt[:, 0:TPC], ps_row[:, :],
                                             1.0 / D, 1e-6, op0=ALU.mult, op1=ALU.add),
         deps=[m_ssq])
    m_v = p.mark("vector")
    p.op("scalar", lambda E: E.activation(rowt[:, TPC:2 * TPC], rowt[:, 0:TPC],
                                          AF.Ln, bias=1e-6), deps=[m_v])
    p.op("scalar", lambda E: E.activation(rowt[:, TPC:2 * TPC], rowt[:, TPC:2 * TPC],
                                          AF.Exp, scale=-0.5))
    m_s = p.mark("scalar")
    _newton_rsqrt(p, rowt[:, TPC:2 * TPC], rowt[:, 0:TPC], rowt[:, 2 * TPC:], [m_s])
    m_rst = p.mark("vector")
    for half in range(2):
        p.op("tensor", lambda E, half=half: E.matmul(
            ps_bc[:, half * 512:(half + 1) * 512], ones_f1[:, :],
            rowt[0:1, TPC + half * 512: TPC + (half + 1) * 512],
            start=True, stop=True), deps=[m_rst, m_cst] if half == 0 else [],
            inc=(1 if half == 1 else 0))
    m_bc = p.mark("tensor")
    for kc in range(KD):
        p.op("vector", lambda E, kc=kc: E.scalar_tensor_tensor(
            h2f[:, kc * TPC:(kc + 1) * TPC], hfm[:, kc * TPC:(kc + 1) * TPC],
            ln2[:, kc:kc + 1], ps_bc[:, :], op0=ALU.mult, op1=ALU.mult),
            deps=[m_bc] if kc == 0 else [])
    m_h2 = p.mark("vector")
    sco2.close()
    sco_h.close()


    sco3 = p.scope()
    wq2 = [p.sb(f"wi{i}", [128, KD * 512], BF16, sco3) for i in range(2)]
    gu = [p.sb(f"gu{i}", [128, TPC], BF16, sco3) for i in range(2)]
    m5b = [p.sb(f"m5b{i}", [128, TPC], BF16, sco3) for i in range(2)]
    psg = [p.ps(f"psg{i}", [128, 512], scope=sco3) for i in range(2)]
    m5b_free = [p.mark("sync")] * 2
    wi_free = [[p.mark("tensor")]] * 2
    gu_free = [p.mark("vector")] * 2
    psg_free = [p.mark("scalar")] * 2
    m_wi = {}

    def fetch_wi(ig, which):
        w = wg5 if which == 0 else wu5
        p.dma("gpsimd", wq2[which][:, :].rearrange("p (kc c) -> p kc c", kc=KD),
              w[:, ig * 512:(ig + 1) * 512].rearrange("(kc p) c -> p kc c", p=128),
              ch=f"wi{which}", deps=wi_free[which])
        m_wi[(ig, which)] = p.mark(f"wi{which}")

    wi_free = [[p.mark("tensor")] + p.allmarks()] * 2
    fetch_wi(0, 0)
    fetch_wi(0, 1)
    for ig in range(10):
        for j in range(4):
            ib = ig * 4 + j
            for which in range(2):
                pm = psg[which]
                for t2 in range(2):
                    for kc in range(KD):
                        dd = []
                        if kc == 0:
                            dd = [m_wi[(ig, which)], psg_free[which]]
                            if t2 == 0 and j == 0 and which == 0 and ig == 0:
                                dd.append(m_h2)
                        p.op("tensor", lambda E, kc=kc, j=j, t2=t2, pm=pm, which=which: E.matmul(
                            pm[:, 0:512],
                            wq2[which][:, kc * 512 + j * 128: kc * 512 + (j + 1) * 128],
                            h2f[:, kc * TPC + t2 * 512: kc * TPC + (t2 + 1) * 512],
                            start=(kc == 0), stop=(kc == KD - 1)),
                            deps=dd, inc=(1 if kc == KD - 1 else 0))
                    m_pm = p.mark("tensor")
                    if which == 0:
                        p.op("scalar", lambda E, t2=t2, pm=pm: E.activation(
                            gu[0][:, t2 * 512:(t2 + 1) * 512], pm[:, 0:512], AF.Silu),
                            deps=[m_pm] + ([gu_free[0]] if t2 == 0 else []))
                    else:
                        p.op("scalar", lambda E, t2=t2, pm=pm: E.activation(
                            gu[1][:, t2 * 512:(t2 + 1) * 512], pm[:, 0:512], AF.Copy),
                            deps=[m_pm] + ([gu_free[1]] if t2 == 0 else []))
                    psg_free[which] = p.mark("scalar")
            m_gu = p.mark("scalar")
            p.op("vector", lambda E, ib=ib: E.tensor_tensor(
                m5b[ib % 2][:, :], gu[0][:, :], gu[1][:, :],
                op=ALU.mult), deps=[m_gu, m5b_free[ib % 2]])
            gu_free[0] = gu_free[1] = p.mark("vector")
            p.dma("sync", m5_d[ib * 128:(ib + 1) * 128, :], m5b[ib % 2][:, :],
                  ch=f"m5{ib % 2}", deps=[p.mark("vector")])
            m5b_free[ib % 2] = p.mark(f"m5{ib % 2}")
        wi_free = [[p.mark("tensor")], [p.mark("tensor")]]
        if ig + 1 < 10:
            fetch_wi(ig + 1, 0)
            fetch_wi(ig + 1, 1)
    m_m5 = p.mark("vector")
    sco3.close()

    barrier4 = p.allmarks()
    for _e in ENGINES:
        p.wait(_e, barrier4)
    sco4 = p.scope()
    wdp = p.sb("wdp", [128, IH * 512], BF16, sco4)
    m5h = p.sb("m5h", [128, IH * 512], BF16, sco4)
    f0b = [p.sb(f"f0b{i}", [128, 512], F32, sco4) for i in range(2)]
    x1ab = [p.sb(f"x1ab{i}", [128, 512], F32, sco4) for i in range(2)]
    x1s = [p.sb(f"x1s{i}", [128, 512], F32, sco4) for i in range(2)]
    psd = [p.ps(f"psd{i}", [128, 512], scope=sco4) for i in range(2)]
    x1f_d = p.dtmp("x1f_d", [D, TPC], F32)
    wd_free = [p.mark("tensor")]
    m5h_free = [p.mark("tensor")]
    psd_free = [p.mark("vector")] * 2
    fb_free = [p.mark("vector")] * 2
    x1s_free = [p.mark("sync")] * 2
    for t2 in range(2):
        p.dma("sync", m5h[:, :].rearrange("p (kc c) -> p kc c", kc=IH),
              m5_d[:, t2 * 512:(t2 + 1) * 512].rearrange("(kc p) c -> p kc c", p=128),
              ch="m5h", deps=m5h_free + [p.mark("m50"), p.mark("m51")])
        m_m5h = p.mark("m5h")
        for og in range(4):
            p.dma("gpsimd", wdp[:, :].rearrange("p (kc c) -> p kc c", kc=IH),
                  wd5[:, og * 512:(og + 1) * 512].rearrange("(kc p) c -> p kc c", p=128),
                  ch="wd", deps=wd_free)
            m_wd = p.mark("wd")
            for j in range(4):
                ob = og * 4 + j
                pm = psd[j % 2]
                p.dma("sync", f0b[ob % 2][:, :],
                      f0_i[ob * 128:(ob + 1) * 128, t2 * 512:(t2 + 1) * 512],
                      ch=f"f0{ob % 2}", deps=[fb_free[ob % 2]])
                p.dma("sync", x1ab[ob % 2][:, :],
                      x1a_d[ob * 128:(ob + 1) * 128, t2 * 512:(t2 + 1) * 512],
                      ch=f"f0{ob % 2}")
                m_fb = p.mark(f"f0{ob % 2}")
                for kc in range(IH):
                    dd = [m_wd, m_m5h, psd_free[j % 2]] if kc == 0 else []
                    p.op("tensor", lambda E, kc=kc, j=j, pm=pm: E.matmul(
                        pm[:, 0:512],
                        wdp[:, kc * 512 + j * 128: kc * 512 + (j + 1) * 128],
                        m5h[:, kc * 512:(kc + 1) * 512],
                        start=(kc == 0), stop=(kc == IH - 1)),
                        deps=dd, inc=(1 if kc == IH - 1 else 0))
                m_dn = p.mark("tensor")
                p.op("vector", lambda E, ob=ob, pm=pm: E.tensor_tensor(
                    x1s[ob % 2][:, :], pm[:, 0:512], x1ab[ob % 2][:, :], op=ALU.add),
                    deps=[m_dn, m_fb, x1s_free[ob % 2]])
                p.op("vector", lambda E, ob=ob: E.tensor_tensor(
                    x1s[ob % 2][:, :], x1s[ob % 2][:, :], f0b[ob % 2][:, :],
                    op=ALU.add))
                m_x1 = p.mark("vector")
                psd_free[j % 2] = m_x1
                fb_free[ob % 2] = m_x1
                p.dma("sync", x1f_d[ob * 128:(ob + 1) * 128, t2 * 512:(t2 + 1) * 512],
                      x1s[ob % 2][:, :], ch=f"x1f{ob % 2}", deps=[m_x1])
                x1s_free[ob % 2] = p.mark(f"x1f{ob % 2}")
            wd_free = [p.mark("tensor")]
        m5h_free = [p.mark("tensor")]
    m_x1f = [p.mark("x1f0"), p.mark("x1f1")]
    sco4.close()

    barrier5 = p.allmarks()
    xn1f = p.sb("xn1f_s", [128, KD * TPC], BF16)
    sco5 = p.scope()
    x1r = [p.sb(f"x1r{i}", [128, KD * 128], F32, sco5) for i in range(2)]
    xb = [p.sb(f"xb{i}", [128, D], F32, sco5) for i in range(2)]
    x1t = [p.sb(f"x1t{i}", [128, D], F32, sco5) for i in range(2)]
    xn1t = p.sb("xn1t", [128, D], BF16, sco5)
    sq5 = p.sb("sq5", [128, D], BF16, sco5)
    st5 = p.sb("st5", [128, 8], F32, sco5)
    ps5 = p.ps("ps5", [128, D // 2], scope=sco5)
    ps5b = p.ps("ps5b", [128, TPC], BF16, scope=sco5)
    x1r_free = [p.mark("tensor")] * 2
    xb_free = [p.mark("vector")] * 2
    x1t_free = [p.mark("sync")] * 2
    m_xn1_done = [None] * 8
    for it in range(8):
        p.dma("sync", x1r[it % 2][:, :].rearrange("p (kc t) -> p kc t", kc=KD),
              x1f_d[:, it * 128:(it + 1) * 128].rearrange("(kc p) t -> p kc t", p=128),
              ch=f"x1r{it % 2}", deps=[x1r_free[it % 2]] + (barrier5 + m_x1f if it < 2 else []))
        m_rl = p.mark(f"x1r{it % 2}")
        p.dma("sync", xb[it % 2][:, :], x_in[it * 128:(it + 1) * 128, :],
              ch=f"xb{it % 2}", deps=[xb_free[it % 2]])
        m_xb = p.mark(f"xb{it % 2}")
        for half in range(2):
            for kc in range(8):
                kk = half * 8 + kc
                dd = []
                if half == 0 and kc == 0:
                    dd = [m_rl, x1r_free[it % 2]]
                if half == 1 and kc == 0:
                    dd = [p.mark("vector")]
                p.op("tensor", lambda E, kk=kk, it=it, half=half, kc=kc: E.transpose(
                    ps5[:, kc * 128:(kc + 1) * 128], x1r[it % 2][:, kk * 128:(kk + 1) * 128],
                    identf[:, :]),
                    deps=dd, inc=(1 if kc == 7 else 0))
            m_tr = p.mark("tensor")
            p.op("vector", lambda E, it=it, half=half: E.tensor_tensor(
                x1t[it % 2][:, half * 1024:(half + 1) * 1024], ps5[:, :],
                xb[it % 2][:, half * 1024:(half + 1) * 1024], op=ALU.add),
                deps=[m_tr, m_xb] + ([x1t_free[it % 2]] if half == 0 else []))
        m_x1t = p.mark("vector")
        x1r_free[it % 2] = p.mark("tensor")
        p.dma("sync", x1_o[it * 128:(it + 1) * 128, :], x1t[it % 2][:, :],
              ch=f"x1o{it % 2}", deps=[m_x1t])
        x1t_free[it % 2] = p.mark(f"x1o{it % 2}")
        s = st5
        p.op("vector", lambda E, it=it, s=s: E.reduce_sum(
            s[:, 0:1], x1t[it % 2][:, :], axis=AX.X), deps=[m_x1t])
        p.op("scalar", lambda E, it=it: E.activation(sq5[:, :], x1t[it % 2][:, :],
                                                     AF.Square), deps=[m_x1t])
        m_sq = p.mark("scalar")
        p.op("vector", lambda E, s=s: E.reduce_sum(s[:, 1:2], sq5[:, :], axis=AX.X),
             deps=[m_sq])
        p.op("vector", lambda E, s=s: E.tensor_scalar(s[:, 2:3], s[:, 0:1], 1.0 / D, None, op0=ALU.mult))
        p.op("vector", lambda E, s=s: E.tensor_scalar(s[:, 3:4], s[:, 1:2], 1.0 / D, None, op0=ALU.mult))
        p.op("vector", lambda E, s=s: E.tensor_tensor(s[:, 4:5], s[:, 2:3], s[:, 2:3], op=ALU.mult))
        p.op("vector", lambda E, s=s: E.tensor_tensor(s[:, 5:6], s[:, 3:4], s[:, 4:5], op=ALU.subtract))
        p.op("vector", lambda E, s=s: E.tensor_scalar(s[:, 4:5], s[:, 5:6], 1e-5, None, op0=ALU.add))
        m_v5 = p.mark("vector")
        p.op("scalar", lambda E, s=s: E.activation(s[:, 6:7], s[:, 5:6], AF.Ln, bias=1e-5),
             deps=[m_v5])
        p.op("scalar", lambda E, s=s: E.activation(s[:, 7:8], s[:, 6:7], AF.Exp, scale=-0.5))
        m_r5 = p.mark("scalar")
        _newton_rsqrt(p, s[:, 7:8], s[:, 4:5], s[:, 6:7], [m_r5])
        m_r5 = p.mark("vector")
        p.op("vector", lambda E, it=it, s=s: E.tensor_scalar(
            xn1t[:, :], x1t[it % 2][:, :], s[:, 2:3], s[:, 7:8],
            op0=ALU.subtract, op1=ALU.mult), deps=[m_r5])
        m_xn1 = p.mark("vector")
        ev5 = p.mark("vector")
        for half in range(2):
            for kc8 in range(8):
                kc = half * 8 + kc8
                dd = [m_xn1] if (half == 0 and kc8 == 0) else []
                if kc8 == 0 and half == 1:
                    dd = [ev5]
                p.op("tensor", lambda E, kc=kc, kc8=kc8, it=it: E.transpose(
                    ps5b[:, kc8 * 128:(kc8 + 1) * 128],
                    xn1t[:, kc * 128:(kc + 1) * 128], ident[:, :]),
                    deps=dd, inc=(1 if kc8 == 7 else 0))
            m_t5 = p.mark("tensor")
            p.op("vector", lambda E, it=it, half=half: E.tensor_copy(
                xn1f[:, :].rearrange("p (kc t) -> p kc t", kc=KD)
                [:, half * 8:(half + 1) * 8, it * 128:(it + 1) * 128],
                ps5b[:, 0:TPC].rearrange("p (a t) -> p a t", a=8)), deps=[m_t5])
            ev5 = p.mark("vector")
        m_xn1_done[it] = p.mark("vector")
        xb_free[it % 2] = m_xn1_done[it]
    m_xn1f = p.mark("vector")
    for kc in range(KD):
        p.dma("sync", xn1_o[kc * 128:(kc + 1) * 128, :],
              xn1f[:, kc * TPC:(kc + 1) * TPC], ch="xn1o", deps=[m_xn1f])

    barrier6 = p.allmarks()
    for _e in ENGINES:
        p.wait(_e, barrier6)
    sco6 = p.scope()
    wq6 = [p.sb(f"wq6{i}", [128, KD * 512], BF16, sco6) for i in range(2)]
    qb = [p.sb(f"qb{i}", [128, TPC], BF16, sco6) for i in range(2)]
    ps6 = [p.ps(f"ps6{i}", [128, 512], scope=sco6) for i in range(2)]
    outs = [q_o, k_o, v_o]
    wsrcs = [wqT_i, wkT_i, wvT_i]
    wq6_free = [[p.mark("tensor")]] * 2
    qb_free = [p.mark("sync")] * 2
    ps6_free = [p.mark("vector")] * 2
    m_w6 = {}

    def fetch_w6(idx):
        which, ng = idx // 4, idx % 4
        p.dma("gpsimd", wq6[idx % 2][:, :].rearrange("p (kc c) -> p kc c", kc=KD),
              wsrcs[which][:, ng * 512:(ng + 1) * 512].rearrange("(kc p) c -> p kc c", p=128),
              ch=f"w6{idx % 2}", deps=wq6_free[idx % 2] + (barrier6 if idx < 2 else []))
        m_w6[idx] = p.mark(f"w6{idx % 2}")

    fetch_w6(0)
    fetch_w6(1)
    for idx in range(12):
        which, ng = idx // 4, idx % 4
        for j in range(4):
            nb = ng * 4 + j
            pm = ps6[j % 2]
            for t2 in range(2):
                for kc in range(KD):
                    dd = [m_w6[idx], m_xn1f, ps6_free[j % 2]] if kc == 0 else []
                    p.op("tensor", lambda E, kc=kc, j=j, t2=t2, pm=pm, idx=idx: E.matmul(
                        pm[:, 0:512],
                        wq6[idx % 2][:, kc * 512 + j * 128: kc * 512 + (j + 1) * 128],
                        xn1f[:, kc * TPC + t2 * 512: kc * TPC + (t2 + 1) * 512],
                        start=(kc == 0), stop=(kc == KD - 1)),
                        deps=dd, inc=(1 if kc == KD - 1 else 0))
                m_pm = p.mark("tensor")
                p.op("vector", lambda E, t2=t2, pm=pm, j=j: E.tensor_copy(
                    qb[j % 2][:, t2 * 512:(t2 + 1) * 512], pm[:, 0:512]),
                    deps=[m_pm] + ([qb_free[j % 2]] if t2 == 0 else []))
                ps6_free[j % 2] = p.mark("vector")
            m_qb = p.mark("vector")
            p.dma("sync", outs[which][nb * 128:(nb + 1) * 128, :], qb[j % 2][:, :],
                  ch=f"qo{j % 2}", deps=[m_qb])
            qb_free[j % 2] = p.mark(f"qo{j % 2}")
        wq6_free[idx % 2] = [p.mark("tensor")]
        if idx + 2 < 12:
            fetch_w6(idx + 2)
    sco6.close()
    p.wait("sync", p.allmarks())
    return p.build()




def build_L4():
    p = P("L4")
    NT16 = 16
    qf = p.din("qfh", [256, L], BF16)
    kf = p.din("kfh", [256, L], BF16)
    vf = p.din("vfh", [256, L], BF16)
    identb_i = p.din("identb", [128, 128], BF16)
    onesb_i = p.din("onesb", [128, 1], BF16)
    onesf_i = p.din("onesf", [1, 128], F32)
    mdiag_i = p.din("mdiag", [128, 4 * 512], F32)
    cbank_i = p.din("cbank", [128, 16], F32)
    z_o = p.dout("z_fm", [256, L], BF16)

    for v in L1_CONSTS:
        p.constf(v)

    ident = p.sb("ident_s", [128, 128], BF16)
    ones_b = p.sb("ones_b", [128, 1], BF16)
    ones_f1 = p.sb("ones_f1", [1, 128], F32)
    mdiag = p.sb("mdiag_s", [128, 4 * 512], F32)
    kb_s = p.sb("kb_s", [128, L], BF16)
    qb_s = p.sb("qb_s", [128, L], BF16)
    vb_s = p.sb("vb_s", [128, L], BF16)
    vtm = p.sb("vtm", [128, L], BF16)
    zb_s = p.sb("zb_s", [128, L], BF16)
    ptile = [p.sb(f"pt{i}", [128, 512], BF16) for i in range(4)]
    dscr = [p.sb(f"dscr4{i}", [128, 512], F32) for i in range(4)]
    rowf = p.sb("rowf", [1, 1024], F32)
    rbc = p.sb("rbc", [128, 512], F32)
    psS = [p.ps(f"psS{i}", [128, 512]) for i in range(4)]
    psZ = p.ps("psZ", [128, 512])
    psR = p.ps("psR", [1, 512])
    psB2 = p.ps("psB2", [128, 512])
    psT4 = p.ps("psT4", [128, 512], BF16)

    p.dma("sync", ident[:, :], identb_i[:, :], ch="cst")
    p.dma("sync", ones_b[:, :], onesb_i[:, :], ch="cst")
    p.dma("sync", ones_f1[:, :], onesf_i[:, :], ch="cst")
    p.dma("sync", mdiag[:, :], mdiag_i[:, :], ch="cst")
    p.dma("sync", p._cbank[:, :], cbank_i[:, :], ch="cst")
    m_cst = p.mark("cst")
    p.const_barrier = m_cst

    zdma = [p.mark("z0"), p.mark("z1")]
    for hh in range(2):
        chq = f"qkv{hh}"
        p.dma("sync", kb_s[:, :], kf[hh * 128:(hh + 1) * 128, :], ch=chq,
              deps=([p.mark("tensor")] if hh else []))
        p.dma("sync", qb_s[:, :], qf[hh * 128:(hh + 1) * 128, :], ch=chq)
        p.dma("sync", vb_s[:, :], vf[hh * 128:(hh + 1) * 128, :], ch=chq)
        m_ld = p.mark(chq)
        vt_free = p.mark("vector")
        for grp in range(16):
            for i4 in range(4):
                kb = grp * 4 + i4
                p.op("tensor", lambda E, kb=kb, i4=i4: E.transpose(
                    psT4[:, i4 * 128:(i4 + 1) * 128],
                    vb_s[:, kb * 128:(kb + 1) * 128], ident[:, :]),
                    deps=([m_ld, m_cst, vt_free] if i4 == 0 else []),
                    inc=(1 if i4 == 3 else 0))
            m_t = p.mark("tensor")
            p.op("vector", lambda E, grp=grp: E.tensor_copy(
                vtm[:, grp * 512:(grp + 1) * 512], psT4[:, :]), deps=[m_t])
            vt_free = p.mark("vector")
        m_vtm = p.mark("vector")

        pt_free = [p.mark("tensor")] * 4
        psS_free = [p.mark("scalar")] * 4
        dscr_free = [p.mark("scalar")] * 4
        psZR_free = p.mark("vector")
        for tq in range(NT16):
            nkb = 4 * tq + 4
            for g in range(tq + 1):
                m_s = [None] * 4
                for j in range(4):
                    kb = 4 * g + j
                    dd = [psS_free[j]]
                    if g == 0 and j == 0:
                        dd += [m_ld, psZR_free]
                    p.op("tensor", lambda E, kb=kb, tq=tq, j=j: E.matmul(
                        psS[j][:, :], kb_s[:, kb * 128:(kb + 1) * 128],
                        qb_s[:, tq * 512:(tq + 1) * 512], start=True, stop=True),
                        deps=dd, inc=1)
                    m_s[j] = p.mark("tensor")
                m_p = [None] * 4
                if g == tq:
                    for j in range(4):
                        p.op("vector", lambda E, j=j: E.tensor_tensor(
                            dscr[j][:, :], psS[j][:, :],
                            mdiag[:, j * 512:(j + 1) * 512], op=ALU.add),
                            deps=[m_s[j], dscr_free[j]], inc=1)
                        m_d = p.mark("vector")
                        psS_free[j] = m_d
                        p.op("scalar", lambda E, j=j: E.activation(
                            ptile[j][:, :], dscr[j][:, :], AF.Exp,
                            scale=0.08838834764831845),
                            deps=[m_d, pt_free[j]], inc=1)
                        m_p[j] = p.mark("scalar")
                        dscr_free[j] = m_p[j]
                else:
                    for j in range(4):
                        p.op("scalar", lambda E, j=j: E.activation(
                            ptile[j][:, :], psS[j][:, :], AF.Exp,
                            scale=0.08838834764831845),
                            deps=[m_s[j], pt_free[j]], inc=1)
                        m_p[j] = p.mark("scalar")
                        psS_free[j] = m_p[j]
                for j in range(4):
                    kb = 4 * g + j
                    p.op("tensor", lambda E, kb=kb, j=j, nkb=nkb: E.matmul(
                        psZ[:, :], vtm[:, kb * 128:(kb + 1) * 128],
                        ptile[j][:, :],
                        start=(kb == 0), stop=(kb == nkb - 1)),
                        deps=([m_p[j], m_vtm] if kb == 0 else [m_p[j]]), inc=0)
                    p.op("tensor", lambda E, kb=kb, j=j, nkb=nkb: E.matmul(
                        psR[:, :], ones_b[:, :], ptile[j][:, :],
                        start=(kb == 0), stop=(kb == nkb - 1)), inc=1)
                    pt_free[j] = p.mark("tensor")
            m_pv = p.mark("tensor")
            p.op("vector", lambda E: E.tensor_copy(rowf[0:1, 0:512], psR[:, :]),
                 deps=[m_pv])
            p.op("vector", lambda E: E.reciprocal(rowf[0:1, 512:1024], rowf[0:1, 0:512]))
            m_r = p.mark("vector")
            p.op("tensor", lambda E: E.matmul(psB2[:, :], ones_f1[:, :],
                                              rowf[0:1, 512:1024], start=True,
                                              stop=True), deps=[m_r], inc=1)
            m_bc = p.mark("tensor")
            p.op("vector", lambda E: E.tensor_copy(rbc[:, :], psB2[:, :]),
                 deps=[m_bc])
            m_rbc = p.mark("vector")
            p.op("vector", lambda E, tq=tq: E.tensor_tensor(
                zb_s[:, tq * 512:(tq + 1) * 512], psZ[:, :], rbc[:, :],
                op=ALU.mult),
                deps=[m_rbc] + ([zdma[0]] if (hh == 1 and tq == 0) else []))
            psZR_free = p.mark("vector")
        m_z = p.mark("vector")
        p.dma("sync", z_o[hh * 128:(hh + 1) * 128, :], zb_s[:, :], ch=f"z{hh}",
              deps=[m_z])
        zdma[hh] = p.mark(f"z{hh}")
    p.wait("sync", p.allmarks())
    return p.build()




def build_L5():
    p = P("L5")
    xn1f_i = p.din("xn1f_i", [D, TPC], BF16)
    zf_i = p.din("zf_i", [D, TPC], BF16)
    x1_i = p.din("x1_i", [TPC, D], F32)
    woT_d = p.din("woT", [D, D], F32)
    WT_d = p.din("WT", [D, D], F32)
    acat_d = p.din("acatT", [D, 256], F32)
    bcat_d = p.din("bcatT", [256, D], F32)
    biasr_d = p.din("biasr", [1, D], F32)
    rw1T_d = p.din("rw1T", [D, 64], F32)
    rw2T_d = p.din("rw2T", [64, 32], F32)
    rb1c_d = p.din("rb1c", [64, 1], F32)
    rb2c_d = p.din("rb2c", [32, 1], F32)
    gatesc_d = p.din("gatesc", [32, 1], F32)
    identb_i = p.din("identb", [128, 128], BF16)
    identf_i = p.din("identf32", [128, 128], F32)
    onesf_i = p.din("onesf", [1, 128], F32)
    cbank_i = p.din("cbank", [128, 16], F32)
    x2_o = p.dout("x2_o", [TPC, D], F32)
    for v in L1_CONSTS:
        p.constf(v)

    identf = p.sb("identf", [128, 128], F32)
    identsb = p.sb("identsb", [128, 128], BF16)
    onesf_s = p.sb("onesf_s", [1, 128], F32)
    xn1f = p.sb("xn1f_s", [128, KD * TPC], BF16)
    zfs = p.sb("zfs", [128, KD * TPC], BF16)
    acat = p.sb("acat_s", [128, KD * 256], BF16)
    bcat = p.sb("bcat_s", [128, 2 * D], BF16)
    biasr = p.sb("biasr_s", [1, D], F32)
    rw1s = p.sb("rw1s", [128, KD * 64], BF16)
    rw2s = p.sb("rw2s", [64, 32], BF16)
    rb1s = p.sb("rb1s", [64, 1], F32)
    rb2s = p.sb("rb2s", [32, 1], F32)
    gts = p.sb("gts", [32, 1], F32)
    h_fm = p.sb("h_fm", [64, TPC], BF16)
    lg_fm = p.sb("lg_fm", [32, TPC], F32)
    coefs = p.sb("coefs", [128, 8 * 16], F32)
    rt = p.sb("rt", [128, 5 * 16], F32)
    rcol = p.sb("rcol", [128, 8], F32)
    wxaT = p.sb("wxaT", [128, 8 * 256], BF16)
    wxa = p.sb("wxa", [128, 256], BF16)
    wob = [p.sb(f"wob{i}", [128, KD * 512], BF16) for i in range(2)]
    wwb = [p.sb(f"wwb{i}", [128, KD * 512], BF16) for i in range(2)]
    x1b = [p.sb(f"x1b{i}", [128, 8 * 512], F32) for i in range(2)]
    x2sb = [p.sb(f"x2sb{i}", [128, 512], F32) for i in range(2)]
    psM = [p.ps(f"psM{i}", [128, 512]) for i in range(2)]
    psR = p.ps("psR5", [64, 512])
    psL = p.ps("psL5", [32, 512])
    psT5 = p.ps("psT5", [128, 16])
    psxa = p.ps("psxa", [128, 256])
    psw = p.ps("psw", [128, 256], BF16)

    p.dma("sync", identf[:, :], identf_i[:, :], ch="cst")
    p.dma("sync", identsb[:, :], identb_i[:, :], ch="cst")
    p.dma("sync", onesf_s[:, :], onesf_i[:, :], ch="cst")
    p.dma("sync", p._cbank[:, :], cbank_i[:, :], ch="cst")
    p.dma("sync", biasr[0:1, :], biasr_d[0:1, :], ch="cst")
    p.dma("sync", rb1s[:, :], rb1c_d[:, :], ch="cst")
    p.dma("sync", rb2s[:, :], rb2c_d[:, :], ch="cst")
    p.dma("sync", gts[:, :], gatesc_d[:, :], ch="cst")
    m_cst = p.mark("cst")
    p.const_barrier = m_cst
    p.dma("sync", xn1f[:, :].rearrange("p (kc t) -> p kc t", kc=KD),
          xn1f_i[:, :].rearrange("(kc p) t -> p kc t", p=128), ch="act")
    p.dma("sync", zfs[:, :].rearrange("p (kc t) -> p kc t", kc=KD),
          zf_i[:, :].rearrange("(kc p) t -> p kc t", p=128), ch="act")
    m_act = p.mark("act")
    p.dma("gpsimd", acat[:, :].rearrange("p (kc c) -> p kc c", kc=KD),
          acat_d[:, :].rearrange("(kc p) c -> p kc c", p=128), ch="wsm")
    p.dma("gpsimd", bcat[:, :].rearrange("p (kc c) -> p kc c", kc=2),
          bcat_d[:, :].rearrange("(kc p) c -> p kc c", p=128), ch="wsm")
    p.dma("gpsimd", rw1s[:, :].rearrange("p (kc c) -> p kc c", kc=KD),
          rw1T_d[:, :].rearrange("(kc p) c -> p kc c", p=128), ch="wsm")
    p.dma("gpsimd", rw2s[:, :], rw2T_d[:, :], ch="wsm")
    m_wsm = p.mark("wsm")
    m_wgt = [None, None]
    m_x1 = [None, None]
    for ob in range(2):
        ch = f"wg{ob}"
        p.dma("gpsimd", wob[ob][:, :].rearrange("p (kc c) -> p kc c", kc=KD),
              woT_d[:, ob * 512:(ob + 1) * 512]
              .rearrange("(kc p) c -> p kc c", p=128), ch=ch)
        p.dma("gpsimd", wwb[ob][:, :].rearrange("p (kc c) -> p kc c", kc=KD),
              WT_d[:, ob * 512:(ob + 1) * 512]
              .rearrange("(kc p) c -> p kc c", p=128), ch=ch)
        m_wgt[ob] = p.mark(ch)
        p.dma("sync", x1b[ob][:, :].rearrange("p (tc c) -> p tc c", tc=8),
              x1_i[:, ob * 512:(ob + 1) * 512]
              .rearrange("(tc p) c -> p tc c", p=128), ch=f"x1{ob}")
        m_x1[ob] = p.mark(f"x1{ob}")

    for hf in range(2):
        for k in range(KD):
            p.op("tensor", lambda E, hf=hf, k=k: E.matmul(
                psR[:, :], rw1s[:, k * 64:(k + 1) * 64],
                xn1f[:, k * TPC + hf * 512: k * TPC + hf * 512 + 512],
                start=(k == 0), stop=(k == KD - 1)),
                deps=([m_act, m_wsm, m_cst] if (hf == 0 and k == 0)
                      else ([p.mark("scalar")] if k == 0 else [])),
                inc=(1 if k == KD - 1 else 0))
        m_mm = p.mark("tensor")
        p.op("scalar", lambda E, hf=hf: E.activation(
            h_fm[:, hf * 512:(hf + 1) * 512], psR[:, :], AF.Silu,
            bias=rb1s[:, 0:1]), deps=[m_mm])
    m_h = p.mark("scalar")
    for hf in range(2):
        p.op("tensor", lambda E, hf=hf: E.matmul(
            psL[:, :], rw2s[:, :], h_fm[:, hf * 512:(hf + 1) * 512],
            start=True, stop=True),
            deps=([m_h] if hf == 0 else [p.mark("vector")]), inc=1)
        m_mm = p.mark("tensor")
        p.op("vector", lambda E, hf=hf: E.tensor_scalar(
            lg_fm[:, hf * 512:(hf + 1) * 512], psL[:, :], rb2s[:, 0:1],
            gts[:, 0:1], op0=ALU.add, op1=ALU.add), deps=[m_mm])
    m_lg = p.mark("vector")
    for tq in range(8):
        p.op("tensor", lambda E, tq=tq: E.transpose(
            psT5[:, :], lg_fm[0:16, tq * 128:(tq + 1) * 128],
            identf[0:16, 0:16]),
            deps=[m_lg] if tq == 0 else [p.mark("vector")], inc=1)
        m_T = p.mark("tensor")
        lg = rt[:, 0:16]
        iseq = rt[:, 16:32]
        l2 = rt[:, 32:48]
        iseq2 = rt[:, 48:64]
        tmp = rt[:, 64:80]
        p.op("vector", lambda E, lg=lg: E.tensor_copy(lg, psT5[:, :]),
             deps=[m_T])
        p.op("vector", lambda E, lg=lg: E.reduce_max(rcol[:, 0:1], lg, axis=AX.X))
        p.op("vector", lambda E, lg=lg, iseq=iseq: E.tensor_scalar(
            iseq, lg, rcol[:, 0:1], None, op0=ALU.is_equal))
        p.op("vector", lambda E, iseq=iseq, tmp=tmp: E.tensor_scalar(
            tmp, iseq, -1e30, None, op0=ALU.mult))
        p.op("vector", lambda E, lg=lg, tmp=tmp, l2=l2: E.tensor_tensor(
            l2, lg, tmp, op=ALU.add))
        p.op("vector", lambda E, l2=l2: E.reduce_max(rcol[:, 1:2], l2, axis=AX.X))
        p.op("vector", lambda E, l2=l2, iseq2=iseq2: E.tensor_scalar(
            iseq2, l2, rcol[:, 1:2], None, op0=ALU.is_equal))
        p.op("vector", lambda E: E.tensor_tensor(
            rcol[:, 2:3], rcol[:, 1:2], rcol[:, 0:1], op=ALU.subtract))
        m_v = p.mark("vector")
        p.op("scalar", lambda E: E.activation(rcol[:, 3:4], rcol[:, 2:3], AF.Exp),
             deps=[m_v])
        m_e = p.mark("scalar")
        p.op("vector", lambda E: E.tensor_scalar(
            rcol[:, 4:5], rcol[:, 3:4], 1.0, None, op0=ALU.add), deps=[m_e])
        p.op("vector", lambda E: E.reciprocal(rcol[:, 5:6], rcol[:, 4:5]))
        p.op("vector", lambda E: E.tensor_tensor(
            rcol[:, 6:7], rcol[:, 3:4], rcol[:, 5:6], op=ALU.mult))
        p.op("vector", lambda E, iseq=iseq: E.tensor_scalar(
            iseq, iseq, rcol[:, 5:6], None, op0=ALU.mult))
        p.op("vector", lambda E, iseq2=iseq2, tmp=tmp: E.tensor_scalar(
            tmp, iseq2, rcol[:, 6:7], None, op0=ALU.mult))
        p.op("vector", lambda E, tq=tq, iseq=iseq, tmp=tmp: E.tensor_tensor(
            coefs[:, tq * 16:(tq + 1) * 16], iseq, tmp, op=ALU.add))
    m_coef = p.mark("vector")

    psxa_free = m_coef
    psw_free = m_coef
    for tq in range(8):
        for k in range(KD):
            p.op("tensor", lambda E, tq=tq, k=k: E.matmul(
                psxa[:, :], xn1f[:, k * TPC + tq * 128: k * TPC + tq * 128 + 128],
                acat[:, k * 256:(k + 1) * 256],
                start=(k == 0), stop=(k == KD - 1)),
                deps=([psxa_free] if k == 0 else []),
                inc=(1 if k == KD - 1 else 0))
        m_mm = p.mark("tensor")
        p.op("vector", lambda E, tq=tq: E.tensor_tensor(
            wxa[:, :].rearrange("p (r e) -> p r e", r=16),
            psxa[:, :].rearrange("p (r e) -> p r e", r=16),
            coefs[:, tq * 16:(tq + 1) * 16].unsqueeze(1)
            .broadcast_to([128, 16, 16]), op=ALU.mult), deps=[m_mm, m_coef])
        m_wxa = p.mark("vector")
        psxa_free = m_wxa
        for pp in range(2):
            p.op("tensor", lambda E, pp=pp: E.transpose(
                psw[:, pp * 128:(pp + 1) * 128],
                wxa[:, pp * 128:(pp + 1) * 128], identsb[:, :]),
                deps=[m_wxa, psw_free] if pp == 0 else [], inc=(pp == 1))
        m_tr = p.mark("tensor")
        p.op("vector", lambda E, tq=tq: E.tensor_copy(
            wxaT[:, tq * 256:(tq + 1) * 256], psw[:, :]), deps=[m_tr])
        psw_free = p.mark("vector")
    m_wxaT = p.mark("vector")

    psM_free = [m_wxaT, m_wxaT]
    out_free = [None, None]
    ev_done = {}
    for ob in range(4):
        b = ob % 2
        for tq in range(8):
            dd0 = [m_wgt[b], psM_free[tq % 2]]
            if ob == 0 and tq == 0:
                dd0 += [m_act, m_wxaT, m_wsm]
            for k in range(KD):
                p.op("tensor", lambda E, tq=tq, k=k, b=b: E.matmul(
                    psM[tq % 2][:, :],
                    zfs[:, k * TPC + tq * 128: k * TPC + tq * 128 + 128],
                    wob[b][:, k * 512:(k + 1) * 512],
                    start=(k == 0), stop=False),
                    deps=(dd0 if k == 0 else []), inc=0)
            for k in range(KD):
                p.op("tensor", lambda E, tq=tq, k=k, b=b: E.matmul(
                    psM[tq % 2][:, :],
                    xn1f[:, k * TPC + tq * 128: k * TPC + tq * 128 + 128],
                    wwb[b][:, k * 512:(k + 1) * 512],
                    start=False, stop=False), inc=0)
            p.op("tensor", lambda E, tq=tq, ob=ob: E.matmul(
                psM[tq % 2][:, :], onesf_s[0:1, 0:128],
                biasr[0:1, ob * 512:(ob + 1) * 512],
                start=False, stop=False), inc=0)
            for pp in range(2):
                p.op("tensor", lambda E, tq=tq, pp=pp, ob=ob: E.matmul(
                    psM[tq % 2][:, :],
                    wxaT[:, tq * 256 + pp * 128: tq * 256 + pp * 128 + 128],
                    bcat[:, pp * D + ob * 512: pp * D + ob * 512 + 512],
                    start=False, stop=(pp == 1)), inc=(1 if pp == 1 else 0))
            m_mm = p.mark("tensor")
            dd = [m_mm, m_x1[b]]
            if out_free[tq % 2] is not None:
                dd.append(out_free[tq % 2])
            p.op("vector", lambda E, tq=tq, b=b: E.tensor_tensor(
                x2sb[tq % 2][:, :], psM[tq % 2][:, :],
                x1b[b][:, tq * 512:(tq + 1) * 512], op=ALU.add), deps=dd)
            psM_free[tq % 2] = p.mark("vector")
            p.dma("sync", x2_o[tq * 128:(tq + 1) * 128, ob * 512:(ob + 1) * 512],
                  x2sb[tq % 2][:, :], ch=f"out{tq % 2}", deps=[psM_free[tq % 2]])
            out_free[tq % 2] = p.mark(f"out{tq % 2}")
        if ob + 2 < 4:
            m_done = p.mark("tensor")
            nb = (ob + 2) % 2
            ch = f"wg{nb}"
            p.dma("gpsimd", wob[nb][:, :].rearrange("p (kc c) -> p kc c", kc=KD),
                  woT_d[:, (ob + 2) * 512:(ob + 3) * 512]
                  .rearrange("(kc p) c -> p kc c", p=128), ch=ch,
                  deps=[m_done])
            p.dma("gpsimd", wwb[nb][:, :].rearrange("p (kc c) -> p kc c", kc=KD),
                  WT_d[:, (ob + 2) * 512:(ob + 3) * 512]
                  .rearrange("(kc p) c -> p kc c", p=128), ch=ch)
            m_wgt[nb] = p.mark(ch)
            p.dma("sync", x1b[nb][:, :].rearrange("p (tc c) -> p tc c", tc=8),
                  x1_i[:, (ob + 2) * 512:(ob + 3) * 512]
                  .rearrange("(tc p) c -> p tc c", p=128), ch=f"x1{nb}",
                  deps=[p.mark("vector")])
            m_x1[nb] = p.mark(f"x1{nb}")
    p.wait("sync", p.allmarks())
    return p.build()


def _prep_L1_shared(ins):
    Wq = np.asarray(ins["L0_attn_W_qkvz"]).reshape(D, 16, 768)
    wmz = np.concatenate([
        Wq[:, :, 0:128].reshape(D, -1), Wq[:, :, 128:256].reshape(D, -1),
        Wq[:, :, 256:512].reshape(D, -1), Wq[:, :, 512:768].reshape(D, -1)], axis=1)
    wmz = np.ascontiguousarray(wmz, np.float32)
    wb = np.asarray(ins["L0_attn_W_ba"]).reshape(D, 16, 4)
    wba = np.ascontiguousarray(np.concatenate(
        [wb[:, :, 0:2].reshape(D, -1), wb[:, :, 2:4].reshape(D, -1)], axis=1), np.float32)
    cwr = np.ascontiguousarray(
        np.asarray(ins["L0_attn_conv_w"]).reshape(64, 128, 4).transpose(1, 0, 2)
        .reshape(128, 256), np.float32)
    ln1c = np.ascontiguousarray(np.asarray(ins["L0_attn_ln1_w"]).reshape(KD, 128).T,
                                np.float32)
    alog = np.asarray(ins["L0_attn_A_log"]).reshape(1, 32).astype(np.float32)
    dtb = np.asarray(ins["L0_attn_dt_bias"]).reshape(1, 32).astype(np.float32)
    wgT = np.ascontiguousarray(np.asarray(ins["L0_ffn_wq_gate"]).T)
    wuT = np.ascontiguousarray(np.asarray(ins["L0_ffn_wq_up"]).T)
    wdT = np.ascontiguousarray(np.asarray(ins["L0_ffn_wq_down"]).T)
    sgc = np.ascontiguousarray(np.asarray(ins["L0_ffn_s_gate"]).reshape(44, 128).T,
                               np.float32)
    suc = np.ascontiguousarray(np.asarray(ins["L0_ffn_s_up"]).reshape(44, 128).T,
                               np.float32)
    sdc = np.ascontiguousarray(np.asarray(ins["L0_ffn_s_down"]).reshape(KD, 128).T,
                               np.float32)
    d = dict(wmz=wmz, wba=wba, cwr=cwr, ln1c=ln1c, alog=alog, dtb=dtb,
             wgT=wgT, wuT=wuT, wdT=wdT, sgc=sgc, suc=suc, sdc=sdc)
    d.update(_const_inputs(L1_CONSTS))
    return d


def _l2_const_inputs():
    r = np.arange(128)
    mTs = np.where(r[:, None] < r[None, :], 0.0, 1e30).astype(np.float32)
    mTi = np.where(r[:, None] <= r[None, :], 0.0, 1e30).astype(np.float32)
    mLs = np.where(r[None, :] < r[:, None], 0.0, 1e30).astype(np.float32)
    d = _const_inputs(L1_CONSTS)
    e4 = np.zeros((4, 512), np.float32)
    for h in range(4):
        e4[h, h * 128:(h + 1) * 128] = 1.0
    return {
        "identb": d["identb"], "identf32": d["identf32"], "cbank": d["cbank"],
        "onesf": d["onesf"], "onest": np.ones((128, 128), np.float32),
        "mTs": mTs, "mTi": mTi, "mLs": mLs,
        "e4n": -e4, "e4p": e4, "onesfn": -np.ones((1, 128), np.float32),
    }


def _l2_maps(qkv_fm, gbl):
    consts = _l2_const_inputs()
    maps = []
    for c in range(NCORES):
        qf = np.ascontiguousarray(qkv_fm[256 * c:256 * (c + 1)])
        kf = np.ascontiguousarray(qkv_fm[2048 + 256 * c:2048 + 256 * (c + 1)])
        vf = np.ascontiguousarray(qkv_fm[4096 + 512 * c:4096 + 512 * (c + 1)])
        cols = []
        for blk in range(3):
            sl = gbl[:, 32 * blk + 4 * c: 32 * blk + 4 * c + 4]
            cols.append(sl.reshape(64, 128, 4))
        gbp = np.zeros((128, 64 * 12), np.float32)
        for grp in range(16):
            for bi in range(3):
                blkv = cols[bi][grp * 4:(grp + 1) * 4]
                gbp[:, grp * 48 + 16 * bi: grp * 48 + 16 * (bi + 1)] = \
                    blkv.transpose(1, 0, 2).reshape(128, 16)
        m = dict(consts)
        m.update(qf=qf, kf=kf, vf=vf, gbp=gbp)
        maps.append(m)
    return maps


def run_L2(qkv_fm, gbl):
    nc = build_L2()
    res = run_bass_kernel_spmd(nc, _l2_maps(qkv_fm, gbl), core_ids=list(range(NCORES)))
    return res.results


def _l1_maps(ins):
    shared = _prep_L1_shared(ins)
    x = np.asarray(ins["L0_attn_hidden_states"]).reshape(L, D).astype(np.float32)
    maps = []
    for c in range(NCORES):
        t0 = c * TPC
        x9 = np.zeros((TPC + 128, D), np.float32)
        if c > 0:
            x9[0:128] = x[t0 - 128:t0]
        x9[128:] = x[t0:t0 + TPC]
        m = dict(shared)
        m["x9"] = x9
        maps.append(m)
    return maps


def run_L1(ins):
    nc = build_L1()
    res = run_bass_kernel_spmd(nc, _l1_maps(ins), core_ids=list(range(NCORES)))
    return res.results


def _prep_L5_shared(ins):
    woT = np.ascontiguousarray(np.asarray(ins["L1_attn_Wo"], np.float32).T)
    WT = np.ascontiguousarray(np.asarray(ins["L1_ffn_W"], np.float32).T)
    A = np.asarray(ins["L1_ffn_A"], np.float32)
    Bm = np.asarray(ins["L1_ffn_B"], np.float32)
    acatT = np.ascontiguousarray(A.transpose(2, 1, 0).reshape(D, 256))
    bcatT = np.ascontiguousarray(Bm.transpose(2, 0, 1).reshape(256, D))
    biasr = np.ascontiguousarray(
        np.asarray(ins["L1_ffn_bias"], np.float32).reshape(1, D))
    rw1T = np.ascontiguousarray(np.asarray(ins["L1_ffn_rw1"], np.float32).T)
    rw2T = np.ascontiguousarray(np.asarray(ins["L1_ffn_rw2"], np.float32).T)
    rb1c = np.ascontiguousarray(
        np.asarray(ins["L1_ffn_rb1"], np.float32).reshape(64, 1))
    rb2c = np.ascontiguousarray(
        np.asarray(ins["L1_ffn_rb2"], np.float32).reshape(32, 1))
    gatesc = np.zeros((32, 1), np.float32)
    gatesc[0:16, 0] = np.asarray(ins["L1_ffn_gates"], np.float32)
    d = dict(woT=woT, WT=WT, acatT=acatT, bcatT=bcatT, biasr=biasr,
             rw1T=rw1T, rw2T=rw2T, rb1c=rb1c, rb2c=rb2c, gatesc=gatesc)
    c = _const_inputs(L1_CONSTS)
    d.update(identb=c["identb"], identf32=c["identf32"], onesf=c["onesf"],
             cbank=c["cbank"])
    return d


def _l5_maps(shared, xn1_fm, z_fm, x1_tm):
    maps = []
    for c in range(NCORES):
        t0 = c * TPC
        m = dict(shared)
        m.update(xn1f_i=np.ascontiguousarray(xn1_fm[:, t0:t0 + TPC]),
                 zf_i=np.ascontiguousarray(z_fm[:, t0:t0 + TPC]),
                 x1_i=np.ascontiguousarray(x1_tm[t0:t0 + TPC]))
        maps.append(m)
    return maps


def _l4_const_inputs():
    d = _const_inputs(L1_CONSTS)
    r128 = np.arange(128)
    r512 = np.arange(512)
    mdiag = np.zeros((128, 4 * 512), np.float32)
    for m in range(4):
        mdiag[:, m * 512:(m + 1) * 512] = np.where(
            (128 * m + r128)[:, None] <= r512[None, :], 0.0, -1e30)
    return {"identb": d["identb"], "onesb": d["onesb"], "onesf": d["onesf"],
            "cbank": d["cbank"], "mdiag": mdiag}


def _l4_maps(q_fm, k_fm, v_fm):
    consts = _l4_const_inputs()
    maps = []
    for c in range(NCORES):
        m = dict(consts)
        m.update(qfh=np.ascontiguousarray(q_fm[256 * c:256 * (c + 1)]),
                 kfh=np.ascontiguousarray(k_fm[256 * c:256 * (c + 1)]),
                 vfh=np.ascontiguousarray(v_fm[256 * c:256 * (c + 1)]))
        maps.append(m)
    return maps


def kernel(**inputs):
    ins = {k: np.asarray(v) for k, v in inputs.items()}
    consts = _const_inputs(L1_CONSTS)

    r1 = run_L1(ins)
    qkv_fm = np.concatenate([np.asarray(r["qkv_fm"]) for r in r1], axis=1)
    zs_fm = np.concatenate([np.asarray(r["zs_fm"]) for r in r1], axis=1)
    xn_fm = np.concatenate([np.asarray(r["xnf"]) for r in r1], axis=1)
    f0_fm = np.concatenate([np.asarray(r["f0f"]) for r in r1], axis=1)
    gbl = np.concatenate([np.asarray(r["gbl"]) for r in r1], axis=0)

    r2 = run_L2(qkv_fm, gbl)
    cn_fm = np.concatenate([np.asarray(r["cno"]) for r in r2], axis=0)

    x = np.asarray(ins["L0_attn_hidden_states"], np.float32).reshape(L, D)
    shared3 = dict(
        wout=np.ascontiguousarray(ins["L0_attn_W_out"], np.float32),
        gnw=np.ascontiguousarray(
            np.asarray(ins["L0_attn_gated_norm_w"]).reshape(128, 1), np.float32),
        ln2c=np.ascontiguousarray(
            np.asarray(ins["L0_attn_ln2_w"]).reshape(KD, 128).T, np.float32),
        wg5=np.ascontiguousarray(ins["L0_attn_W_gate"], np.float32),
        wu5=np.ascontiguousarray(ins["L0_attn_W_up"], np.float32),
        wd5=np.ascontiguousarray(ins["L0_attn_W_down"], np.float32),
        wqT=np.ascontiguousarray(np.asarray(ins["L1_attn_Wq"]).T, np.float32),
        wkT=np.ascontiguousarray(np.asarray(ins["L1_attn_Wk"]).T, np.float32),
        wvT=np.ascontiguousarray(np.asarray(ins["L1_attn_Wv"]).T, np.float32),
        identb=consts["identb"], identf32=consts["identf32"],
        onesb=consts["onesb"], onesf=consts["onesf"], cbank=consts["cbank"],
    )
    maps3 = []
    for c in range(NCORES):
        s = slice(c * TPC, (c + 1) * TPC)
        m = dict(shared3)
        m["xt"] = np.ascontiguousarray(x[s])
        m["xnf_i"] = np.ascontiguousarray(xn_fm[:, s])
        m["zsf_i"] = np.ascontiguousarray(zs_fm[:, s])
        m["cnf_i"] = np.ascontiguousarray(cn_fm[:, s])
        m["f0_i"] = np.ascontiguousarray(f0_fm[:, s])
        maps3.append(m)
    r3 = run_bass_kernel_spmd(build_L3(), maps3,
                              core_ids=list(range(NCORES))).results
    x1_tm = np.concatenate([np.asarray(r["x1_o"]) for r in r3], axis=0)
    xn1_fm = np.concatenate([np.asarray(r["xn1f"]) for r in r3], axis=1)
    q_fm = np.concatenate([np.asarray(r["qf_o"]) for r in r3], axis=1)
    k_fm = np.concatenate([np.asarray(r["kf_o"]) for r in r3], axis=1)
    v_fm = np.concatenate([np.asarray(r["vf_o"]) for r in r3], axis=1)

    r4 = run_bass_kernel_spmd(build_L4(), _l4_maps(q_fm, k_fm, v_fm),
                              core_ids=list(range(NCORES))).results
    z_fm = np.concatenate([np.asarray(r["z_fm"]) for r in r4], axis=0)

    maps5 = _l5_maps(_prep_L5_shared(ins), xn1_fm, z_fm, x1_tm)
    r5 = run_bass_kernel_spmd(build_L5(), maps5,
                              core_ids=list(range(NCORES))).results
    x2 = np.concatenate([np.asarray(r["x2_o"]) for r in r5], axis=0)
    return np.ascontiguousarray(x2.reshape(1, L, D), np.float32)


# version 03
# speedup vs baseline: 1.2671x; 1.0539x over previous
import numpy as np
import ml_dtypes
from contextlib import ExitStack

import concourse.bass as bass
import concourse.mybir as mybir
from concourse.bass_utils import run_bass_kernel_spmd

F32 = mybir.dt.float32
BF16 = mybir.dt.bfloat16
I32 = mybir.dt.int32
AF = mybir.ActivationFunctionType
ALU = mybir.AluOpType
AX = mybir.AxisListType
BF = ml_dtypes.bfloat16

NCORES = 8
L = 8192
D = 2048
TPC = L // NCORES
KD = D // 128

ENGINES = ("sync", "gpsimd", "vector", "scalar", "tensor")


class P:

    def __init__(self, name):
        self.nc = bass.Bass("TRN2", target_bir_lowering=False, debug=False)
        self.st = ExitStack()
        self.ops = {e: [] for e in ENGINES}
        self.cnt = {e: 0 for e in ENGINES}
        self.sem = {e: self.st.enter_context(self.nc.semaphore(f"sp_{e}"))
                    for e in ENGINES}
        self.name = name
        self.chs = set()
        self._cbank = self.st.enter_context(
            self.nc.sbuf_tensor("constbank", [128, 16], F32))
        self._cslots = {}
        self.const_barrier = None

    def constf(self, value):
        value = float(value)
        if value in self._cslots:
            return
        k = len(self._cslots)
        assert k < 16
        self._cslots[value] = k
        self.nc.const_aps.register(F32, value, self._cbank[:, k:k + 1])

    def cbank_host(self):
        a = np.zeros((128, 16), np.float32)
        for v, k in self._cslots.items():
            a[:, k] = v
        return a

    def din(self, name, shape, dt):
        return self.nc.dram_tensor(name, list(shape), dt, kind="ExternalInput").ap()

    def dout(self, name, shape, dt):
        return self.nc.dram_tensor(name, list(shape), dt, kind="ExternalOutput").ap()

    def dtmp(self, name, shape, dt):
        return self.nc.dram_tensor(name, list(shape), dt, kind="Internal").ap()

    def sb(self, name, shape, dt, scope=None):
        return (scope or self.st).enter_context(
            self.nc.sbuf_tensor(name, list(shape), dt))

    def ps(self, name, shape, dt=F32, scope=None):
        return (scope or self.st).enter_context(
            self.nc.psum_tensor(name, list(shape), dt))

    def scope(self):
        return self.st.enter_context(ExitStack())

    def _sem(self, name):
        if name not in self.sem:
            self.sem[name] = self.st.enter_context(self.nc.semaphore(f"ch_{name}"))
            self.cnt[name] = 0
            self.chs.add(name)
        return self.sem[name]

    def _emit(self, eng, k, deps, f, ksem=None):
        for d in deps:
            if d is not None:
                self._sem(d[0])
        deps2 = [(self.sem[d[0]], d[1]) for d in deps
                 if d is not None and d[1] > 0]
        if eng in ("vector", "scalar") and k > 0 and ksem is None \
                and self.cnt[eng] > 0:
            deps2.append((self.sem[eng], self.cnt[eng]))
        tgt = ksem if ksem is not None else eng
        self.cnt[tgt] = self.cnt.get(tgt, 0) + k
        semh = self.sem[tgt] if k else None

        def run(E, deps2=deps2, f=f, semh=semh, k=k):
            for h, v in deps2:
                E.wait_ge(h, v)
            ins = f(E)
            if semh is not None:
                ins.then_inc(semh, k)
        self.ops[eng].append(run)

    def mark(self, eng):
        return (eng, self.cnt.get(eng, 0))

    def allmarks(self):
        return [self.mark(e) for e in ENGINES] + [self.mark(c) for c in self.chs]

    def dma(self, eng, out, in_, ch, deps=()):
        self._sem(ch)
        self._emit(eng, 16, deps, lambda E: E.dma_start(out=out, in_=in_), ksem=ch)

    def op(self, eng, f, deps=(), inc=1):
        self._emit(eng, inc, deps, f)

    def wait(self, eng, deps):
        self._emit(eng, 0, deps, lambda E: None)

    def build(self):
        nc = self.nc
        blk = self.st.enter_context(nc.Block())
        cb = self.const_barrier
        for eng in ENGINES:
            ops = self.ops[eng]
            dec = getattr(blk, eng)

            def section(E, ops=ops, eng=eng, cb=cb):
                if cb is not None and eng != getattr(self, "cb_eng", "sync"):
                    E.wait_ge(self.sem[cb[0]], cb[1])
                for f in ops:
                    f(E)
            dec(section)
        self.st.close()
        return nc


NEWTON = True


def _newton_rsqrt(p, r_ap, v_ap, t_ap, deps):
    if not NEWTON:
        return
    p.op("vector", lambda E: E.tensor_tensor(t_ap, r_ap, r_ap, op=ALU.mult), deps=deps)
    p.op("vector", lambda E: E.tensor_tensor(t_ap, t_ap, v_ap, op=ALU.mult))
    p.op("vector", lambda E: E.tensor_scalar(t_ap, t_ap, -0.5, 1.5,
                                             op0=ALU.mult, op1=ALU.add))
    p.op("vector", lambda E: E.tensor_tensor(r_ap, r_ap, t_ap, op=ALU.mult))


def _mk_ident(p, tile):
    p.op("gpsimd", lambda E: E.memset(tile[:, :], 0.0))
    p.op("gpsimd", lambda E: E.affine_select(
        out=tile[:, :], in_=tile[:, :], compare_op=ALU.not_equal,
        fill=1.0, base=0, pattern=[[-1, 128]], channel_multiplier=1))


L1_CONSTS = (1e-5, 1e-6, 1.0)


def _const_inputs(consts):
    cb = np.zeros((128, 16), np.float32)
    for k, v in enumerate(consts):
        cb[:, k] = v
    colsel = np.zeros((128, 1024), np.float32)
    for c in range(32):
        colsel[:, c * 32 + c] = 1.0
    return {
        "identb": np.eye(128, dtype=BF),
        "identf32": np.eye(128, dtype=np.float32),
        "colsel_i": colsel.astype(BF),
        "onesb": np.ones((128, 1), BF),
        "onesf": np.ones((1, 128), np.float32),
        "cbank": cb,
    }


def build_L1():
    p = P("L1")
    TH = TPC + 128
    NT = TH // 128

    x_in = p.din("x9", [TH, D], F32)
    wmz = p.din("wmz", [D, 12288], F32)
    wba = p.din("wba", [D, 64], F32)
    cwr = p.din("cwr", [128, 256], F32)
    ln1c = p.din("ln1c", [128, KD], F32)
    alog = p.din("alog", [1, 32], F32)
    dtb = p.din("dtb", [1, 32], F32)
    wgT = p.din("wgT", [D, 5632], I32)
    wuT = p.din("wuT", [D, 5632], I32)
    wdT = p.din("wdT", [5632, D], I32)
    sgc = p.din("sgc", [128, 44], F32)
    suc = p.din("suc", [128, 44], F32)
    sdc = p.din("sdc", [128, KD], F32)
    identb_i = p.din("identb", [128, 128], BF16)
    identf_i = p.din("identf32", [128, 128], F32)
    colsel_i = p.din("colsel_i", [128, 1024], BF16)
    onesb_i = p.din("onesb", [128, 1], BF16)
    onesf_i = p.din("onesf", [1, 128], F32)
    cbank_i = p.din("cbank", [128, 16], F32)

    qkv_o = p.dout("qkv_fm", [8192, TPC], BF16)
    zs_o = p.dout("zs_fm", [4096, TPC], BF16)
    gbl_o = p.dout("gbl", [TPC, 96], F32)
    xnf_o = p.dout("xnf", [D, TPC], BF16)
    f0_o = p.dout("f0f", [D, TPC], F32)
    rows_d = p.dtmp("rows_d", [32, TPC], F32)

    ident = p.sb("ident", [128, 128], BF16)
    identf = p.sb("identf", [128, 128], F32)
    ones_b = p.sb("ones_b", [128, 1], BF16)
    ones_f1 = p.sb("ones_f1", [1, 128], F32)
    cw = p.sb("cw", [128, 256], F32)
    ln1 = p.sb("ln1", [128, KD], F32)
    small = p.sb("small", [1, 64], F32)
    sgs = p.sb("sgs", [128, 44], F32)
    sus = p.sb("sus", [128, 44], F32)
    sds = p.sb("sds", [128, KD], F32)
    colsel = p.sb("colsel", [128, 1024], BF16)

    for v in L1_CONSTS:
        p.constf(v)
    p.dma("sync", ident[:, :], identb_i[:, :], ch="cst")
    p.dma("sync", identf[:, :], identf_i[:, :], ch="cst")
    p.dma("sync", colsel[:, :], colsel_i[:, :], ch="cst")
    p.dma("sync", ones_b[:, :], onesb_i[:, :], ch="cst")
    p.dma("sync", ones_f1[:, :], onesf_i[:, :], ch="cst")
    p.dma("sync", p._cbank[:, :], cbank_i[:, :], ch="cst")

    p.dma("sync", cw[:, :], cwr[:, :], ch="cst")
    p.dma("sync", ln1[:, :], ln1c[:, :], ch="cst")
    p.dma("sync", small[0:1, 0:32], alog[:, :], ch="cst")
    p.dma("sync", small[0:1, 32:64], dtb[:, :], ch="cst")
    p.dma("sync", sgs[:, :], sgc[:, :], ch="cst")
    p.dma("sync", sus[:, :], suc[:, :], ch="cst")
    p.dma("sync", sds[:, :], sdc[:, :], ch="cst")
    m_cst = p.mark("sync")

    sco_c = p.scope()
    hfm = p.sb("hfm", [128, KD * TH], BF16, sco_c)

    sco_b = p.scope()
    xb = [p.sb(f"xb{i}", [128, D], F32, sco_b) for i in range(3)]
    xn_t = [p.sb(f"xn{i}", [128, D], BF16, sco_b) for i in range(NT)]
    h_t = [p.sb(f"h{i}", [128, D], BF16, sco_b) for i in range(NT)]
    st8 = [p.sb(f"st{i}", [128, 8], F32, sco_b) for i in range(3)]
    sq_s = [p.sb(f"sq{i}", [128, D], BF16, sco_b) for i in range(2)]

    m_xn_done = [None] * NT
    m_h_done = [None] * NT
    sq_free = [p.mark("vector")] * 2
    xb_free = [p.mark("vector")] * 3
    st_free = [p.mark("scalar")] * 3
    for i in range(NT):
        b = xb[i % 3]
        p.dma("sync", b[:, :], x_in[i * 128:(i + 1) * 128, :], ch=f"xb{i % 3}",
              deps=[xb_free[i % 3]])
        m_x = p.mark(f"xb{i % 3}")
        s = st8[i % 3]
        p.op("vector", lambda E, b=b, s=s: E.reduce_sum(s[:, 0:1], b[:, :], axis=AX.X),
             deps=[m_x, st_free[i % 3]])
        sq = sq_s[i % 2]
        p.op("scalar", lambda E, b=b, sq=sq: E.activation(
            sq[:, :], b[:, :], AF.Square),
            deps=[m_x, sq_free[i % 2], st_free[i % 3]])
        m_sq = p.mark("scalar")
        p.op("vector", lambda E, s=s, sq=sq: E.reduce_sum(s[:, 1:2], sq[:, :], axis=AX.X),
             deps=[m_sq])
        p.op("vector", lambda E, s=s: E.tensor_scalar(s[:, 2:3], s[:, 0:1], 1.0 / D, None, op0=ALU.mult))
        p.op("vector", lambda E, s=s: E.tensor_scalar(s[:, 3:4], s[:, 1:2], 1.0 / D, None, op0=ALU.mult))
        p.op("vector", lambda E, s=s: E.tensor_tensor(s[:, 4:5], s[:, 2:3], s[:, 2:3], op=ALU.mult))
        p.op("vector", lambda E, s=s: E.tensor_tensor(s[:, 5:6], s[:, 3:4], s[:, 4:5], op=ALU.subtract))
        p.op("vector", lambda E, s=s: E.tensor_scalar(s[:, 4:5], s[:, 5:6], 1e-5, None,
                                                      op0=ALU.add))
        m_v = p.mark("vector")
        p.op("scalar", lambda E, s=s: E.activation(s[:, 6:7], s[:, 5:6], AF.Ln, bias=1e-5),
             deps=[m_v])
        p.op("scalar", lambda E, s=s: E.activation(s[:, 7:8], s[:, 6:7], AF.Exp, scale=-0.5))
        m_r = p.mark("scalar")
        _newton_rsqrt(p, s[:, 7:8], s[:, 4:5], s[:, 6:7], [m_r])
        m_r = p.mark("vector")
        xnt = xn_t[i]
        p.op("vector", lambda E, b=b, s=s, xnt=xnt: E.tensor_scalar(
            xnt[:, :], b[:, :], s[:, 2:3], s[:, 7:8], op0=ALU.subtract, op1=ALU.mult),
            deps=[m_r])
        m_xn = p.mark("vector")
        xb_free[i % 3] = m_xn
        m_xn_done[i] = m_xn
        p.op("scalar", lambda E, xnt=xnt, sq=sq: E.activation(
            sq[:, :], xnt[:, :], AF.Square), deps=[m_xn])
        m_sq2 = p.mark("scalar")
        p.op("vector", lambda E, s=s, sq=sq: E.reduce_sum(s[:, 1:2], sq[:, :], axis=AX.X),
             deps=[m_sq2])
        sq_free[i % 2] = p.mark("vector")
        p.op("vector", lambda E, s=s: E.tensor_scalar(s[:, 3:4], s[:, 1:2], 1.0 / D, None, op0=ALU.mult))
        p.op("vector", lambda E, s=s: E.tensor_scalar(s[:, 4:5], s[:, 3:4], 1e-6, None,
                                                      op0=ALU.add))
        m_v2 = p.mark("vector")
        p.op("scalar", lambda E, s=s: E.activation(s[:, 6:7], s[:, 3:4], AF.Ln, bias=1e-6),
             deps=[m_v2])
        p.op("scalar", lambda E, s=s: E.activation(s[:, 7:8], s[:, 6:7], AF.Exp, scale=-0.5))
        m_r2 = p.mark("scalar")
        _newton_rsqrt(p, s[:, 7:8], s[:, 4:5], s[:, 6:7], [m_r2])
        m_r2 = p.mark("vector")
        ht = h_t[i]
        p.op("vector", lambda E, xnt=xnt, s=s, ht=ht: E.tensor_scalar(
            ht[:, :], xnt[:, :], s[:, 7:8], None, op0=ALU.mult), deps=[m_r2])
        m_h_done[i] = p.mark("vector")
        st_free[i % 3] = p.mark("scalar")

    sco_tr = p.scope()
    xnf = p.sb("xnf_s", [128, KD * TPC], BF16, sco_tr)
    ps_tr = p.ps("pstr", [128, TH], BF16, sco_tr)

    prev_evac = p.mark("vector")
    for kc in range(KD):
        for it in range(NT):
            dd = [m_h_done[it]]
            if it == 0:
                dd += [m_cst, prev_evac]
            p.op("tensor", lambda E, kc=kc, it=it: E.transpose(
                ps_tr[:, it * 128:(it + 1) * 128],
                h_t[it][:, kc * 128:(kc + 1) * 128], ident[:, :]),
                deps=dd, inc=(1 if it == NT - 1 else 0))
        m_t = p.mark("tensor")
        p.op("vector", lambda E, kc=kc: E.tensor_scalar(
            hfm[:, kc * TH:(kc + 1) * TH], ps_tr[:, :], ln1[:, kc:kc + 1], None,
            op0=ALU.mult), deps=[m_t, m_cst])
        m_ev = p.mark("vector")
        for it in range(1, NT):
            dd = [m_xn_done[it]]
            if it == 1:
                dd.append(m_ev)
            p.op("tensor", lambda E, kc=kc, it=it: E.transpose(
                ps_tr[:, (it - 1) * 128:it * 128],
                xn_t[it][:, kc * 128:(kc + 1) * 128], ident[:, :]),
                deps=dd, inc=(1 if it == NT - 1 else 0))
        m_t2 = p.mark("tensor")
        p.op("vector", lambda E, kc=kc: E.tensor_copy(
            xnf[:, kc * TPC:(kc + 1) * TPC], ps_tr[:, 0:TPC]), deps=[m_t2])
        prev_evac = p.mark("vector")
        p.dma("sync", xnf_o[kc * 128:(kc + 1) * 128, :],
              xnf[:, kc * TPC:(kc + 1) * TPC], ch="xnfo", deps=[prev_evac])
    m_hfm = p.mark("vector")
    m_xnf_out = p.mark("xnfo")
    sco_tr.close()
    sco_b.close()

    sco_d = p.scope()
    wpan = [p.sb(f"wpan{i}", [128, 16 * 512], BF16, sco_d) for i in range(2)]
    mixed = p.sb("mixed", [128, TH], BF16, sco_d)
    conv2 = [p.sb(f"conv{i}", [128, TPC], BF16, sco_d) for i in range(2)]
    qkstore = p.sb("qkstore", [128, 32 * TPC], BF16, sco_d)
    sqb = [p.sb(f"sqb{i}", [128, TPC], BF16, sco_d) for i in range(2)]
    diag4 = [p.sb(f"diag4{i}", [128, 512], BF16, sco_d) for i in range(2)]
    zsb = [p.sb(f"zsb{i}", [128, TPC], BF16, sco_d) for i in range(2)]
    rows32 = p.sb("rows32", [32, TPC], F32, sco_d)
    ps_mm = [p.ps(f"psmm{i}", [128, 512], scope=sco_d) for i in range(2)]
    ps_cv = p.ps("pscv", [128, TPC], scope=sco_d)
    ps_rows = p.ps("psrows", [32, TPC], scope=sco_d)

    NCG = 24
    wpan_free = [p.mark("tensor")] * 2
    m_wpan = {}

    def _fetch_panel(cg):
        p.dma("gpsimd", wpan[cg % 2][:, :].rearrange("p (kc c) -> p kc c", kc=KD),
              wmz[:, cg * 512:(cg + 1) * 512].rearrange("(kc p) c -> p kc c", p=128),
              ch=f"wp{cg % 2}", deps=[wpan_free[cg % 2]])
        m_wpan[cg] = p.mark(f"wp{cg % 2}")

    _fetch_panel(0)
    _fetch_panel(1)

    ps_free = [p.mark("vector"), p.mark("vector")]
    conv_free = [p.mark("sync")] * 2
    cv_ps_free = p.mark("scalar")
    sqb_free = [p.mark("tensor")] * 2
    zsb_free = [p.mark("sync")] * 2
    diag_free = [p.mark("tensor")] * 2
    mixed_free = p.mark("tensor")
    nqk = 0
    for cg in range(NCG):
        is_mixed = cg < 16
        for j in range(4):
            cb = cg * 4 + j
            spans = [(0, 512), (512, 512), (1024, 128)] if is_mixed \
                else [(128, 512), (640, 512)]
            ev_marks = []
            for si, (t0, tsz) in enumerate(spans):
                pm = ps_mm[si % 2]
                for kc in range(KD):
                    dd = []
                    if kc == 0:
                        dd = [m_wpan[cg], m_hfm, ps_free[si % 2]]
                    p.op("tensor", lambda E, kc=kc, cg=cg, j=j, t0=t0, tsz=tsz, pm=pm: E.matmul(
                        pm[:, 0:tsz],
                        wpan[cg % 2][:, kc * 512 + j * 128: kc * 512 + (j + 1) * 128],
                        hfm[:, kc * TH + t0: kc * TH + t0 + tsz],
                        start=(kc == 0), stop=(kc == KD - 1)),
                        deps=dd, inc=(1 if kc == KD - 1 else 0))
                m_mm = p.mark("tensor")
                if is_mixed:
                    dd = [m_mm]
                    if si == 0:
                        dd.append(mixed_free)
                    p.op("vector", lambda E, t0=t0, tsz=tsz, pm=pm: E.tensor_copy(
                        mixed[:, t0:t0 + tsz], pm[:, 0:tsz]), deps=dd)
                    ps_free[si % 2] = p.mark("vector")
                else:
                    zt = zsb[cb % 2]
                    dd = [m_mm]
                    if si == 0:
                        dd.append(zsb_free[cb % 2])
                    p.op("scalar", lambda E, t0=t0, tsz=tsz, pm=pm, zt=zt: E.activation(
                        zt[:, t0 - 128:t0 - 128 + tsz], pm[:, 0:tsz], AF.Silu), deps=dd)
                    ps_free[si % 2] = p.mark("scalar")
            if not is_mixed:
                m_z = p.mark("scalar")
                zrow = cb - 64
                p.dma("sync", zs_o[zrow * 128:(zrow + 1) * 128, :], zsb[cb % 2][:, :],
                      ch=f"zs{cb % 2}", deps=[m_z])
                zsb_free[cb % 2] = p.mark(f"zs{cb % 2}")
                continue
            m_mixed = p.mark("vector")
            cwv = cw[:, cb * 4:(cb + 1) * 4].unsqueeze(2).broadcast_to([128, 4, 128])
            identv = identf[:, :].unsqueeze(1).broadcast_to([128, 4, 128])
            dg = diag4[cb % 2]
            p.op("vector", lambda E, dg=dg, identv=identv, cwv=cwv: E.tensor_tensor(
                dg[:, :].rearrange("p (a b) -> p a b", a=4), identv, cwv, op=ALU.mult),
                deps=[m_cst, diag_free[cb % 2]])
            m_dg = p.mark("vector")
            cvt = conv2[cb % 2]
            for t2 in range(2):
                for i4 in range(4):
                    dd = [m_dg, m_mixed, cv_ps_free] if (t2 == 0 and i4 == 0) else []
                    p.op("tensor", lambda E, t2=t2, i4=i4, dg=dg: E.matmul(
                        ps_cv[:, t2 * 512:(t2 + 1) * 512], dg[:, i4 * 128:(i4 + 1) * 128],
                        mixed[:, 125 + t2 * 512 + i4: 125 + t2 * 512 + i4 + 512],
                        start=(i4 == 0), stop=(i4 == 3)),
                        deps=dd, inc=(1 if (t2 == 1 and i4 == 3) else 0))
            m_cv = p.mark("tensor")
            mixed_free = m_cv
            diag_free[cb % 2] = m_cv
            p.op("scalar", lambda E, cvt=cvt: E.activation(cvt[:, :], ps_cv[:, :], AF.Silu),
                 deps=[m_cv, conv_free[cb % 2]])
            m_cs = p.mark("scalar")
            cv_ps_free = m_cs
            if cb >= 32:
                p.dma("sync", qkv_o[cb * 128:(cb + 1) * 128, :], cvt[:, :],
                      ch=f"cv{cb % 2}", deps=[m_cs])
                conv_free[cb % 2] = p.mark(f"cv{cb % 2}")
            else:
                p.op("vector", lambda E, cb=cb, cvt=cvt: E.tensor_copy(
                    qkstore[:, cb * TPC:(cb + 1) * TPC], cvt[:, :]), deps=[m_cs])
                conv_free[cb % 2] = p.mark("vector")
                sq2 = sqb[cb % 2]
                p.op("scalar", lambda E, cvt=cvt, sq2=sq2: E.activation(
                    sq2[:, :], cvt[:, :], AF.Square), deps=[sqb_free[cb % 2]])
                m_sq3 = p.mark("scalar")
                for half in range(2):
                    p.op("tensor", lambda E, cb=cb, half=half, sq2=sq2: E.matmul(
                        ps_rows[:, half * 512:(half + 1) * 512],
                        colsel[:, cb * 32:(cb + 1) * 32],
                        sq2[:, half * 512:(half + 1) * 512],
                        start=(cb == 0), stop=(cb == 31)),
                        deps=([m_sq3, m_cst] if half == 0 else []),
                        inc=(1 if half == 1 else 0))
                sqb_free[cb % 2] = p.mark("tensor")
                nqk += 1
        wpan_free[cg % 2] = p.mark("tensor")
        if cg + 2 < NCG:
            _fetch_panel(cg + 2)

    m_rowsmm = p.mark("tensor")
    rows32v = p.sb("rows32v", [32, TPC], F32, sco_d)
    rows32t = p.sb("rows32t", [32, TPC], F32, sco_d)
    p.op("vector", lambda E: E.tensor_scalar(rows32v[:, :], ps_rows[:, :], 1e-6, None,
                                             op0=ALU.add), deps=[m_rowsmm])
    p.op("vector", lambda E: E.tensor_copy(rows32[:, :], ps_rows[:, :]))
    m_rows = p.mark("vector")
    p.op("scalar", lambda E: E.activation(rows32[:, :], rows32[:, :], AF.Ln, bias=1e-6),
         deps=[m_rows])
    p.op("scalar", lambda E: E.activation(rows32[:, :], rows32[:, :], AF.Exp, scale=-0.5))
    m_rstd = p.mark("scalar")
    _newton_rsqrt(p, rows32[:, :], rows32v[:, :], rows32t[:, :], [m_rstd])
    m_rstd = p.mark("vector")
    p.op("vector", lambda E: E.tensor_scalar(rows32[0:16, :], rows32[0:16, :],
                                             float(128.0 ** -0.5), None, op0=ALU.mult),
         deps=[m_rstd])
    m_rstd2 = p.mark("vector")
    p.dma("sync", rows_d[:, :], rows32[:, :], ch="rw", deps=[m_rstd2])
    m_rows_out = p.mark("rw")

    sco_ap = p.scope()
    flatr = p.sb("flatr", [1, 2 * TPC], F32, sco_ap)
    ps_bc = p.ps("psbc", [128, TPC], scope=sco_ap)
    bc_free = p.mark("vector")
    fl_free = p.mark("tensor")
    for grp in range(16):
        p.dma("sync", flatr[:, :],
              rows_d[grp * 2:(grp + 1) * 2, :].rearrange("a b -> (a b)").unsqueeze(0),
              ch="fl", deps=[m_rows_out, fl_free])
        m_fl = p.mark("fl")
        for jj in range(2):
            cb = grp * 2 + jj
            for half in range(2):
                dd = [m_fl, bc_free] if half == 0 else []
                p.op("tensor", lambda E, cb=cb, jj=jj, half=half: E.matmul(
                    ps_bc[:, half * 512:(half + 1) * 512], ones_f1[:, :],
                    flatr[0:1, jj * TPC + half * 512: jj * TPC + (half + 1) * 512],
                    start=True, stop=True),
                    deps=dd, inc=(1 if half == 1 else 0))
            m_bc = p.mark("tensor")
            if jj == 1:
                fl_free = m_bc
            cvt2 = conv2[cb % 2]
            p.op("vector", lambda E, cb=cb, cvt2=cvt2: E.tensor_tensor(
                cvt2[:, :], ps_bc[:, :], qkstore[:, cb * TPC:(cb + 1) * TPC],
                op=ALU.mult), deps=[m_bc, conv_free[cb % 2]])
            m_q = p.mark("vector")
            bc_free = m_q
            p.dma("sync", qkv_o[cb * 128:(cb + 1) * 128, :], cvt2[:, :],
                  ch=f"cv{cb % 2}", deps=[m_q])
            conv_free[cb % 2] = p.mark(f"cv{cb % 2}")

    sco_ba = p.scope()
    wbas = p.sb("wbas", [128, 16 * 64], BF16, sco_ba)
    dtb_bc = p.sb("dtb_bc", [128, 32], F32, sco_ba)
    nea_bc = p.sb("nea_bc", [128, 32], F32, sco_ba)
    ba_s = p.sb("ba_s", [128, 64], F32, sco_ba)
    gbl_s = [p.sb(f"gbl{i}", [128, 96], F32, sco_ba) for i in range(2)]
    scr32 = p.sb("scr32", [128, 32], F32, sco_ba)
    scr32b = p.sb("scr32b", [128, 32], F32, sco_ba)

    p.dma("gpsimd", wbas[:, :].rearrange("p (kc c) -> p kc c", kc=KD),
          wba[:, :].rearrange("(kc p) c -> p kc c", p=128), ch="wba")
    m_wba = p.mark("wba")
    p.op("scalar", lambda E: E.activation(small[0:1, 0:32], small[0:1, 0:32], AF.Exp),
         deps=[m_cst])
    m_sc1 = p.mark("scalar")
    p.op("vector", lambda E: E.tensor_scalar(small[0:1, 0:32], small[0:1, 0:32], -1.0,
                                             None, op0=ALU.mult), deps=[m_sc1])
    m_sm = p.mark("vector")
    p.op("tensor", lambda E: E.matmul(ps_bc[:, 0:32], ones_f1[:, :], small[0:1, 0:32],
                                      start=True, stop=True),
         deps=[m_sm, bc_free])
    p.op("tensor", lambda E: E.matmul(ps_bc[:, 32:64], ones_f1[:, :], small[0:1, 32:64],
                                      start=True, stop=True))
    m_bc2 = p.mark("tensor")
    p.op("vector", lambda E: E.tensor_copy(nea_bc[:, :], ps_bc[:, 0:32]), deps=[m_bc2])
    p.op("vector", lambda E: E.tensor_copy(dtb_bc[:, :], ps_bc[:, 32:64]))
    m_bcs = p.mark("vector")

    m_sca_d = p.mark("scalar")
    gbl_free = [p.mark("sync")] * 2
    ba_ps_free = [m_sca_d, p.mark("vector")]
    ba_war = p.mark("scalar")
    for tb in range(8):
        pm = ps_mm[0]
        for kc in range(KD):
            dd = ([m_wba, m_hfm, m_bcs] + ba_ps_free) if kc == 0 else []
            p.op("tensor", lambda E, tb=tb, kc=kc, pm=pm: E.matmul(
                pm[:, 0:64],
                hfm[:, kc * TH + 128 + tb * 128: kc * TH + 256 + tb * 128],
                wbas[:, kc * 64:(kc + 1) * 64],
                start=(kc == 0), stop=(kc == KD - 1)),
                deps=dd, inc=(1 if kc == KD - 1 else 0))
        m_ba = p.mark("tensor")
        p.op("vector", lambda E, pm=pm: E.tensor_copy(ba_s[:, :], pm[:, 0:64]),
             deps=[m_ba, ba_war])
        m_bas = p.mark("vector")
        ba_ps_free = [m_bas]
        gb = gbl_s[tb % 2]
        p.op("vector", lambda E: E.tensor_tensor(scr32[:, :], ba_s[:, 32:64],
                                                 dtb_bc[:, :], op=ALU.add), deps=[m_bas])
        m1 = p.mark("vector")
        p.op("scalar", lambda E: E.activation(scr32b[:, :], scr32[:, :], AF.Exp,
                                              scale=-1.0), deps=[m1])
        p.op("scalar", lambda E: E.activation(scr32b[:, :], scr32b[:, :], AF.Ln, bias=1.0))
        m2 = p.mark("scalar")
        p.op("vector", lambda E: E.tensor_tensor(scr32[:, :], scr32[:, :], scr32b[:, :],
                                                 op=ALU.add), deps=[m2])
        p.op("vector", lambda E, gb=gb: E.tensor_tensor(
            gb[:, 0:32], scr32[:, :], nea_bc[:, :], op=ALU.mult),
            deps=[gbl_free[tb % 2]])
        m3 = p.mark("vector")
        p.op("scalar", lambda E: E.activation(scr32b[:, :], ba_s[:, 0:32], AF.Exp,
                                              scale=-1.0), deps=[m3])
        p.op("scalar", lambda E: E.activation(scr32b[:, :], scr32b[:, :], AF.Ln, bias=1.0))
        m4 = p.mark("scalar")
        p.op("vector", lambda E, gb=gb: E.tensor_scalar(
            gb[:, 64:96], scr32b[:, :], -1.0, None, op0=ALU.mult), deps=[m4])
        m5 = p.mark("vector")
        p.op("scalar", lambda E, gb=gb: E.activation(gb[:, 32:64], gb[:, 64:96], AF.Exp),
             deps=[m5])
        m6 = p.mark("scalar")
        ba_war = m6
        p.dma("sync", gbl_o[tb * 128:(tb + 1) * 128, :], gb[:, :],
              ch=f"gbl{tb % 2}", deps=[m6])
        gbl_free[tb % 2] = p.mark(f"gbl{tb % 2}")

    sco_ba.close()
    sco_ap.close()
    sco_d.close()
    sco_c.close()

    barrier = p.allmarks()
    sco_e = p.scope()
    msb = p.sb("msb", [128, 44 * TPC], BF16, sco_e)
    ps_g = [p.ps(f"psg{i}", [128, 512], scope=sco_e) for i in range(2)]
    sco_g = p.scope()
    xnf2 = p.sb("xnf2_s", [128, KD * TPC], BF16, sco_g)
    wq2 = [p.sb(f"wq2{i}", [128, 16 * 512], BF16, sco_g) for i in range(4)]
    gu = [p.sb(f"gu{i}", [128, TPC], BF16, sco_g) for i in range(2)]

    p.dma("sync", xnf2[:, :].rearrange("p (kc t) -> p kc t", kc=KD),
          xnf_o[:, :].rearrange("(kc p) t -> p kc t", p=128),
          ch="xnf2", deps=barrier + [m_xnf_out])
    m_xnf2 = p.mark("xnf2")

    wq2_free = [barrier[4]] * 4
    gu_free = [p.mark("vector")] * 2
    psg_free = [p.mark("scalar")] * 2
    msb_marks = []

    def _fetch_gu(hg):
        bsel = hg % 2
        bg, bu = wq2[bsel * 2], wq2[bsel * 2 + 1]
        p.dma("gpsimd", bg[:, :].rearrange("p (kc h) -> p kc h", kc=KD),
              wgT[:, hg * 512:(hg + 1) * 512].rearrange("(kc p) h -> p kc h", p=128),
              ch=f"wg{bsel}",
              deps=[wq2_free[bsel * 2]] + (barrier if hg == 0 else []))
        mg = p.mark(f"wg{bsel}")
        p.dma("gpsimd", bu[:, :].rearrange("p (kc h) -> p kc h", kc=KD),
              wuT[:, hg * 512:(hg + 1) * 512].rearrange("(kc p) h -> p kc h", p=128),
              ch=f"wu{bsel}", deps=[wq2_free[bsel * 2 + 1]])
        mu = p.mark(f"wu{bsel}")
        return mg, mu

    pend = {0: _fetch_gu(0), 1: _fetch_gu(1)}
    for hg in range(11):
        bsel = hg % 2
        bg, bu = wq2[bsel * 2], wq2[bsel * 2 + 1]
        mg, mu = pend[hg]
        p.op("vector", lambda E, bg=bg: E.tensor_scalar(bg[:, :], bg[:, :], 128.0, None,
                                                        op0=ALU.subtract), deps=[mg])
        m_g128 = p.mark("vector")
        p.op("vector", lambda E, bu=bu: E.tensor_scalar(bu[:, :], bu[:, :], 128.0, None,
                                                        op0=ALU.subtract), deps=[mu])
        m_u128 = p.mark("vector")
        for j in range(4):
            hb = hg * 4 + j
            for which in range(2):
                wsrc = bg if which == 0 else bu
                mdep = m_g128 if which == 0 else m_u128
                pm = ps_g[which]
                gut = gu[which]
                for t2 in range(2):
                    for kc in range(KD):
                        dd = [mdep, m_xnf2, psg_free[which]] if kc == 0 else []
                        p.op("tensor", lambda E, kc=kc, j=j, t2=t2, wsrc=wsrc, pm=pm: E.matmul(
                            pm[:, 0:512],
                            wsrc[:, kc * 512 + j * 128: kc * 512 + (j + 1) * 128],
                            xnf2[:, kc * TPC + t2 * 512: kc * TPC + (t2 + 1) * 512],
                            start=(kc == 0), stop=(kc == KD - 1)),
                            deps=dd, inc=(1 if kc == KD - 1 else 0))
                    m_pm = p.mark("tensor")
                    if which == 0:
                        p.op("scalar", lambda E, hb=hb, t2=t2, pm=pm, gut=gut: E.activation(
                            gut[:, t2 * 512:(t2 + 1) * 512], pm[:, 0:512], AF.Silu,
                            scale=sgs[:, hb:hb + 1]),
                            deps=[m_pm] + ([gu_free[0]] if t2 == 0 else []))
                    else:
                        p.op("scalar", lambda E, hb=hb, t2=t2, pm=pm, gut=gut: E.activation(
                            gut[:, t2 * 512:(t2 + 1) * 512], pm[:, 0:512], AF.Copy,
                            scale=sus[:, hb:hb + 1]),
                            deps=[m_pm] + ([gu_free[1]] if t2 == 0 else []))
                    psg_free[which] = p.mark("scalar")
            m_gu = p.mark("scalar")
            p.op("vector", lambda E, hb=hb: E.tensor_tensor(
                msb[:, hb * TPC:(hb + 1) * TPC], gu[0][:, :], gu[1][:, :], op=ALU.mult),
                deps=[m_gu])
            gu_free[0] = gu_free[1] = p.mark("vector")
            msb_marks.append(p.mark("vector"))
        m_hg = p.mark("tensor")
        wq2_free[bsel * 2] = wq2_free[bsel * 2 + 1] = m_hg
        if hg + 2 < 11:
            pend[hg + 2] = _fetch_gu(hg + 2)
    m_msb = msb_marks[-1]
    sco_g.close()

    sco_dn = p.scope()
    wdp = [p.sb(f"wdp{i}", [128, 44 * 512], BF16, sco_dn) for i in range(2)]
    f0s = [p.sb(f"f0s{i}", [128, TPC], F32, sco_dn) for i in range(2)]
    f0_free = [p.mark("sync")] * 2
    wdp_free = [p.mark("tensor")] * 2
    psd_free = [p.mark("vector")] * 2
    m_wdd = {}

    def _fetch_wd(og):
        b = wdp[og % 2]
        p.dma("gpsimd", b[:, :].rearrange("p (kc o) -> p kc o", kc=44),
              wdT[:, og * 512:(og + 1) * 512].rearrange("(kc p) o -> p kc o", p=128),
              ch=f"wd{og % 2}",
              deps=[wdp_free[og % 2]] + (barrier if og == 0 else []))
        m_wdd[og] = p.mark(f"wd{og % 2}")

    _fetch_wd(0)
    _fetch_wd(1)
    for og in range(4):
        wb = wdp[og % 2]
        p.op("vector", lambda E, wb=wb: E.tensor_scalar(
            wb[:, :], wb[:, :], 128.0, None, op0=ALU.subtract),
            deps=[m_wdd[og]])
        m_wd2 = p.mark("vector")
        for j in range(4):
            ob = og * 4 + j
            pm = ps_g[j % 2]
            f0t = f0s[ob % 2]
            for t2 in range(2):
                for kc in range(44):
                    dd = [m_wd2, m_msb, psd_free[j % 2]] if kc == 0 else []
                    p.op("tensor", lambda E, kc=kc, j=j, t2=t2, pm=pm, wb=wb: E.matmul(
                        pm[:, 0:512],
                        wb[:, kc * 512 + j * 128: kc * 512 + (j + 1) * 128],
                        msb[:, kc * TPC + t2 * 512: kc * TPC + (t2 + 1) * 512],
                        start=(kc == 0), stop=(kc == 43)),
                        deps=dd, inc=(1 if kc == 43 else 0))
                m_dn = p.mark("tensor")
                p.op("vector", lambda E, ob=ob, t2=t2, pm=pm, f0t=f0t: E.tensor_scalar(
                    f0t[:, t2 * 512:(t2 + 1) * 512], pm[:, 0:512], sds[:, ob:ob + 1],
                    None, op0=ALU.mult),
                    deps=[m_dn] + ([f0_free[ob % 2]] if t2 == 0 else []))
                psd_free[j % 2] = p.mark("vector")
            m_f0 = p.mark("vector")
            p.dma("sync", f0_o[ob * 128:(ob + 1) * 128, :], f0s[ob % 2][:, :],
                  ch=f"f0{ob % 2}", deps=[m_f0])
            f0_free[ob % 2] = p.mark(f"f0{ob % 2}")
        wdp_free[og % 2] = p.mark("tensor")
        if og + 2 < 4:
            _fetch_wd(og + 2)

    sco_dn.close()
    sco_e.close()
    p.wait("sync", p.allmarks())
    return p.build()




L2_NCH = 64
L2_STAGE = 9
L2_INVL = 6
L2_BARRIER = False
L2_SKIPEV = False


def build_L2():
    p = P("L2")
    NCH = L2_NCH
    NG = max(1, NCH // 4)
    qf = p.din("qf", [256, L], BF16)
    kf = p.din("kf", [256, L], BF16)
    vf = p.din("vf", [512, L], BF16)
    gbp = p.din("gbp", [128, 64 * 12], F32)
    identb_i = p.din("identb", [128, 128], BF16)
    identf_i = p.din("identf32", [128, 128], F32)
    onesf_i = p.din("onesf", [1, 128], F32)
    onest_i = p.din("onest", [128, 128], F32)
    mTs_i = p.din("mTs", [128, 128], F32)
    mTi_i = p.din("mTi", [128, 128], F32)
    mLs_i = p.din("mLs", [128, 128], F32)
    e4n_i = p.din("e4n", [4, 512], F32)
    e4p_i = p.din("e4p", [4, 512], F32)
    onesfn_i = p.din("onesfn", [1, 128], F32)
    cbank_i = p.din("cbank", [128, 16], F32)
    cno_o = p.dout("cno", [512, L], BF16)
    rows_d = p.dtmp("rows_d2", [2, 16, 128], F32)

    for v in L1_CONSTS:
        p.constf(v)

    ident = p.sb("ident", [128, 128], BF16)
    identf = p.sb("identf", [128, 128], F32)
    onesf = p.sb("onesf_s", [1, 128], F32)
    onest = p.sb("onest_s", [128, 128], F32)
    mTs = p.sb("mTs_s", [128, 128], F32)
    mTi = p.sb("mTi_s", [128, 128], F32)
    mLs = p.sb("mLs_s", [128, 128], F32)
    gb = p.sb("gb_s", [128, 64 * 12], F32)
    n_sb = p.sb("n_sb", [128, 512], F32)
    n_bf = p.sb("n_bf", [128, 512], BF16)
    cno_sb = p.sb("cno_sb", [128, 4 * L], BF16)
    qf4 = [p.sb(f"qf4{i}", [128, 1024], BF16) for i in range(2)]
    kf4 = [p.sb(f"kf4{i}", [128, 1024], BF16) for i in range(2)]
    vf4 = [p.sb(f"vf4{i}", [128, 2048], BF16) for i in range(2)]
    rowb = p.sb("rowb", [48, 128], F32)
    lnbr = p.sb("lnbr", [16, 128], F32)
    brow = p.sb("brow", [16, 128], F32)
    bpl = p.sb("bpl", [16, 128], F32)
    bf_f = p.sb("bf_f", [1, 2048], F32)
    bplf = p.sb("bplf", [1, 2048], F32)
    bs4g = p.sb("bs4g", [4, 512], F32)
    bpl4g = p.sb("bpl4g", [4, 512], F32)
    e4n = p.sb("e4n_s", [4, 512], F32)
    e4p = p.sb("e4p_s", [4, 512], F32)
    onesfn = p.sb("onesfn_s", [1, 128], F32)
    ebf = p.sb("ebf", [1, 2048], F32)
    gcol = p.sb("gcol", [128, 64], F32)
    ebendb = p.sb("ebendb", [128, 16], F32)
    ebmb = p.sb("ebmb", [128, 16], F32)
    kkn = p.sb("kkn", [128, 256], BF16)
    qks = p.sb("qks", [128, 256], BF16)
    kvtm = p.sb("kvtm", [128, 768], BF16)
    dscr = p.sb("dscr", [128, 512], F32)
    expb = p.sb("expb", [128, 512], BF16)
    TAp = [p.sb(f"TA{i}", [128, 512], BF16) for i in range(2)]
    TBp = [p.sb(f"TB{i}", [128, 512], BF16) for i in range(2)]
    Aqs = p.sb("Aqs", [128, 512], BF16)
    Ysb = p.sb("Ysb", [128, 512], BF16)
    Tt = p.sb("Tt", [128, 512], BF16)
    khat = p.sb("khat", [128, 512], BF16)
    qhat = p.sb("qhat", [128, 512], BF16)
    tmpf = p.sb("tmpf", [128, 512], F32)
    x1s = p.sb("x1s", [128, 512], BF16)
    usb = p.sb("usb", [128, 512], BF16)
    sqo = p.sb("sqo", [128, 512], BF16)
    rms4 = p.sb("rms4", [128, 8], F32)
    kbar = p.sb("kbar", [128, 512], BF16)
    nsc = p.sb("nsc", [128, 512], F32)
    corn = p.sb("corn", [128, 512], BF16)
    psA = p.ps("psA", [128, 512])
    psT = p.ps("psT", [128, 768], BF16)
    psB = p.ps("psB", [128, 512])
    psQ1 = p.ps("psQ1", [128, 512])
    psQ2 = p.ps("psQ2", [128, 512])
    psY = p.ps("psY", [128, 512])
    psS = p.ps("psS", [128, 128])

    p.dma("sync", ident[:, :], identb_i[:, :], ch="cst")
    p.dma("sync", identf[:, :], identf_i[:, :], ch="cst")
    p.dma("sync", onesf[:, :], onesf_i[:, :], ch="cst")
    p.dma("sync", onest[:, :], onest_i[:, :], ch="cst")
    p.dma("sync", mTs[:, :], mTs_i[:, :], ch="cst")
    p.dma("sync", mTi[:, :], mTi_i[:, :], ch="cst")
    p.dma("sync", mLs[:, :], mLs_i[:, :], ch="cst")
    p.dma("sync", e4n[:, :], e4n_i[:, :], ch="cst")
    p.dma("sync", e4p[:, :], e4p_i[:, :], ch="cst")
    p.dma("sync", onesfn[:, :], onesfn_i[:, :], ch="cst")
    p.dma("sync", gb[:, :], gbp[:, :], ch="cst")
    p.dma("sync", p._cbank[:, :], cbank_i[:, :], ch="cst")
    m_cst = p.mark("cst")
    p.const_barrier = m_cst
    p.op("vector", lambda E: E.memset(n_sb[:, :], 0.0))
    p.op("vector", lambda E: E.memset(n_bf[:, :], 0.0))

    def rep4(ap):
        return ap.unsqueeze(2).broadcast_to([128, 4, 128])

    def out3(ap):
        return ap.rearrange("p (a b) -> p a b", a=4)

    def bcast_mask(m):
        return m[:, :].unsqueeze(1).broadcast_to([128, 4, 128])

    def v4(ap):
        return ap[:, :].rearrange("p (j r d) -> p j r d", j=2, r=2)

    def khrep(ap256):
        return ap256.rearrange("p (j d) -> p j d", j=2).unsqueeze(2) \
                    .broadcast_to([128, 2, 2, 128])

    def kchunk_view(buf, cc):
        return buf[:, :].rearrange("p (j q) -> p j q", j=2)[:, :, cc * 128:(cc + 1) * 128] \
                        .unsqueeze(2).broadcast_to([128, 2, 2, 128])

    grp_dma = {}
    buf_free = [[p.mark("tensor")], [p.mark("tensor")]]

    def stage1(g):
        bsel = g % 2
        dd = list(buf_free[bsel])
        for j in range(2):
            p.dma("sync", qf4[bsel][:, j * 512:(j + 1) * 512],
                  qf[j * 128:(j + 1) * 128, g * 512:(g + 1) * 512],
                  ch=f"grp{bsel}", deps=dd)
            p.dma("sync", kf4[bsel][:, j * 512:(j + 1) * 512],
                  kf[j * 128:(j + 1) * 128, g * 512:(g + 1) * 512],
                  ch=f"grp{bsel}", deps=dd)
        for h in range(4):
            p.dma("sync", vf4[bsel][:, h * 512:(h + 1) * 512],
                  vf[h * 128:(h + 1) * 128, g * 512:(g + 1) * 512],
                  ch=f"grp{bsel}", deps=dd)
        grp_dma[g] = p.mark(f"grp{bsel}")

    m_g1 = {}

    def stage1b(g, war):
        dd = [m_cst] + war
        p.op("tensor", lambda E, g=g: E.transpose(
            psS[0:48, 0:128], gb[:, g * 48:(g + 1) * 48], identf[:, :]), deps=dd)
        m_t = p.mark("tensor")
        p.op("vector", lambda E: E.tensor_copy(rowb[:, :], psS[0:48, 0:128]),
             deps=[m_t] + war)
        p.op("vector", lambda E: E.tensor_copy(lnbr[:, :], psS[32:48, 0:128]),
             deps=[m_t])
        p.op("vector", lambda E: E.tensor_tensor_scan(
            brow[:, :], onest[0:16, :], rowb[0:16, :], 0.0, op0=ALU.mult, op1=ALU.add))
        p.op("vector", lambda E: E.tensor_tensor(bpl[:, :], brow[:, :], lnbr[:, :],
                                                 op=ALU.add))
        m_rows = p.mark("vector")
        p.dma("sync", rows_d[0, :, :], brow[:, :], ch="rw", deps=[m_rows] + war)
        p.dma("sync", rows_d[1, :, :], bpl[:, :], ch="rw", deps=[m_rows])
        m_w = p.mark("rw")
        p.dma("sync", bf_f[:, :], rows_d[0:1, :, :].rearrange("a b c -> a (b c)"),
              ch="fl", deps=[m_w] + war)
        p.dma("sync", bplf[:, :], rows_d[1:2, :, :].rearrange("a b c -> a (b c)"),
              ch="fl", deps=[m_w])
        p.dma("sync", bs4g[:, :].rearrange("p (cc s) -> p cc s", cc=4),
              rows_d[0, :, :].rearrange("(cc h) s -> h cc s", h=4),
              ch="fl", deps=[m_w])
        p.dma("sync", bpl4g[:, :].rearrange("p (cc s) -> p cc s", cc=4),
              rows_d[1, :, :].rearrange("(cc h) s -> h cc s", h=4),
              ch="fl", deps=[m_w])
        m_fl = p.mark("fl")
        p.op("scalar", lambda E: E.activation(ebf[:, :], bf_f[:, :], AF.Exp),
             deps=[m_fl] + war)
        m_ebf = p.mark("scalar")
        p.op("tensor", lambda E: E.transpose(psS[:, 0:16], brow[:, :], identf[0:16, 0:16]),
             deps=[m_rows])
        m_t2 = p.mark("tensor")
        p.op("vector", lambda E: E.tensor_copy(gcol[:, 0:16], psS[:, 0:16]),
             deps=[m_t2])
        m_v1 = p.mark("vector")
        p.op("tensor", lambda E: E.matmul(psS[:, 32:48], onesf[:, :],
                                          bf_f[0:1, 127::128], start=True, stop=True),
             deps=[m_fl, m_v1])
        m_t4 = p.mark("tensor")
        p.op("vector", lambda E: E.tensor_copy(gcol[:, 32:48], psS[:, 32:48]),
             deps=[m_t4])
        m_v3 = p.mark("vector")
        p.op("scalar", lambda E: E.activation(ebendb[:, :], gcol[:, 32:48], AF.Exp),
             deps=[m_v3] + war)
        p.op("vector", lambda E: E.tensor_tensor(gcol[:, 48:64], gcol[:, 32:48],
                                                 gcol[:, 0:16], op=ALU.subtract))
        m_v4 = p.mark("vector")
        p.op("scalar", lambda E: E.activation(ebmb[:, :], gcol[:, 48:64], AF.Exp),
             deps=[m_v4])
        m_g1[g] = (p.mark("vector"), p.mark("scalar"), p.mark("tensor"))

    stage1(0)
    if NG > 1:
        stage1(1)
    stage1b(0, [])

    war_psA = p.mark("scalar")
    war_psT = p.mark("scalar")
    war_psB = p.mark("vector")
    war_psQa = p.mark("vector")
    war_psQb = p.mark("vector")
    war_psY = p.mark("scalar")
    war_kkn = p.mark("tensor")
    war_kvtm = p.mark("tensor")
    war_dscr = p.mark("scalar")
    war_expb = p.mark("vector")
    war_TA = p.mark("tensor")
    war_TB = p.mark("tensor")
    war_Aqs = p.mark("tensor")
    war_Y = p.mark("tensor")
    war_Tt = p.mark("tensor")
    war_khat = p.mark("tensor")
    war_usb = p.mark("tensor")
    war_sqo = p.mark("vector")
    war_corn = p.mark("tensor")
    war_n = p.mark("vector")
    war_x1 = p.mark("tensor")
    war_nsc = p.mark("tensor")
    war_kbar = p.mark("tensor")

    for c in range(NCH):
        g, cc = c // 4, c % 4
        bsel = g % 2
        mv, ms, mt = m_g1[g]
        mdma = grp_dma[g]

        def ksl(j, cc=cc, bsel=bsel):
            return kf4[bsel][:, j * 512 + cc * 128: j * 512 + cc * 128 + 128]

        def qsl(j, cc=cc, bsel=bsel):
            return qf4[bsel][:, j * 512 + cc * 128: j * 512 + cc * 128 + 128]

        def vsl(h, cc=cc, bsel=bsel):
            return vf4[bsel][:, h * 512 + cc * 128: h * 512 + cc * 128 + 128]

        bcol = gcol[:, cc * 4:cc * 4 + 4]
        bplcol = gcol[:, 16 + cc * 4: 16 + cc * 4 + 4]
        betacol = gb[:, g * 48 + 16 + cc * 4: g * 48 + 16 + cc * 4 + 4]
        ebendc = ebendb[:, cc * 4:cc * 4 + 4]
        ebmbc = ebmb[:, cc * 4:cc * 4 + 4]

        for j in range(2):
            p.op("tensor", lambda E, j=j, ksl=ksl: E.matmul(
                psA[:, j * 128:(j + 1) * 128], ksl(j), ksl(j), start=True, stop=True),
                deps=[mdma, war_psA] if j == 0 else [], inc=0)
            p.op("tensor", lambda E, j=j, ksl=ksl, qsl=qsl: E.matmul(
                psA[:, 256 + j * 128: 256 + (j + 1) * 128], ksl(j), qsl(j),
                start=True, stop=True), inc=(1 if j == 1 else 0))
        m_kq = p.mark("tensor")
        p.op("scalar", lambda E: E.activation(kkn[:, :], psA[:, 0:256], AF.Copy,
                                              scale=-1.0), deps=[m_kq, war_kkn])
        p.op("scalar", lambda E: E.activation(qks[:, :], psA[:, 256:512], AF.Copy))
        m_kkn = p.mark("scalar")
        war_psA = m_kkn
        war_kkn = None

        for j in range(2):
            p.op("tensor", lambda E, j=j, ksl=ksl: E.transpose(
                psT[:, j * 128:(j + 1) * 128], ksl(j), ident[:, :]),
                deps=[war_psT, mdma] if j == 0 else [], inc=0)
        for h in range(4):
            p.op("tensor", lambda E, h=h, vsl=vsl: E.transpose(
                psT[:, 256 + h * 128: 256 + (h + 1) * 128], vsl(h), ident[:, :]),
                inc=(1 if h == 3 else 0))
        m_tv = p.mark("tensor")
        p.op("scalar", lambda E: E.activation(kvtm[:, :], psT[:, :], AF.Copy),
             deps=[m_tv, war_kvtm])
        m_kv = p.mark("scalar")
        war_psT = m_kv
        war_kvtm = None
        ktm_view = khrep(kvtm[:, 0:256])
        vtm = kvtm[:, 256:768]
        if L2_STAGE <= 1:
            continue

        bs4c = bs4g[0:4, cc * 128:(cc + 1) * 128]
        bpl4c = bpl4g[0:4, cc * 128:(cc + 1) * 128]
        p.op("tensor", lambda E, cc=cc: E.matmul(
            psB[:, :], onesf[:, :], bplf[0:1, cc * 512:(cc + 1) * 512],
            start=True, stop=False), deps=[war_psB, mv, ms], inc=0)
        p.op("tensor", lambda E, bs4c=bs4c: E.matmul(
            psB[:, :], bs4c, e4n[:, :], start=False, stop=True), inc=1)
        m_b1 = p.mark("tensor")
        p.op("vector", lambda E: E.tensor_tensor(
            out3(dscr), out3(psB), bcast_mask(mTs), op=ALU.subtract),
            deps=[m_b1, war_dscr])
        m_d1 = p.mark("vector")
        war_psB = m_d1
        p.op("scalar", lambda E: E.activation(expb[:, :], dscr[:, :], AF.Exp),
             deps=[m_d1, war_expb])
        m_e1 = p.mark("scalar")
        p.op("vector", lambda E: E.tensor_tensor(
            v4(TAp[0]), v4(expb), khrep(kkn[:, :]), op=ALU.mult),
            deps=[m_e1, m_kkn, war_TA])
        m_TAb = p.mark("vector")
        p.op("tensor", lambda E, cc=cc: E.matmul(
            psB[:, :], onesfn[:, :], bf_f[0:1, cc * 512:(cc + 1) * 512],
            start=True, stop=False), deps=[war_psB], inc=0)
        p.op("tensor", lambda E, bpl4c=bpl4c: E.matmul(
            psB[:, :], bpl4c, e4p[:, :], start=False, stop=True), inc=1)
        m_b2 = p.mark("tensor")
        p.op("vector", lambda E: E.tensor_tensor(
            out3(dscr), out3(psB), bcast_mask(mLs), op=ALU.subtract),
            deps=[m_b2, m_e1])
        m_d2 = p.mark("vector")
        war_psB = m_d2
        p.op("scalar", lambda E: E.activation(expb[:, :], dscr[:, :], AF.Exp),
             deps=[m_d2, m_TAb])
        m_e2 = p.mark("scalar")
        p.op("vector", lambda E: E.tensor_tensor(
            v4(TBp[0]), v4(expb), khrep(kkn[:, :]), op=ALU.mult),
            deps=[m_e2, war_TB])
        m_TA0 = p.mark("vector")
        p.op("tensor", lambda E, cc=cc: E.matmul(
            psB[:, :], onesf[:, :], bf_f[0:1, cc * 512:(cc + 1) * 512],
            start=True, stop=False), deps=[war_psB], inc=0)
        p.op("tensor", lambda E, bs4c=bs4c: E.matmul(
            psB[:, :], bs4c, e4n[:, :], start=False, stop=True), inc=1)
        m_b3 = p.mark("tensor")
        p.op("vector", lambda E: E.tensor_tensor(
            out3(dscr), out3(psB), bcast_mask(mTi), op=ALU.subtract),
            deps=[m_b3, m_e2])
        m_d3 = p.mark("vector")
        war_psB = m_d3
        p.op("scalar", lambda E: E.activation(expb[:, :], dscr[:, :], AF.Exp),
             deps=[m_d3, m_TA0])
        m_e3 = p.mark("scalar")
        war_dscr = m_e3
        p.op("vector", lambda E: E.tensor_tensor(
            v4(Aqs), v4(expb), khrep(qks[:, :]), op=ALU.mult),
            deps=[m_e3, war_Aqs])
        m_Aq = p.mark("vector")
        war_expb = m_Aq
        if L2_STAGE <= 2:
            continue

        p.op("vector", lambda E: E.tensor_tensor(
            out3(Ysb[:, :]), out3(TBp[0][:, :]), bcast_mask(ident), op=ALU.add),
            deps=[m_TA0, war_Y])
        m_y = p.mark("vector")
        if L2_STAGE == 25:
            continue
        cur = 0
        for lvl in range(1, 1 + L2_INVL):
            nxt = 1 - cur
            dd = ([m_TAb, m_TA0, war_psQa, war_psQb] if lvl == 1
                  else [m_cp, m_cps])
            if lvl < L2_INVL and L2_STAGE not in (31, 33):
                for h in range(4):
                    sl = slice(h * 128, (h + 1) * 128)
                    p.op("tensor", lambda E, sl=sl, cur=cur: E.matmul(
                        psQ1[:, sl], TAp[cur][:, sl], TBp[cur][:, sl],
                        start=True, stop=True), deps=dd, inc=0)
                    dd = []
            for h in range(4):
                sl = slice(h * 128, (h + 1) * 128)
                p.op("tensor", lambda E, sl=sl, cur=cur: E.matmul(
                    psQ2[:, sl], TBp[cur][:, sl], TAp[cur][:, sl],
                    start=True, stop=True), deps=dd, inc=(1 if h == 3 else 0))
                dd = []
            m_sq = p.mark("tensor")
            for h in range(4):
                sl = slice(h * 128, (h + 1) * 128)
                p.op("tensor", lambda E, sl=sl, cur=cur: E.matmul(
                    psY[:, sl], TAp[cur][:, sl], Ysb[:, sl], start=True, stop=True),
                    deps=[m_y, war_psY] if h == 0 else [], inc=(1 if h == 3 else 0))
            m_ap = p.mark("tensor")
            if lvl < L2_INVL and L2_STAGE not in (31, 32):
                p.op("scalar", lambda E, nxt=nxt: E.activation(
                    TBp[nxt][:, :], psQ1[:, :] if L2_STAGE != 33 else psQ2[:, :],
                    AF.Copy), deps=[m_sq])
            m_cps = p.mark("scalar")
            p.op("vector", lambda E, nxt=nxt: E.tensor_copy(TAp[nxt][:, :],
                                                            psQ2[:, :]),
                 deps=[m_sq])
            m_cp = p.mark("vector")
            war_psQa = m_cps
            war_psQb = m_cp
            p.op("vector", lambda E: E.tensor_tensor(Ysb[:, :], psY[:, :], Ysb[:, :],
                                                     op=ALU.add), deps=[m_ap])
            m_y = p.mark("vector")
            war_psY = m_y
            cur = nxt
        war_TA = m_y
        war_TB = m_y
        for h in range(4):
            p.op("tensor", lambda E, h=h: E.transpose(
                psT[:, h * 128:(h + 1) * 128], Ysb[:, h * 128:(h + 1) * 128],
                ident[:, :]), deps=[m_y, m_kv] if h == 0 else [],
                inc=(1 if h == 3 else 0))
        m_tt = p.mark("tensor")
        war_Y = m_tt
        p.op("scalar", lambda E: E.activation(Tt[:, :], psT[:, 0:512], AF.Copy),
             deps=[m_tt, war_Tt])
        m_Tt = p.mark("scalar")
        war_psT = m_Tt
        if L2_STAGE <= 3 or L2_STAGE in (31, 32, 33):
            continue

        p.op("tensor", lambda E, cc=cc: E.matmul(
            psQ1[:, :], onesf[:, :], ebf[0:1, cc * 512:(cc + 1) * 512],
            start=True, stop=True), deps=[war_psQa, ms], inc=1)
        m_eb = p.mark("tensor")
        p.op("vector", lambda E, bsel=bsel, cc=cc: E.tensor_tensor(
            v4(khat), v4(psQ1), kchunk_view(kf4[bsel], cc), op=ALU.mult),
            deps=[m_eb, war_khat, mdma])
        p.op("vector", lambda E, bsel=bsel, cc=cc: E.tensor_tensor(
            v4(qhat), v4(psQ1), kchunk_view(qf4[bsel], cc), op=ALU.mult))
        m_kq2 = p.mark("vector")
        war_psQa = m_kq2

        for h in range(4):
            sl = slice(h * 128, (h + 1) * 128)
            p.op("tensor", lambda E, sl=sl: E.matmul(
                psQ2[:, sl], khat[:, sl], n_bf[:, sl],
                start=True, stop=True), deps=[m_kq2, war_n, war_psQb] if h == 0 else [],
                inc=(1 if h == 3 else 0))
        m_kn = p.mark("tensor")
        war_khat = m_kn
        p.op("vector", lambda E: E.tensor_tensor(tmpf[:, :], vtm, psQ2[:, :],
                                                 op=ALU.subtract), deps=[m_kn, m_kv])
        for h in range(4):
            p.op("vector", lambda E, h=h, betacol=betacol: E.tensor_scalar(
                x1s[:, h * 128:(h + 1) * 128], tmpf[:, h * 128:(h + 1) * 128],
                betacol[:, h:h + 1], None, op0=ALU.mult),
                deps=[war_x1] if h == 0 else [])
        m_x1 = p.mark("vector")
        war_psQb = m_x1

        for h in range(4):
            sl = slice(h * 128, (h + 1) * 128)
            p.op("tensor", lambda E, sl=sl: E.matmul(
                psY[:, sl], Tt[:, sl], x1s[:, sl], start=True, stop=True),
                deps=[m_x1, m_Tt, war_psY] if h == 0 else [], inc=(1 if h == 3 else 0))
        m_u = p.mark("tensor")
        war_x1 = m_u
        war_Tt = m_u
        p.op("scalar", lambda E: E.activation(usb[:, :], psY[:, 0:512], AF.Copy),
             deps=[m_u, war_usb])
        m_us = p.mark("scalar")
        war_psY = m_us
        if L2_STAGE <= 4:
            continue

        for h in range(4):
            sl = slice(h * 128, (h + 1) * 128)
            p.op("tensor", lambda E, sl=sl: E.matmul(
                psB[:, sl], qhat[:, sl], n_bf[:, sl], start=True, stop=False),
                deps=[m_kq2, m_Aq, war_psB, war_n] if h == 0 else [], inc=0)
            p.op("tensor", lambda E, sl=sl: E.matmul(
                psB[:, sl], Aqs[:, sl], usb[:, sl], start=False, stop=True),
                deps=[m_us] if h == 0 else [], inc=(1 if h == 3 else 0))
        m_out = p.mark("tensor")
        war_Aqs = m_out
        p.op("scalar", lambda E: E.activation(sqo[:, :], psB[:, :], AF.Square),
             deps=[m_out, war_sqo])
        m_sq2 = p.mark("scalar")
        p.op("vector", lambda E: E.reduce_sum(rms4[:, 0:4], out3(sqo), axis=AX.X),
             deps=[m_sq2])
        p.op("vector", lambda E: E.tensor_scalar(rms4[:, 4:8], rms4[:, 0:4],
                                                 1.0 / 128.0, None, op0=ALU.mult))
        m_r4 = p.mark("vector")
        p.op("scalar", lambda E: E.activation(rms4[:, 0:4], rms4[:, 4:8], AF.Ln,
                                              bias=1e-6), deps=[m_r4])
        p.op("scalar", lambda E: E.activation(rms4[:, 4:8], rms4[:, 0:4], AF.Exp,
                                              scale=-0.5))
        m_rst = p.mark("scalar")
        for h in range(4):
            p.op("vector", lambda E, h=h: E.tensor_scalar(
                corn[:, h * 128:(h + 1) * 128], psB[:, h * 128:(h + 1) * 128],
                rms4[:, 4 + h:5 + h], None, op0=ALU.mult),
                deps=[m_rst, war_corn] if h == 0 else [])
        m_corn = p.mark("vector")
        war_psB = m_corn
        war_sqo = m_corn
        for h in range(4):
            p.op("tensor", lambda E, h=h: E.transpose(
                psT[:, h * 128:(h + 1) * 128], corn[:, h * 128:(h + 1) * 128],
                ident[:, :]), deps=[m_corn, war_psT] if h == 0 else [],
                inc=(1 if h == 3 else 0))
        m_ct = p.mark("tensor")
        war_corn = m_ct
        if c == 0 or not L2_SKIPEV:
            p.op("vector", lambda E, c=c: E.tensor_copy(
                cno_sb[:, :].rearrange("p (h t) -> p h t", h=4)[:, :, c * 128:(c + 1) * 128],
                psT[:, 0:512].rearrange("p (h t) -> p h t", h=4)), deps=[m_ct])
        m_cno = p.mark("vector")
        war_psT = m_cno
        if L2_STAGE <= 5:
            continue

        for h in range(4):
            p.op("vector", lambda E, h=h, ebendc=ebendc: E.tensor_scalar(
                nsc[:, h * 128:(h + 1) * 128], n_sb[:, h * 128:(h + 1) * 128],
                ebendc[:, h:h + 1], None, op0=ALU.mult),
                deps=[mv, ms, war_nsc] if h == 0 else [])
        m_nsc = p.mark("vector")
        for h in range(4):
            p.op("vector", lambda E, h=h, ebmbc=ebmbc: E.tensor_scalar(
                kbar[:, h * 128:(h + 1) * 128],
                kvtm[:, (h // 2) * 128:(h // 2) * 128 + 128],
                ebmbc[:, h:h + 1], None, op0=ALU.mult),
                deps=[m_kv, war_kbar] if h == 0 else [])
        m_kb = p.mark("vector")
        for h in range(4):
            sl = slice(h * 128, (h + 1) * 128)
            p.op("tensor", lambda E, sl=sl: E.matmul(
                psQ2[:, sl], kbar[:, sl], usb[:, sl],
                start=True, stop=True), deps=[m_kb, m_us, war_psQb] if h == 0 else [],
                inc=(1 if h == 3 else 0))
        m_n2 = p.mark("tensor")
        war_nsc = m_n2
        war_kbar = m_n2
        war_usb = m_n2
        p.op("vector", lambda E: E.tensor_tensor(n_sb[:, :], psQ2[:, :],
                                                 nsc[:, :], op=ALU.add),
             deps=[m_n2, m_nsc])
        p.op("vector", lambda E: E.tensor_copy(n_bf[:, :], n_sb[:, :]))
        war_n = p.mark("vector")
        war_psQb = war_n

        if L2_BARRIER:
            bm = p.allmarks()
            for e in ENGINES:
                p.wait(e, bm)
        if cc == 3:
            buf_free[bsel] = [p.mark("tensor"), p.mark("vector")]
            if g + 2 < NG:
                stage1(g + 2)
            if g + 1 < NG:
                stage1b(g + 1, p.allmarks())

    m_fin = p.mark("vector")
    for h in range(4):
        p.dma("sync", cno_o[h * 128:(h + 1) * 128, :],
              cno_sb[:, h * L:(h + 1) * L], ch="cno", deps=[m_fin])
    p.wait("sync", p.allmarks())
    return p.build()




def build_L3():
    p = P("L3")
    IH = 40
    x_in = p.din("xt", [TPC, D], F32)
    xnf_i = p.din("xnf_i", [D, TPC], BF16)
    zsf_i = p.din("zsf_i", [4096, TPC], BF16)
    cnf_i = p.din("cnf_i", [4096, TPC], BF16)
    f0_i = p.din("f0_i", [D, TPC], F32)
    wout = p.din("wout", [4096, D], F32)
    gnw_i = p.din("gnw", [128, 1], F32)
    ln2c = p.din("ln2c", [128, KD], F32)
    wg5 = p.din("wg5", [D, 5120], F32)
    wu5 = p.din("wu5", [D, 5120], F32)
    wd5 = p.din("wd5", [5120, D], F32)
    wqT_i = p.din("wqT", [D, D], F32)
    wkT_i = p.din("wkT", [D, D], F32)
    wvT_i = p.din("wvT", [D, D], F32)
    identb_i = p.din("identb", [128, 128], BF16)
    identf_i = p.din("identf32", [128, 128], F32)
    onesb_i = p.din("onesb", [128, 1], BF16)
    onesf_i = p.din("onesf", [1, 128], F32)
    cbank_i = p.din("cbank", [128, 16], F32)

    x1_o = p.dout("x1_o", [TPC, D], F32)
    xn1_o = p.dout("xn1f", [D, TPC], BF16)
    q_o = p.dout("qf_o", [D, TPC], BF16)
    k_o = p.dout("kf_o", [D, TPC], BF16)
    v_o = p.dout("vf_o", [D, TPC], BF16)
    m5_d = p.dtmp("m5_d", [5120, TPC], BF16)
    x1a_d = p.dtmp("x1a_d", [D, TPC], F32)

    for v in L1_CONSTS:
        p.constf(v)

    ident = p.sb("ident_s", [128, 128], BF16)
    identf = p.sb("identf_s", [128, 128], F32)
    ones_b = p.sb("ones_b", [128, 1], BF16)
    ones_f1 = p.sb("ones_f1", [1, 128], F32)
    gnw = p.sb("gnw_s", [128, 1], F32)
    ln2 = p.sb("ln2_s", [128, KD], F32)

    p.dma("sync", ident[:, :], identb_i[:, :], ch="cst")
    p.dma("sync", identf[:, :], identf_i[:, :], ch="cst")
    p.dma("sync", ones_b[:, :], onesb_i[:, :], ch="cst")
    p.dma("sync", ones_f1[:, :], onesf_i[:, :], ch="cst")
    p.dma("sync", gnw[:, :], gnw_i[:, :], ch="cst")
    p.dma("sync", ln2[:, :], ln2c[:, :], ch="cst")
    p.dma("sync", p._cbank[:, :], cbank_i[:, :], ch="cst")
    m_cst = p.mark("cst")
    p.const_barrier = m_cst

    h2f = p.sb("h2f", [128, KD * TPC], BF16)
    sco_h = p.scope()
    hfm = p.sb("hfm_s", [128, KD * TPC], BF16, sco_h)
    sco1 = p.scope()
    xnf = p.sb("xnf_s", [128, KD * TPC], BF16, sco1)
    zsf = p.sb("zsf_s", [128, 32 * TPC], BF16, sco1)
    cn2 = [p.sb(f"cn2{i}", [128, TPC], BF16, sco1) for i in range(2)]
    wpan = [p.sb(f"wop{i}", [128, 32 * 256], BF16, sco1) for i in range(2)]
    x1b = [p.sb(f"x1b{i}", [128, TPC], F32, sco1) for i in range(2)]
    psA = p.ps("psA", [128, TPC], scope=sco1)
    psB = p.ps("psB", [128, TPC], scope=sco1)

    p.dma("sync", xnf[:, :].rearrange("p (kc t) -> p kc t", kc=KD),
          xnf_i[:, :].rearrange("(kc p) t -> p kc t", p=128), ch="xnf")
    m_xnf = p.mark("xnf")
    p.dma("sync", zsf[:, :].rearrange("p (kc t) -> p kc t", kc=32),
          zsf_i[:, :].rearrange("(kc p) t -> p kc t", p=128), ch="zsf")
    m_zsf = p.mark("zsf")
    m_gated = [None] * 32
    cn_free = [(p.mark("vector"))] * 2
    for kc in range(32):
        p.dma("sync", cn2[kc % 2][:, :], cnf_i[kc * 128:(kc + 1) * 128, :],
              ch=f"cn{kc % 2}", deps=[cn_free[kc % 2]])
        m_cn = p.mark(f"cn{kc % 2}")
        p.op("vector", lambda E, kc=kc: E.scalar_tensor_tensor(
            zsf[:, kc * TPC:(kc + 1) * TPC], cn2[kc % 2][:, :], gnw[:, 0:1],
            zsf[:, kc * TPC:(kc + 1) * TPC], op0=ALU.mult, op1=ALU.mult),
            deps=[m_cn, m_zsf, m_cst])
        m_gated[kc] = p.mark("vector")
        cn_free[kc % 2] = m_gated[kc]
    m_gate_all = p.mark("vector")

    wo_free = [[p.mark("tensor")]] * 2
    m_wo = {}

    def fetch_wout(og):
        p.dma("gpsimd", wpan[og % 2][:, :].rearrange("p (kc c) -> p kc c", kc=32),
              wout[:, og * 256:(og + 1) * 256].rearrange("(kc p) c -> p kc c", p=128),
              ch=f"wo{og % 2}", deps=wo_free[og % 2])
        m_wo[og] = p.mark(f"wo{og % 2}")

    fetch_wout(0)
    fetch_wout(1)
    ps_free = [p.mark("vector")] * 2
    x1b_free = [p.mark("sync")] * 2
    for og in range(8):
        for j in range(2):
            ob = og * 2 + j
            pm = psA if ob % 2 == 0 else psB
            for t2 in range(2):
                for kc in range(32):
                    dd = []
                    if kc == 0 and t2 == 0:
                        dd = [m_wo[og], m_gate_all, ps_free[ob % 2]]
                    p.op("tensor", lambda E, kc=kc, og=og, j=j, t2=t2, pm=pm: E.matmul(
                        pm[:, t2 * 512:(t2 + 1) * 512],
                        wpan[og % 2][:, kc * 256 + j * 128: kc * 256 + (j + 1) * 128],
                        zsf[:, kc * TPC + t2 * 512: kc * TPC + (t2 + 1) * 512],
                        start=(kc == 0), stop=(kc == 31)),
                        deps=dd, inc=(1 if kc == 31 else 0))
            m_mm = p.mark("tensor")
            p.op("vector", lambda E, ob=ob, pm=pm: E.tensor_tensor(
                hfm[:, ob * TPC:(ob + 1) * TPC], pm[:, :],
                xnf[:, ob * TPC:(ob + 1) * TPC], op=ALU.add),
                deps=[m_mm, m_xnf])
            p.op("vector", lambda E, ob=ob, pm=pm: E.tensor_tensor(
                x1b[ob % 2][:, :], pm[:, :], xnf[:, ob * TPC:(ob + 1) * TPC],
                op=ALU.add), deps=[x1b_free[ob % 2]])
            m_ev = p.mark("vector")
            ps_free[ob % 2] = m_ev
            p.dma("sync", x1a_d[ob * 128:(ob + 1) * 128, :], x1b[ob % 2][:, :],
                  ch=f"x1b{ob % 2}", deps=[m_ev])
            x1b_free[ob % 2] = p.mark(f"x1b{ob % 2}")
        wo_free[og % 2] = [p.mark("tensor")]
        if og + 2 < 8:
            fetch_wout(og + 2)
    m_hfm = p.mark("vector")
    sco1.close()

    barrier2 = p.allmarks()
    sco2 = p.scope()
    sqs = [p.sb(f"sqs{i}", [128, TPC], BF16, sco2) for i in range(2)]
    rowt = p.sb("rowt", [1, 3 * TPC], F32, sco2)
    ps_row = p.ps("psrow", [1, TPC], scope=sco2)
    ps_bc = p.ps("psbc", [128, TPC], scope=sco2)
    sq_free = [p.mark("tensor")] * 2
    for kc in range(KD):
        p.op("scalar", lambda E, kc=kc: E.activation(
            sqs[kc % 2][:, :], hfm[:, kc * TPC:(kc + 1) * TPC], AF.Square),
            deps=[m_hfm, sq_free[kc % 2]] + (barrier2 if kc == 0 else []))
        m_sq = p.mark("scalar")
        for half in range(2):
            p.op("tensor", lambda E, kc=kc, half=half: E.matmul(
                ps_row[:, half * 512:(half + 1) * 512], ones_b[:, :],
                sqs[kc % 2][:, half * 512:(half + 1) * 512],
                start=(kc == 0), stop=(kc == KD - 1)),
                deps=[m_sq] if half == 0 else [], inc=(1 if half == 1 else 0))
        sq_free[kc % 2] = p.mark("tensor")
    m_ssq = p.mark("tensor")
    p.op("vector", lambda E: E.tensor_scalar(rowt[:, 0:TPC], ps_row[:, :],
                                             1.0 / D, 1e-6, op0=ALU.mult, op1=ALU.add),
         deps=[m_ssq])
    m_v = p.mark("vector")
    p.op("scalar", lambda E: E.activation(rowt[:, TPC:2 * TPC], rowt[:, 0:TPC],
                                          AF.Ln, bias=1e-6), deps=[m_v])
    p.op("scalar", lambda E: E.activation(rowt[:, TPC:2 * TPC], rowt[:, TPC:2 * TPC],
                                          AF.Exp, scale=-0.5))
    m_s = p.mark("scalar")
    _newton_rsqrt(p, rowt[:, TPC:2 * TPC], rowt[:, 0:TPC], rowt[:, 2 * TPC:], [m_s])
    m_rst = p.mark("vector")
    for half in range(2):
        p.op("tensor", lambda E, half=half: E.matmul(
            ps_bc[:, half * 512:(half + 1) * 512], ones_f1[:, :],
            rowt[0:1, TPC + half * 512: TPC + (half + 1) * 512],
            start=True, stop=True), deps=[m_rst, m_cst] if half == 0 else [],
            inc=(1 if half == 1 else 0))
    m_bc = p.mark("tensor")
    for kc in range(KD):
        p.op("vector", lambda E, kc=kc: E.scalar_tensor_tensor(
            h2f[:, kc * TPC:(kc + 1) * TPC], hfm[:, kc * TPC:(kc + 1) * TPC],
            ln2[:, kc:kc + 1], ps_bc[:, :], op0=ALU.mult, op1=ALU.mult),
            deps=[m_bc] if kc == 0 else [])
    m_h2 = p.mark("vector")
    sco2.close()
    sco_h.close()


    sco3 = p.scope()
    wq2 = [p.sb(f"wi{i}", [128, KD * 512], BF16, sco3) for i in range(4)]
    gu = [p.sb(f"gu{i}", [128, TPC], BF16, sco3) for i in range(2)]
    m5b = [p.sb(f"m5b{i}", [128, TPC], BF16, sco3) for i in range(2)]
    psg = [p.ps(f"psg{i}", [128, 512], scope=sco3) for i in range(2)]
    m5b_free = [p.mark("sync")] * 2
    wi_free = [[p.mark("tensor")]] * 2
    gu_free = [p.mark("vector")] * 2
    psg_free = [p.mark("scalar")] * 2
    m_wi = {}

    def fetch_wi(ig, which):
        w = wg5 if which == 0 else wu5
        bsl = (ig % 2) * 2 + which
        p.dma("gpsimd", wq2[bsl][:, :].rearrange("p (kc c) -> p kc c", kc=KD),
              w[:, ig * 512:(ig + 1) * 512].rearrange("(kc p) c -> p kc c", p=128),
              ch=f"wi{bsl}", deps=wi_free[bsl])
        m_wi[(ig, which)] = p.mark(f"wi{bsl}")

    wi_free = [[p.mark("tensor")] + p.allmarks()] * 4
    fetch_wi(0, 0)
    fetch_wi(0, 1)
    fetch_wi(1, 0)
    fetch_wi(1, 1)
    for ig in range(10):
        for j in range(4):
            ib = ig * 4 + j
            for which in range(2):
                pm = psg[which]
                for t2 in range(2):
                    for kc in range(KD):
                        dd = []
                        if kc == 0:
                            dd = [m_wi[(ig, which)], psg_free[which]]
                            if t2 == 0 and j == 0 and which == 0 and ig == 0:
                                dd.append(m_h2)
                        p.op("tensor", lambda E, kc=kc, j=j, t2=t2, pm=pm,
                             which=which, ig=ig: E.matmul(
                            pm[:, 0:512],
                            wq2[(ig % 2) * 2 + which][:, kc * 512 + j * 128:
                                                      kc * 512 + (j + 1) * 128],
                            h2f[:, kc * TPC + t2 * 512: kc * TPC + (t2 + 1) * 512],
                            start=(kc == 0), stop=(kc == KD - 1)),
                            deps=dd, inc=(1 if kc == KD - 1 else 0))
                    m_pm = p.mark("tensor")
                    if which == 0:
                        p.op("scalar", lambda E, t2=t2, pm=pm: E.activation(
                            gu[0][:, t2 * 512:(t2 + 1) * 512], pm[:, 0:512], AF.Silu),
                            deps=[m_pm] + ([gu_free[0]] if t2 == 0 else []))
                    else:
                        p.op("scalar", lambda E, t2=t2, pm=pm: E.activation(
                            gu[1][:, t2 * 512:(t2 + 1) * 512], pm[:, 0:512], AF.Copy),
                            deps=[m_pm] + ([gu_free[1]] if t2 == 0 else []))
                    psg_free[which] = p.mark("scalar")
            m_gu = p.mark("scalar")
            p.op("vector", lambda E, ib=ib: E.tensor_tensor(
                m5b[ib % 2][:, :], gu[0][:, :], gu[1][:, :],
                op=ALU.mult), deps=[m_gu, m5b_free[ib % 2]])
            gu_free[0] = gu_free[1] = p.mark("vector")
            p.dma("sync", m5_d[ib * 128:(ib + 1) * 128, :], m5b[ib % 2][:, :],
                  ch=f"m5{ib % 2}", deps=[p.mark("vector")])
            m5b_free[ib % 2] = p.mark(f"m5{ib % 2}")
        mt3 = p.mark("tensor")
        wi_free[(ig % 2) * 2] = [mt3]
        wi_free[(ig % 2) * 2 + 1] = [mt3]
        if ig + 2 < 10:
            fetch_wi(ig + 2, 0)
            fetch_wi(ig + 2, 1)
    m_m5 = p.mark("vector")
    sco3.close()

    barrier4 = p.allmarks()
    for _e in ENGINES:
        p.wait(_e, barrier4)
    sco4 = p.scope()
    wdp = [p.sb(f"wdp{i}", [128, IH * 512], BF16, sco4) for i in range(2)]
    m5h = p.sb("m5h", [128, IH * 512], BF16, sco4)
    f0b = [p.sb(f"f0b{i}", [128, 512], F32, sco4) for i in range(2)]
    x1ab = [p.sb(f"x1ab{i}", [128, 512], F32, sco4) for i in range(2)]
    x1s = [p.sb(f"x1s{i}", [128, 512], F32, sco4) for i in range(2)]
    psd = [p.ps(f"psd{i}", [128, 512], scope=sco4) for i in range(2)]
    x1f_d = p.dtmp("x1f_d", [D, TPC], F32)
    wd_free = [[p.mark("tensor")], [p.mark("tensor")]]
    m5h_free = [p.mark("tensor")]
    psd_free = [p.mark("vector")] * 2
    fb_free = [p.mark("vector")] * 2
    x1s_free = [p.mark("sync")] * 2
    m_wd5 = {}

    def _fetch_wd5(i):
        og5 = i % 4
        p.dma("gpsimd", wdp[i % 2][:, :].rearrange("p (kc c) -> p kc c", kc=IH),
              wd5[:, og5 * 512:(og5 + 1) * 512]
              .rearrange("(kc p) c -> p kc c", p=128),
              ch=f"wd{i % 2}", deps=wd_free[i % 2])
        m_wd5[i] = p.mark(f"wd{i % 2}")

    _fetch_wd5(0)
    _fetch_wd5(1)
    for t2 in range(2):
        p.dma("sync", m5h[:, :].rearrange("p (kc c) -> p kc c", kc=IH),
              m5_d[:, t2 * 512:(t2 + 1) * 512].rearrange("(kc p) c -> p kc c", p=128),
              ch="m5h", deps=m5h_free + [p.mark("m50"), p.mark("m51")])
        m_m5h = p.mark("m5h")
        for og in range(4):
            ii = t2 * 4 + og
            wdb = wdp[ii % 2]
            m_wd = m_wd5[ii]
            for j in range(4):
                ob = og * 4 + j
                pm = psd[j % 2]
                p.dma("sync", f0b[ob % 2][:, :],
                      f0_i[ob * 128:(ob + 1) * 128, t2 * 512:(t2 + 1) * 512],
                      ch=f"f0{ob % 2}", deps=[fb_free[ob % 2]])
                p.dma("sync", x1ab[ob % 2][:, :],
                      x1a_d[ob * 128:(ob + 1) * 128, t2 * 512:(t2 + 1) * 512],
                      ch=f"f0{ob % 2}")
                m_fb = p.mark(f"f0{ob % 2}")
                for kc in range(IH):
                    dd = [m_wd, m_m5h, psd_free[j % 2]] if kc == 0 else []
                    p.op("tensor", lambda E, kc=kc, j=j, pm=pm, wdb=wdb: E.matmul(
                        pm[:, 0:512],
                        wdb[:, kc * 512 + j * 128: kc * 512 + (j + 1) * 128],
                        m5h[:, kc * 512:(kc + 1) * 512],
                        start=(kc == 0), stop=(kc == IH - 1)),
                        deps=dd, inc=(1 if kc == IH - 1 else 0))
                m_dn = p.mark("tensor")
                p.op("vector", lambda E, ob=ob, pm=pm: E.tensor_tensor(
                    x1s[ob % 2][:, :], pm[:, 0:512], x1ab[ob % 2][:, :], op=ALU.add),
                    deps=[m_dn, m_fb, x1s_free[ob % 2]])
                p.op("vector", lambda E, ob=ob: E.tensor_tensor(
                    x1s[ob % 2][:, :], x1s[ob % 2][:, :], f0b[ob % 2][:, :],
                    op=ALU.add))
                m_x1 = p.mark("vector")
                psd_free[j % 2] = m_x1
                fb_free[ob % 2] = m_x1
                p.dma("sync", x1f_d[ob * 128:(ob + 1) * 128, t2 * 512:(t2 + 1) * 512],
                      x1s[ob % 2][:, :], ch=f"x1f{ob % 2}", deps=[m_x1])
                x1s_free[ob % 2] = p.mark(f"x1f{ob % 2}")
            wd_free[ii % 2] = [p.mark("tensor")]
            if ii + 2 < 8:
                _fetch_wd5(ii + 2)
        m5h_free = [p.mark("tensor")]
    m_x1f = [p.mark("x1f0"), p.mark("x1f1")]
    sco4.close()

    barrier5 = p.allmarks()
    xn1f = p.sb("xn1f_s", [128, KD * TPC], BF16)
    sco5 = p.scope()
    x1r = [p.sb(f"x1r{i}", [128, KD * 128], F32, sco5) for i in range(2)]
    xb = [p.sb(f"xb{i}", [128, D], F32, sco5) for i in range(2)]
    x1t = [p.sb(f"x1t{i}", [128, D], F32, sco5) for i in range(2)]
    xn1t = p.sb("xn1t", [128, D], BF16, sco5)
    sq5 = p.sb("sq5", [128, D], BF16, sco5)
    st5 = p.sb("st5", [128, 8], F32, sco5)
    ps5 = p.ps("ps5", [128, D // 2], scope=sco5)
    ps5b = p.ps("ps5b", [128, TPC], BF16, scope=sco5)
    x1r_free = [p.mark("tensor")] * 2
    xb_free = [p.mark("vector")] * 2
    x1t_free = [p.mark("sync")] * 2
    m_xn1_done = [None] * 8
    for it in range(8):
        p.dma("sync", x1r[it % 2][:, :].rearrange("p (kc t) -> p kc t", kc=KD),
              x1f_d[:, it * 128:(it + 1) * 128].rearrange("(kc p) t -> p kc t", p=128),
              ch=f"x1r{it % 2}", deps=[x1r_free[it % 2]] + (barrier5 + m_x1f if it < 2 else []))
        m_rl = p.mark(f"x1r{it % 2}")
        p.dma("sync", xb[it % 2][:, :], x_in[it * 128:(it + 1) * 128, :],
              ch=f"xb{it % 2}", deps=[xb_free[it % 2]])
        m_xb = p.mark(f"xb{it % 2}")
        for half in range(2):
            for kc in range(8):
                kk = half * 8 + kc
                dd = []
                if half == 0 and kc == 0:
                    dd = [m_rl, x1r_free[it % 2]]
                if half == 1 and kc == 0:
                    dd = [p.mark("vector")]
                p.op("tensor", lambda E, kk=kk, it=it, half=half, kc=kc: E.transpose(
                    ps5[:, kc * 128:(kc + 1) * 128], x1r[it % 2][:, kk * 128:(kk + 1) * 128],
                    identf[:, :]),
                    deps=dd, inc=(1 if kc == 7 else 0))
            m_tr = p.mark("tensor")
            p.op("vector", lambda E, it=it, half=half: E.tensor_tensor(
                x1t[it % 2][:, half * 1024:(half + 1) * 1024], ps5[:, :],
                xb[it % 2][:, half * 1024:(half + 1) * 1024], op=ALU.add),
                deps=[m_tr, m_xb] + ([x1t_free[it % 2]] if half == 0 else []))
        m_x1t = p.mark("vector")
        x1r_free[it % 2] = p.mark("tensor")
        p.dma("sync", x1_o[it * 128:(it + 1) * 128, :], x1t[it % 2][:, :],
              ch=f"x1o{it % 2}", deps=[m_x1t])
        x1t_free[it % 2] = p.mark(f"x1o{it % 2}")
        s = st5
        p.op("vector", lambda E, it=it, s=s: E.reduce_sum(
            s[:, 0:1], x1t[it % 2][:, :], axis=AX.X), deps=[m_x1t])
        p.op("scalar", lambda E, it=it: E.activation(sq5[:, :], x1t[it % 2][:, :],
                                                     AF.Square), deps=[m_x1t])
        m_sq = p.mark("scalar")
        p.op("vector", lambda E, s=s: E.reduce_sum(s[:, 1:2], sq5[:, :], axis=AX.X),
             deps=[m_sq])
        p.op("vector", lambda E, s=s: E.tensor_scalar(s[:, 2:3], s[:, 0:1], 1.0 / D, None, op0=ALU.mult))
        p.op("vector", lambda E, s=s: E.tensor_scalar(s[:, 3:4], s[:, 1:2], 1.0 / D, None, op0=ALU.mult))
        p.op("vector", lambda E, s=s: E.tensor_tensor(s[:, 4:5], s[:, 2:3], s[:, 2:3], op=ALU.mult))
        p.op("vector", lambda E, s=s: E.tensor_tensor(s[:, 5:6], s[:, 3:4], s[:, 4:5], op=ALU.subtract))
        p.op("vector", lambda E, s=s: E.tensor_scalar(s[:, 4:5], s[:, 5:6], 1e-5, None, op0=ALU.add))
        m_v5 = p.mark("vector")
        p.op("scalar", lambda E, s=s: E.activation(s[:, 6:7], s[:, 5:6], AF.Ln, bias=1e-5),
             deps=[m_v5])
        p.op("scalar", lambda E, s=s: E.activation(s[:, 7:8], s[:, 6:7], AF.Exp, scale=-0.5))
        m_r5 = p.mark("scalar")
        _newton_rsqrt(p, s[:, 7:8], s[:, 4:5], s[:, 6:7], [m_r5])
        m_r5 = p.mark("vector")
        p.op("vector", lambda E, it=it, s=s: E.tensor_scalar(
            xn1t[:, :], x1t[it % 2][:, :], s[:, 2:3], s[:, 7:8],
            op0=ALU.subtract, op1=ALU.mult), deps=[m_r5])
        m_xn1 = p.mark("vector")
        ev5 = p.mark("vector")
        for half in range(2):
            for kc8 in range(8):
                kc = half * 8 + kc8
                dd = [m_xn1] if (half == 0 and kc8 == 0) else []
                if kc8 == 0 and half == 1:
                    dd = [ev5]
                p.op("tensor", lambda E, kc=kc, kc8=kc8, it=it: E.transpose(
                    ps5b[:, kc8 * 128:(kc8 + 1) * 128],
                    xn1t[:, kc * 128:(kc + 1) * 128], ident[:, :]),
                    deps=dd, inc=(1 if kc8 == 7 else 0))
            m_t5 = p.mark("tensor")
            p.op("vector", lambda E, it=it, half=half: E.tensor_copy(
                xn1f[:, :].rearrange("p (kc t) -> p kc t", kc=KD)
                [:, half * 8:(half + 1) * 8, it * 128:(it + 1) * 128],
                ps5b[:, 0:TPC].rearrange("p (a t) -> p a t", a=8)), deps=[m_t5])
            ev5 = p.mark("vector")
        m_xn1_done[it] = p.mark("vector")
        xb_free[it % 2] = m_xn1_done[it]
    m_xn1f = p.mark("vector")
    for kc in range(KD):
        p.dma("sync", xn1_o[kc * 128:(kc + 1) * 128, :],
              xn1f[:, kc * TPC:(kc + 1) * TPC], ch="xn1o", deps=[m_xn1f])

    barrier6 = p.allmarks()
    for _e in ENGINES:
        p.wait(_e, barrier6)
    sco6 = p.scope()
    wq6 = [p.sb(f"wq6{i}", [128, KD * 512], BF16, sco6) for i in range(2)]
    qb = [p.sb(f"qb{i}", [128, TPC], BF16, sco6) for i in range(2)]
    ps6 = [p.ps(f"ps6{i}", [128, 512], scope=sco6) for i in range(2)]
    outs = [q_o, k_o, v_o]
    wsrcs = [wqT_i, wkT_i, wvT_i]
    wq6_free = [[p.mark("tensor")]] * 2
    qb_free = [p.mark("sync")] * 2
    ps6_free = [p.mark("vector")] * 2
    m_w6 = {}

    def fetch_w6(idx):
        which, ng = idx // 4, idx % 4
        p.dma("gpsimd", wq6[idx % 2][:, :].rearrange("p (kc c) -> p kc c", kc=KD),
              wsrcs[which][:, ng * 512:(ng + 1) * 512].rearrange("(kc p) c -> p kc c", p=128),
              ch=f"w6{idx % 2}", deps=wq6_free[idx % 2] + (barrier6 if idx < 2 else []))
        m_w6[idx] = p.mark(f"w6{idx % 2}")

    fetch_w6(0)
    fetch_w6(1)
    for idx in range(12):
        which, ng = idx // 4, idx % 4
        for j in range(4):
            nb = ng * 4 + j
            pm = ps6[j % 2]
            for t2 in range(2):
                for kc in range(KD):
                    dd = [m_w6[idx], m_xn1f, ps6_free[j % 2]] if kc == 0 else []
                    p.op("tensor", lambda E, kc=kc, j=j, t2=t2, pm=pm, idx=idx: E.matmul(
                        pm[:, 0:512],
                        wq6[idx % 2][:, kc * 512 + j * 128: kc * 512 + (j + 1) * 128],
                        xn1f[:, kc * TPC + t2 * 512: kc * TPC + (t2 + 1) * 512],
                        start=(kc == 0), stop=(kc == KD - 1)),
                        deps=dd, inc=(1 if kc == KD - 1 else 0))
                m_pm = p.mark("tensor")
                p.op("vector", lambda E, t2=t2, pm=pm, j=j: E.tensor_copy(
                    qb[j % 2][:, t2 * 512:(t2 + 1) * 512], pm[:, 0:512]),
                    deps=[m_pm] + ([qb_free[j % 2]] if t2 == 0 else []))
                ps6_free[j % 2] = p.mark("vector")
            m_qb = p.mark("vector")
            p.dma("sync", outs[which][nb * 128:(nb + 1) * 128, :], qb[j % 2][:, :],
                  ch=f"qo{j % 2}", deps=[m_qb])
            qb_free[j % 2] = p.mark(f"qo{j % 2}")
        wq6_free[idx % 2] = [p.mark("tensor")]
        if idx + 2 < 12:
            fetch_w6(idx + 2)
    sco6.close()
    p.wait("sync", p.allmarks())
    return p.build()




def build_L4():
    p = P("L4")
    NT16 = 16
    qf = p.din("qfh", [256, L], BF16)
    kf = p.din("kfh", [256, L], BF16)
    vf = p.din("vfh", [256, L], BF16)
    identb_i = p.din("identb", [128, 128], BF16)
    onesb_i = p.din("onesb", [128, 1], BF16)
    onesf_i = p.din("onesf", [1, 128], F32)
    mdiag_i = p.din("mdiag", [128, 4 * 512], F32)
    cbank_i = p.din("cbank", [128, 16], F32)
    z_o = p.dout("z_fm", [256, L], BF16)

    for v in L1_CONSTS:
        p.constf(v)

    ident = p.sb("ident_s", [128, 128], BF16)
    ones_b = p.sb("ones_b", [128, 1], BF16)
    ones_f1 = p.sb("ones_f1", [1, 128], F32)
    mdiag = p.sb("mdiag_s", [128, 4 * 512], F32)
    kb_s = p.sb("kb_s", [128, L], BF16)
    qb_s = p.sb("qb_s", [128, L], BF16)
    vb_s = p.sb("vb_s", [128, L], BF16)
    vtm = p.sb("vtm", [128, L], BF16)
    zb_s = p.sb("zb_s", [128, L], BF16)
    ptile = [p.sb(f"pt{i}", [128, 512], BF16) for i in range(4)]
    dscr = [p.sb(f"dscr4{i}", [128, 512], F32) for i in range(4)]
    rowf = p.sb("rowf", [1, 1024], F32)
    rbc = p.sb("rbc", [128, 512], F32)
    psS = [p.ps(f"psS{i}", [128, 512]) for i in range(4)]
    psZ = p.ps("psZ", [128, 512])
    psR = p.ps("psR", [1, 512])
    psB2 = p.ps("psB2", [128, 512])
    psT4 = p.ps("psT4", [128, 512], BF16)

    p.dma("sync", ident[:, :], identb_i[:, :], ch="cst")
    p.dma("sync", ones_b[:, :], onesb_i[:, :], ch="cst")
    p.dma("sync", ones_f1[:, :], onesf_i[:, :], ch="cst")
    p.dma("sync", mdiag[:, :], mdiag_i[:, :], ch="cst")
    p.dma("sync", p._cbank[:, :], cbank_i[:, :], ch="cst")
    m_cst = p.mark("cst")
    p.const_barrier = m_cst

    zdma = [p.mark("z0"), p.mark("z1")]
    for hh in range(2):
        chq = f"qkv{hh}"
        p.dma("sync", kb_s[:, :], kf[hh * 128:(hh + 1) * 128, :], ch=chq,
              deps=([p.mark("tensor")] if hh else []))
        p.dma("sync", qb_s[:, :], qf[hh * 128:(hh + 1) * 128, :], ch=chq)
        p.dma("sync", vb_s[:, :], vf[hh * 128:(hh + 1) * 128, :], ch=chq)
        m_ld = p.mark(chq)
        vt_free = p.mark("vector")
        for grp in range(16):
            for i4 in range(4):
                kb = grp * 4 + i4
                p.op("tensor", lambda E, kb=kb, i4=i4: E.transpose(
                    psT4[:, i4 * 128:(i4 + 1) * 128],
                    vb_s[:, kb * 128:(kb + 1) * 128], ident[:, :]),
                    deps=([m_ld, m_cst, vt_free] if i4 == 0 else []),
                    inc=(1 if i4 == 3 else 0))
            m_t = p.mark("tensor")
            p.op("vector", lambda E, grp=grp: E.tensor_copy(
                vtm[:, grp * 512:(grp + 1) * 512], psT4[:, :]), deps=[m_t])
            vt_free = p.mark("vector")
        m_vtm = p.mark("vector")

        pt_free = [p.mark("tensor")] * 4
        psS_free = [p.mark("scalar")] * 4
        dscr_free = [p.mark("scalar")] * 4
        psZR_free = p.mark("vector")
        for tq in range(NT16):
            nkb = 4 * tq + 4
            for g in range(tq + 1):
                m_s = [None] * 4
                for j in range(4):
                    kb = 4 * g + j
                    dd = [psS_free[j]]
                    if g == 0 and j == 0:
                        dd += [m_ld, psZR_free]
                    p.op("tensor", lambda E, kb=kb, tq=tq, j=j: E.matmul(
                        psS[j][:, :], kb_s[:, kb * 128:(kb + 1) * 128],
                        qb_s[:, tq * 512:(tq + 1) * 512], start=True, stop=True),
                        deps=dd, inc=1)
                    m_s[j] = p.mark("tensor")
                m_p = [None] * 4
                if g == tq:
                    for j in range(4):
                        p.op("vector", lambda E, j=j: E.tensor_tensor(
                            dscr[j][:, :], psS[j][:, :],
                            mdiag[:, j * 512:(j + 1) * 512], op=ALU.add),
                            deps=[m_s[j], dscr_free[j]], inc=1)
                        m_d = p.mark("vector")
                        psS_free[j] = m_d
                        p.op("scalar", lambda E, j=j: E.activation(
                            ptile[j][:, :], dscr[j][:, :], AF.Exp,
                            scale=0.08838834764831845),
                            deps=[m_d, pt_free[j]], inc=1)
                        m_p[j] = p.mark("scalar")
                        dscr_free[j] = m_p[j]
                else:
                    for j in range(4):
                        p.op("scalar", lambda E, j=j: E.activation(
                            ptile[j][:, :], psS[j][:, :], AF.Exp,
                            scale=0.08838834764831845),
                            deps=[m_s[j], pt_free[j]], inc=1)
                        m_p[j] = p.mark("scalar")
                        psS_free[j] = m_p[j]
                for j in range(4):
                    kb = 4 * g + j
                    p.op("tensor", lambda E, kb=kb, j=j, nkb=nkb: E.matmul(
                        psZ[:, :], vtm[:, kb * 128:(kb + 1) * 128],
                        ptile[j][:, :],
                        start=(kb == 0), stop=(kb == nkb - 1)),
                        deps=([m_p[j], m_vtm] if kb == 0 else [m_p[j]]), inc=0)
                    p.op("tensor", lambda E, kb=kb, j=j, nkb=nkb: E.matmul(
                        psR[:, :], ones_b[:, :], ptile[j][:, :],
                        start=(kb == 0), stop=(kb == nkb - 1)), inc=1)
                    pt_free[j] = p.mark("tensor")
            m_pv = p.mark("tensor")
            p.op("vector", lambda E: E.tensor_copy(rowf[0:1, 0:512], psR[:, :]),
                 deps=[m_pv])
            p.op("vector", lambda E: E.reciprocal(rowf[0:1, 512:1024], rowf[0:1, 0:512]))
            m_r = p.mark("vector")
            p.op("tensor", lambda E: E.matmul(psB2[:, :], ones_f1[:, :],
                                              rowf[0:1, 512:1024], start=True,
                                              stop=True), deps=[m_r], inc=1)
            m_bc = p.mark("tensor")
            p.op("vector", lambda E: E.tensor_copy(rbc[:, :], psB2[:, :]),
                 deps=[m_bc])
            m_rbc = p.mark("vector")
            p.op("vector", lambda E, tq=tq: E.tensor_tensor(
                zb_s[:, tq * 512:(tq + 1) * 512], psZ[:, :], rbc[:, :],
                op=ALU.mult),
                deps=[m_rbc] + ([zdma[0]] if (hh == 1 and tq == 0) else []))
            psZR_free = p.mark("vector")
        m_z = p.mark("vector")
        p.dma("sync", z_o[hh * 128:(hh + 1) * 128, :], zb_s[:, :], ch=f"z{hh}",
              deps=[m_z])
        zdma[hh] = p.mark(f"z{hh}")
    p.wait("sync", p.allmarks())
    return p.build()




def build_L5():
    p = P("L5")
    xn1f_i = p.din("xn1f_i", [D, TPC], BF16)
    zf_i = p.din("zf_i", [D, TPC], BF16)
    x1_i = p.din("x1_i", [TPC, D], F32)
    woT_d = p.din("woT", [D, D], F32)
    WT_d = p.din("WT", [D, D], F32)
    acat_d = p.din("acatT", [D, 256], F32)
    bcat_d = p.din("bcatT", [256, D], F32)
    biasr_d = p.din("biasr", [1, D], F32)
    rw1T_d = p.din("rw1T", [D, 64], F32)
    rw2T_d = p.din("rw2T", [64, 32], F32)
    rb1c_d = p.din("rb1c", [64, 1], F32)
    rb2c_d = p.din("rb2c", [32, 1], F32)
    gatesc_d = p.din("gatesc", [32, 1], F32)
    identb_i = p.din("identb", [128, 128], BF16)
    identf_i = p.din("identf32", [128, 128], F32)
    onesf_i = p.din("onesf", [1, 128], F32)
    cbank_i = p.din("cbank", [128, 16], F32)
    x2_o = p.dout("x2_o", [TPC, D], F32)
    for v in L1_CONSTS:
        p.constf(v)

    identf = p.sb("identf", [128, 128], F32)
    identsb = p.sb("identsb", [128, 128], BF16)
    onesf_s = p.sb("onesf_s", [1, 128], F32)
    xn1f = p.sb("xn1f_s", [128, KD * TPC], BF16)
    zfs = p.sb("zfs", [128, KD * TPC], BF16)
    acat = p.sb("acat_s", [128, KD * 256], BF16)
    bcat = p.sb("bcat_s", [128, 2 * D], BF16)
    biasr = p.sb("biasr_s", [1, D], F32)
    rw1s = p.sb("rw1s", [128, KD * 64], BF16)
    rw2s = p.sb("rw2s", [64, 32], BF16)
    rb1s = p.sb("rb1s", [64, 1], F32)
    rb2s = p.sb("rb2s", [32, 1], F32)
    gts = p.sb("gts", [32, 1], F32)
    h_fm = p.sb("h_fm", [64, TPC], BF16)
    lg_fm = p.sb("lg_fm", [32, TPC], F32)
    coefs = p.sb("coefs", [128, 8 * 16], F32)
    rt = p.sb("rt", [128, 5 * 16], F32)
    rcol = p.sb("rcol", [128, 8], F32)
    wxaT = p.sb("wxaT", [128, 8 * 256], BF16)
    wxa = p.sb("wxa", [128, 256], BF16)
    wob = [p.sb(f"wob{i}", [128, KD * 512], BF16) for i in range(2)]
    wwb = [p.sb(f"wwb{i}", [128, KD * 512], BF16) for i in range(2)]
    x1b = [p.sb(f"x1b{i}", [128, 8 * 512], F32) for i in range(2)]
    x2sb = [p.sb(f"x2sb{i}", [128, 512], F32) for i in range(2)]
    psM = [p.ps(f"psM{i}", [128, 512]) for i in range(2)]
    psR = p.ps("psR5", [64, 512])
    psL = p.ps("psL5", [32, 512])
    psT5 = p.ps("psT5", [128, 16])
    psxa = p.ps("psxa", [128, 256])
    psw = p.ps("psw", [128, 256], BF16)

    p.dma("sync", identf[:, :], identf_i[:, :], ch="cst")
    p.dma("sync", identsb[:, :], identb_i[:, :], ch="cst")
    p.dma("sync", onesf_s[:, :], onesf_i[:, :], ch="cst")
    p.dma("sync", p._cbank[:, :], cbank_i[:, :], ch="cst")
    p.dma("sync", biasr[0:1, :], biasr_d[0:1, :], ch="cst")
    p.dma("sync", rb1s[:, :], rb1c_d[:, :], ch="cst")
    p.dma("sync", rb2s[:, :], rb2c_d[:, :], ch="cst")
    p.dma("sync", gts[:, :], gatesc_d[:, :], ch="cst")
    m_cst = p.mark("cst")
    p.const_barrier = m_cst
    p.dma("sync", xn1f[:, :].rearrange("p (kc t) -> p kc t", kc=KD),
          xn1f_i[:, :].rearrange("(kc p) t -> p kc t", p=128), ch="act")
    p.dma("sync", zfs[:, :].rearrange("p (kc t) -> p kc t", kc=KD),
          zf_i[:, :].rearrange("(kc p) t -> p kc t", p=128), ch="act")
    m_act = p.mark("act")
    p.dma("gpsimd", acat[:, :].rearrange("p (kc c) -> p kc c", kc=KD),
          acat_d[:, :].rearrange("(kc p) c -> p kc c", p=128), ch="wsm")
    p.dma("gpsimd", bcat[:, :].rearrange("p (kc c) -> p kc c", kc=2),
          bcat_d[:, :].rearrange("(kc p) c -> p kc c", p=128), ch="wsm")
    p.dma("gpsimd", rw1s[:, :].rearrange("p (kc c) -> p kc c", kc=KD),
          rw1T_d[:, :].rearrange("(kc p) c -> p kc c", p=128), ch="wsm")
    p.dma("gpsimd", rw2s[:, :], rw2T_d[:, :], ch="wsm")
    m_wsm = p.mark("wsm")
    m_wgt = [None, None]
    m_x1 = [None, None]
    for ob in range(2):
        ch = f"wg{ob}"
        p.dma("gpsimd", wob[ob][:, :].rearrange("p (kc c) -> p kc c", kc=KD),
              woT_d[:, ob * 512:(ob + 1) * 512]
              .rearrange("(kc p) c -> p kc c", p=128), ch=ch)
        p.dma("gpsimd", wwb[ob][:, :].rearrange("p (kc c) -> p kc c", kc=KD),
              WT_d[:, ob * 512:(ob + 1) * 512]
              .rearrange("(kc p) c -> p kc c", p=128), ch=ch)
        m_wgt[ob] = p.mark(ch)
        p.dma("sync", x1b[ob][:, :].rearrange("p (tc c) -> p tc c", tc=8),
              x1_i[:, ob * 512:(ob + 1) * 512]
              .rearrange("(tc p) c -> p tc c", p=128), ch=f"x1{ob}")
        m_x1[ob] = p.mark(f"x1{ob}")

    for hf in range(2):
        for k in range(KD):
            p.op("tensor", lambda E, hf=hf, k=k: E.matmul(
                psR[:, :], rw1s[:, k * 64:(k + 1) * 64],
                xn1f[:, k * TPC + hf * 512: k * TPC + hf * 512 + 512],
                start=(k == 0), stop=(k == KD - 1)),
                deps=([m_act, m_wsm, m_cst] if (hf == 0 and k == 0)
                      else ([p.mark("scalar")] if k == 0 else [])),
                inc=(1 if k == KD - 1 else 0))
        m_mm = p.mark("tensor")
        p.op("scalar", lambda E, hf=hf: E.activation(
            h_fm[:, hf * 512:(hf + 1) * 512], psR[:, :], AF.Silu,
            bias=rb1s[:, 0:1]), deps=[m_mm])
    m_h = p.mark("scalar")
    for hf in range(2):
        p.op("tensor", lambda E, hf=hf: E.matmul(
            psL[:, :], rw2s[:, :], h_fm[:, hf * 512:(hf + 1) * 512],
            start=True, stop=True),
            deps=([m_h] if hf == 0 else [p.mark("vector")]), inc=1)
        m_mm = p.mark("tensor")
        p.op("vector", lambda E, hf=hf: E.tensor_scalar(
            lg_fm[:, hf * 512:(hf + 1) * 512], psL[:, :], rb2s[:, 0:1],
            gts[:, 0:1], op0=ALU.add, op1=ALU.add), deps=[m_mm])
    m_lg = p.mark("vector")
    for tq in range(8):
        p.op("tensor", lambda E, tq=tq: E.transpose(
            psT5[:, :], lg_fm[0:16, tq * 128:(tq + 1) * 128],
            identf[0:16, 0:16]),
            deps=[m_lg] if tq == 0 else [p.mark("vector")], inc=1)
        m_T = p.mark("tensor")
        lg = rt[:, 0:16]
        iseq = rt[:, 16:32]
        l2 = rt[:, 32:48]
        iseq2 = rt[:, 48:64]
        tmp = rt[:, 64:80]
        p.op("vector", lambda E, lg=lg: E.tensor_copy(lg, psT5[:, :]),
             deps=[m_T])
        p.op("vector", lambda E, lg=lg: E.reduce_max(rcol[:, 0:1], lg, axis=AX.X))
        p.op("vector", lambda E, lg=lg, iseq=iseq: E.tensor_scalar(
            iseq, lg, rcol[:, 0:1], None, op0=ALU.is_equal))
        p.op("vector", lambda E, iseq=iseq, tmp=tmp: E.tensor_scalar(
            tmp, iseq, -1e30, None, op0=ALU.mult))
        p.op("vector", lambda E, lg=lg, tmp=tmp, l2=l2: E.tensor_tensor(
            l2, lg, tmp, op=ALU.add))
        p.op("vector", lambda E, l2=l2: E.reduce_max(rcol[:, 1:2], l2, axis=AX.X))
        p.op("vector", lambda E, l2=l2, iseq2=iseq2: E.tensor_scalar(
            iseq2, l2, rcol[:, 1:2], None, op0=ALU.is_equal))
        p.op("vector", lambda E: E.tensor_tensor(
            rcol[:, 2:3], rcol[:, 1:2], rcol[:, 0:1], op=ALU.subtract))
        m_v = p.mark("vector")
        p.op("scalar", lambda E: E.activation(rcol[:, 3:4], rcol[:, 2:3], AF.Exp),
             deps=[m_v])
        m_e = p.mark("scalar")
        p.op("vector", lambda E: E.tensor_scalar(
            rcol[:, 4:5], rcol[:, 3:4], 1.0, None, op0=ALU.add), deps=[m_e])
        p.op("vector", lambda E: E.reciprocal(rcol[:, 5:6], rcol[:, 4:5]))
        p.op("vector", lambda E: E.tensor_tensor(
            rcol[:, 6:7], rcol[:, 3:4], rcol[:, 5:6], op=ALU.mult))
        p.op("vector", lambda E, iseq=iseq: E.tensor_scalar(
            iseq, iseq, rcol[:, 5:6], None, op0=ALU.mult))
        p.op("vector", lambda E, iseq2=iseq2, tmp=tmp: E.tensor_scalar(
            tmp, iseq2, rcol[:, 6:7], None, op0=ALU.mult))
        p.op("vector", lambda E, tq=tq, iseq=iseq, tmp=tmp: E.tensor_tensor(
            coefs[:, tq * 16:(tq + 1) * 16], iseq, tmp, op=ALU.add))
    m_coef = p.mark("vector")

    psxa_free = m_coef
    psw_free = m_coef
    for tq in range(8):
        for k in range(KD):
            p.op("tensor", lambda E, tq=tq, k=k: E.matmul(
                psxa[:, :], xn1f[:, k * TPC + tq * 128: k * TPC + tq * 128 + 128],
                acat[:, k * 256:(k + 1) * 256],
                start=(k == 0), stop=(k == KD - 1)),
                deps=([psxa_free] if k == 0 else []),
                inc=(1 if k == KD - 1 else 0))
        m_mm = p.mark("tensor")
        p.op("vector", lambda E, tq=tq: E.tensor_tensor(
            wxa[:, :].rearrange("p (r e) -> p r e", r=16),
            psxa[:, :].rearrange("p (r e) -> p r e", r=16),
            coefs[:, tq * 16:(tq + 1) * 16].unsqueeze(1)
            .broadcast_to([128, 16, 16]), op=ALU.mult), deps=[m_mm, m_coef])
        m_wxa = p.mark("vector")
        psxa_free = m_wxa
        for pp in range(2):
            p.op("tensor", lambda E, pp=pp: E.transpose(
                psw[:, pp * 128:(pp + 1) * 128],
                wxa[:, pp * 128:(pp + 1) * 128], identsb[:, :]),
                deps=[m_wxa, psw_free] if pp == 0 else [], inc=(pp == 1))
        m_tr = p.mark("tensor")
        p.op("vector", lambda E, tq=tq: E.tensor_copy(
            wxaT[:, tq * 256:(tq + 1) * 256], psw[:, :]), deps=[m_tr])
        psw_free = p.mark("vector")
    m_wxaT = p.mark("vector")

    psM_free = [m_wxaT, m_wxaT]
    out_free = [None, None]
    ev_done = {}
    for ob in range(4):
        b = ob % 2
        for tq in range(8):
            dd0 = [m_wgt[b], psM_free[tq % 2]]
            if ob == 0 and tq == 0:
                dd0 += [m_act, m_wxaT, m_wsm]
            for k in range(KD):
                p.op("tensor", lambda E, tq=tq, k=k, b=b: E.matmul(
                    psM[tq % 2][:, :],
                    zfs[:, k * TPC + tq * 128: k * TPC + tq * 128 + 128],
                    wob[b][:, k * 512:(k + 1) * 512],
                    start=(k == 0), stop=False),
                    deps=(dd0 if k == 0 else []), inc=0)
            for k in range(KD):
                p.op("tensor", lambda E, tq=tq, k=k, b=b: E.matmul(
                    psM[tq % 2][:, :],
                    xn1f[:, k * TPC + tq * 128: k * TPC + tq * 128 + 128],
                    wwb[b][:, k * 512:(k + 1) * 512],
                    start=False, stop=False), inc=0)
            p.op("tensor", lambda E, tq=tq, ob=ob: E.matmul(
                psM[tq % 2][:, :], onesf_s[0:1, 0:128],
                biasr[0:1, ob * 512:(ob + 1) * 512],
                start=False, stop=False), inc=0)
            for pp in range(2):
                p.op("tensor", lambda E, tq=tq, pp=pp, ob=ob: E.matmul(
                    psM[tq % 2][:, :],
                    wxaT[:, tq * 256 + pp * 128: tq * 256 + pp * 128 + 128],
                    bcat[:, pp * D + ob * 512: pp * D + ob * 512 + 512],
                    start=False, stop=(pp == 1)), inc=(1 if pp == 1 else 0))
            m_mm = p.mark("tensor")
            dd = [m_mm, m_x1[b]]
            if out_free[tq % 2] is not None:
                dd.append(out_free[tq % 2])
            p.op("vector", lambda E, tq=tq, b=b: E.tensor_tensor(
                x2sb[tq % 2][:, :], psM[tq % 2][:, :],
                x1b[b][:, tq * 512:(tq + 1) * 512], op=ALU.add), deps=dd)
            psM_free[tq % 2] = p.mark("vector")
            p.dma("sync", x2_o[tq * 128:(tq + 1) * 128, ob * 512:(ob + 1) * 512],
                  x2sb[tq % 2][:, :], ch=f"out{tq % 2}", deps=[psM_free[tq % 2]])
            out_free[tq % 2] = p.mark(f"out{tq % 2}")
        if ob + 2 < 4:
            m_done = p.mark("tensor")
            nb = (ob + 2) % 2
            ch = f"wg{nb}"
            p.dma("gpsimd", wob[nb][:, :].rearrange("p (kc c) -> p kc c", kc=KD),
                  woT_d[:, (ob + 2) * 512:(ob + 3) * 512]
                  .rearrange("(kc p) c -> p kc c", p=128), ch=ch,
                  deps=[m_done])
            p.dma("gpsimd", wwb[nb][:, :].rearrange("p (kc c) -> p kc c", kc=KD),
                  WT_d[:, (ob + 2) * 512:(ob + 3) * 512]
                  .rearrange("(kc p) c -> p kc c", p=128), ch=ch)
            m_wgt[nb] = p.mark(ch)
            p.dma("sync", x1b[nb][:, :].rearrange("p (tc c) -> p tc c", tc=8),
                  x1_i[:, (ob + 2) * 512:(ob + 3) * 512]
                  .rearrange("(tc p) c -> p tc c", p=128), ch=f"x1{nb}",
                  deps=[p.mark("vector")])
            m_x1[nb] = p.mark(f"x1{nb}")
    p.wait("sync", p.allmarks())
    return p.build()


def _prep_L1_shared(ins):
    Wq = np.asarray(ins["L0_attn_W_qkvz"]).reshape(D, 16, 768)
    wmz = np.concatenate([
        Wq[:, :, 0:128].reshape(D, -1), Wq[:, :, 128:256].reshape(D, -1),
        Wq[:, :, 256:512].reshape(D, -1), Wq[:, :, 512:768].reshape(D, -1)], axis=1)
    wmz = np.ascontiguousarray(wmz, np.float32)
    wb = np.asarray(ins["L0_attn_W_ba"]).reshape(D, 16, 4)
    wba = np.ascontiguousarray(np.concatenate(
        [wb[:, :, 0:2].reshape(D, -1), wb[:, :, 2:4].reshape(D, -1)], axis=1), np.float32)
    cwr = np.ascontiguousarray(
        np.asarray(ins["L0_attn_conv_w"]).reshape(64, 128, 4).transpose(1, 0, 2)
        .reshape(128, 256), np.float32)
    ln1c = np.ascontiguousarray(np.asarray(ins["L0_attn_ln1_w"]).reshape(KD, 128).T,
                                np.float32)
    alog = np.asarray(ins["L0_attn_A_log"]).reshape(1, 32).astype(np.float32)
    dtb = np.asarray(ins["L0_attn_dt_bias"]).reshape(1, 32).astype(np.float32)
    wgT = np.ascontiguousarray(np.asarray(ins["L0_ffn_wq_gate"]).T)
    wuT = np.ascontiguousarray(np.asarray(ins["L0_ffn_wq_up"]).T)
    wdT = np.ascontiguousarray(np.asarray(ins["L0_ffn_wq_down"]).T)
    sgc = np.ascontiguousarray(np.asarray(ins["L0_ffn_s_gate"]).reshape(44, 128).T,
                               np.float32)
    suc = np.ascontiguousarray(np.asarray(ins["L0_ffn_s_up"]).reshape(44, 128).T,
                               np.float32)
    sdc = np.ascontiguousarray(np.asarray(ins["L0_ffn_s_down"]).reshape(KD, 128).T,
                               np.float32)
    d = dict(wmz=wmz, wba=wba, cwr=cwr, ln1c=ln1c, alog=alog, dtb=dtb,
             wgT=wgT, wuT=wuT, wdT=wdT, sgc=sgc, suc=suc, sdc=sdc)
    d.update(_const_inputs(L1_CONSTS))
    return d


def _l2_const_inputs():
    r = np.arange(128)
    mTs = np.where(r[:, None] < r[None, :], 0.0, 1e30).astype(np.float32)
    mTi = np.where(r[:, None] <= r[None, :], 0.0, 1e30).astype(np.float32)
    mLs = np.where(r[None, :] < r[:, None], 0.0, 1e30).astype(np.float32)
    d = _const_inputs(L1_CONSTS)
    e4 = np.zeros((4, 512), np.float32)
    for h in range(4):
        e4[h, h * 128:(h + 1) * 128] = 1.0
    return {
        "identb": d["identb"], "identf32": d["identf32"], "cbank": d["cbank"],
        "onesf": d["onesf"], "onest": np.ones((128, 128), np.float32),
        "mTs": mTs, "mTi": mTi, "mLs": mLs,
        "e4n": -e4, "e4p": e4, "onesfn": -np.ones((1, 128), np.float32),
    }


def _l2_maps(qkv_fm, gbl):
    consts = _l2_const_inputs()
    maps = []
    for c in range(NCORES):
        qf = np.ascontiguousarray(qkv_fm[256 * c:256 * (c + 1)])
        kf = np.ascontiguousarray(qkv_fm[2048 + 256 * c:2048 + 256 * (c + 1)])
        vf = np.ascontiguousarray(qkv_fm[4096 + 512 * c:4096 + 512 * (c + 1)])
        cols = []
        for blk in range(3):
            sl = gbl[:, 32 * blk + 4 * c: 32 * blk + 4 * c + 4]
            cols.append(sl.reshape(64, 128, 4))
        gbp = np.zeros((128, 64 * 12), np.float32)
        for grp in range(16):
            for bi in range(3):
                blkv = cols[bi][grp * 4:(grp + 1) * 4]
                gbp[:, grp * 48 + 16 * bi: grp * 48 + 16 * (bi + 1)] = \
                    blkv.transpose(1, 0, 2).reshape(128, 16)
        m = dict(consts)
        m.update(qf=qf, kf=kf, vf=vf, gbp=gbp)
        maps.append(m)
    return maps


def run_L2(qkv_fm, gbl):
    nc = build_L2()
    res = run_bass_kernel_spmd(nc, _l2_maps(qkv_fm, gbl), core_ids=list(range(NCORES)))
    return res.results


def _l1_maps(ins):
    shared = _prep_L1_shared(ins)
    x = np.asarray(ins["L0_attn_hidden_states"]).reshape(L, D).astype(np.float32)
    maps = []
    for c in range(NCORES):
        t0 = c * TPC
        x9 = np.zeros((TPC + 128, D), np.float32)
        if c > 0:
            x9[0:128] = x[t0 - 128:t0]
        x9[128:] = x[t0:t0 + TPC]
        m = dict(shared)
        m["x9"] = x9
        maps.append(m)
    return maps


def run_L1(ins):
    nc = build_L1()
    res = run_bass_kernel_spmd(nc, _l1_maps(ins), core_ids=list(range(NCORES)))
    return res.results


def _prep_L5_shared(ins):
    woT = np.ascontiguousarray(np.asarray(ins["L1_attn_Wo"], np.float32).T)
    WT = np.ascontiguousarray(np.asarray(ins["L1_ffn_W"], np.float32).T)
    A = np.asarray(ins["L1_ffn_A"], np.float32)
    Bm = np.asarray(ins["L1_ffn_B"], np.float32)
    acatT = np.ascontiguousarray(A.transpose(2, 1, 0).reshape(D, 256))
    bcatT = np.ascontiguousarray(Bm.transpose(2, 0, 1).reshape(256, D))
    biasr = np.ascontiguousarray(
        np.asarray(ins["L1_ffn_bias"], np.float32).reshape(1, D))
    rw1T = np.ascontiguousarray(np.asarray(ins["L1_ffn_rw1"], np.float32).T)
    rw2T = np.ascontiguousarray(np.asarray(ins["L1_ffn_rw2"], np.float32).T)
    rb1c = np.ascontiguousarray(
        np.asarray(ins["L1_ffn_rb1"], np.float32).reshape(64, 1))
    rb2c = np.ascontiguousarray(
        np.asarray(ins["L1_ffn_rb2"], np.float32).reshape(32, 1))
    gatesc = np.zeros((32, 1), np.float32)
    gatesc[0:16, 0] = np.asarray(ins["L1_ffn_gates"], np.float32)
    d = dict(woT=woT, WT=WT, acatT=acatT, bcatT=bcatT, biasr=biasr,
             rw1T=rw1T, rw2T=rw2T, rb1c=rb1c, rb2c=rb2c, gatesc=gatesc)
    c = _const_inputs(L1_CONSTS)
    d.update(identb=c["identb"], identf32=c["identf32"], onesf=c["onesf"],
             cbank=c["cbank"])
    return d


def _l5_maps(shared, xn1_fm, z_fm, x1_tm):
    maps = []
    for c in range(NCORES):
        t0 = c * TPC
        m = dict(shared)
        m.update(xn1f_i=np.ascontiguousarray(xn1_fm[:, t0:t0 + TPC]),
                 zf_i=np.ascontiguousarray(z_fm[:, t0:t0 + TPC]),
                 x1_i=np.ascontiguousarray(x1_tm[t0:t0 + TPC]))
        maps.append(m)
    return maps


def _l4_const_inputs():
    d = _const_inputs(L1_CONSTS)
    r128 = np.arange(128)
    r512 = np.arange(512)
    mdiag = np.zeros((128, 4 * 512), np.float32)
    for m in range(4):
        mdiag[:, m * 512:(m + 1) * 512] = np.where(
            (128 * m + r128)[:, None] <= r512[None, :], 0.0, -1e30)
    return {"identb": d["identb"], "onesb": d["onesb"], "onesf": d["onesf"],
            "cbank": d["cbank"], "mdiag": mdiag}


def _l4_maps(q_fm, k_fm, v_fm):
    consts = _l4_const_inputs()
    maps = []
    for c in range(NCORES):
        m = dict(consts)
        m.update(qfh=np.ascontiguousarray(q_fm[256 * c:256 * (c + 1)]),
                 kfh=np.ascontiguousarray(k_fm[256 * c:256 * (c + 1)]),
                 vfh=np.ascontiguousarray(v_fm[256 * c:256 * (c + 1)]))
        maps.append(m)
    return maps


def kernel(**inputs):
    ins = {k: np.asarray(v) for k, v in inputs.items()}
    consts = _const_inputs(L1_CONSTS)

    r1 = run_L1(ins)
    qkv_fm = np.concatenate([np.asarray(r["qkv_fm"]) for r in r1], axis=1)
    zs_fm = np.concatenate([np.asarray(r["zs_fm"]) for r in r1], axis=1)
    xn_fm = np.concatenate([np.asarray(r["xnf"]) for r in r1], axis=1)
    f0_fm = np.concatenate([np.asarray(r["f0f"]) for r in r1], axis=1)
    gbl = np.concatenate([np.asarray(r["gbl"]) for r in r1], axis=0)

    r2 = run_L2(qkv_fm, gbl)
    cn_fm = np.concatenate([np.asarray(r["cno"]) for r in r2], axis=0)

    x = np.asarray(ins["L0_attn_hidden_states"], np.float32).reshape(L, D)
    shared3 = dict(
        wout=np.ascontiguousarray(ins["L0_attn_W_out"], np.float32),
        gnw=np.ascontiguousarray(
            np.asarray(ins["L0_attn_gated_norm_w"]).reshape(128, 1), np.float32),
        ln2c=np.ascontiguousarray(
            np.asarray(ins["L0_attn_ln2_w"]).reshape(KD, 128).T, np.float32),
        wg5=np.ascontiguousarray(ins["L0_attn_W_gate"], np.float32),
        wu5=np.ascontiguousarray(ins["L0_attn_W_up"], np.float32),
        wd5=np.ascontiguousarray(ins["L0_attn_W_down"], np.float32),
        wqT=np.ascontiguousarray(np.asarray(ins["L1_attn_Wq"]).T, np.float32),
        wkT=np.ascontiguousarray(np.asarray(ins["L1_attn_Wk"]).T, np.float32),
        wvT=np.ascontiguousarray(np.asarray(ins["L1_attn_Wv"]).T, np.float32),
        identb=consts["identb"], identf32=consts["identf32"],
        onesb=consts["onesb"], onesf=consts["onesf"], cbank=consts["cbank"],
    )
    maps3 = []
    for c in range(NCORES):
        s = slice(c * TPC, (c + 1) * TPC)
        m = dict(shared3)
        m["xt"] = np.ascontiguousarray(x[s])
        m["xnf_i"] = np.ascontiguousarray(xn_fm[:, s])
        m["zsf_i"] = np.ascontiguousarray(zs_fm[:, s])
        m["cnf_i"] = np.ascontiguousarray(cn_fm[:, s])
        m["f0_i"] = np.ascontiguousarray(f0_fm[:, s])
        maps3.append(m)
    r3 = run_bass_kernel_spmd(build_L3(), maps3,
                              core_ids=list(range(NCORES))).results
    x1_tm = np.concatenate([np.asarray(r["x1_o"]) for r in r3], axis=0)
    xn1_fm = np.concatenate([np.asarray(r["xn1f"]) for r in r3], axis=1)
    q_fm = np.concatenate([np.asarray(r["qf_o"]) for r in r3], axis=1)
    k_fm = np.concatenate([np.asarray(r["kf_o"]) for r in r3], axis=1)
    v_fm = np.concatenate([np.asarray(r["vf_o"]) for r in r3], axis=1)

    r4 = run_bass_kernel_spmd(build_L4(), _l4_maps(q_fm, k_fm, v_fm),
                              core_ids=list(range(NCORES))).results
    z_fm = np.concatenate([np.asarray(r["z_fm"]) for r in r4], axis=0)

    maps5 = _l5_maps(_prep_L5_shared(ins), xn1_fm, z_fm, x1_tm)
    r5 = run_bass_kernel_spmd(build_L5(), maps5,
                              core_ids=list(range(NCORES))).results
    x2 = np.concatenate([np.asarray(r["x2_o"]) for r in r5], axis=0)
    return np.ascontiguousarray(x2.reshape(1, L, D), np.float32)
